# version 06
# baseline (speedup 1.0000x reference)
import contextlib
import numpy as np
import ml_dtypes
import concourse.bass as bass
import concourse.mybir as mybir
from concourse.bass_utils import run_bass_kernel_spmd

F32 = mybir.dt.float32
BF16 = mybir.dt.bfloat16
AF = mybir.ActivationFunctionType
ALU = mybir.AluOpType
AX = mybir.AxisListType

BF = ml_dtypes.bfloat16

B, S, D = 2, 8192, 1024
NCORE = 8
CH = 1024
HALO = 64
CHH = CH + 2 * HALO
T = 2 * CH
NEG = -60000.0


def core_chunks(c):
    b, j = c // 4, c % 4
    return b, (j, 7 - j)



def host_x_slice(x, c):
    b, (c0, c1) = core_chunks(c)
    out = np.zeros((2, CHH, D), np.float32)
    for i, ch in enumerate((c0, c1)):
        lo, hi = ch * CH - HALO, (ch + 1) * CH + HALO
        slo, shi = max(lo, 0), min(hi, S)
        out[i, slo - lo:shi - lo, :] = x[b, slo:shi, :]
    return out


def host_rope_tables(c):
    b, (c0, c1) = core_chunks(c)
    pos = np.concatenate([np.arange(c0 * CH, (c0 + 1) * CH),
                          np.arange(c1 * CH, (c1 + 1) * CH)]).astype(np.float64)
    inv = 1.0 / 10000.0 ** (np.arange(0, D, 2, dtype=np.float64) / D)
    inv2 = np.repeat(inv, 2)
    ang = inv2[:, None] * pos[None, :]
    return (np.cos(ang).astype(np.float32).astype(BF),
            np.sin(ang).astype(np.float32).astype(BF))


def host_swap_matrix():
    Sm = np.zeros((128, 128), np.float32)
    for p in range(128):
        if p % 2 == 0:
            Sm[p + 1, p] = -1.0
        else:
            Sm[p - 1, p] = 1.0
    return Sm.astype(BF)


def host_sel2():
    m = np.zeros((2, 128), np.float32)
    m[0, :64] = 1.0
    m[1, 64:] = 1.0
    return m.astype(BF)


def host_tri_masks():
    k = np.arange(128)[:, None]
    q = np.arange(512)[None, :]
    return np.stack([(k <= q - 128 * i).astype(np.float32) for i in range(4)]).astype(BF)


def host_kperm(c):
    b, (c0, c1) = core_chunks(c)
    own = np.concatenate([np.arange(c0 * CH, (c0 + 1) * CH),
                          np.arange(c1 * CH, (c1 + 1) * CH)])
    rest = np.setdiff1d(np.arange(S), own)
    return np.concatenate([own, rest])


def host_bias_tab(c):
    b, (c0, c1) = core_chunks(c)
    perm = host_kperm(c)
    qbase = [c0 * CH, c0 * CH + 512, c1 * CH, c1 * CH + 512]
    tab = np.zeros((128, 4 * 64), np.float32)
    for qc in range(4):
        for kt in range(64):
            col = qc * 64 + kt
            if 4 * qc <= kt < 4 * qc + 4:
                continue
            kbase = int(perm[kt * 128])
            if kbase < qbase[qc]:
                tab[:, col] = 0.0
            else:
                tab[:, col] = NEG
    return tab



_scr_ctr = [0]


def _mk_dt(nc, io):
    def DT(name, shape, dtype, kind):
        if io is not None:
            if kind == "Internal":
                _scr_ctr[0] += 1
                return nc.dram_tensor(f"{name}_s{_scr_ctr[0]}", shape, dtype, kind="Internal").ap()
            return io[name]
        return nc.dram_tensor(name, shape, dtype, kind=kind).ap()
    return DT




class Ctr:
    def __init__(self, sem):
        self.sem = sem
        self.val = 0

    def inc(self, n=1):
        self.val += n
        return self.val


def tm_layernorm_tiles(nc, v, s, src_tile, dst_tile, scr, m, r, eps):
    v.reduce_sum(m[:, 0:1], src_tile, axis=AX.X)
    v.tensor_scalar(m[:, 1:2], m[:, 0:1], 1.0 / D, None, ALU.mult)
    s.activation(out=scr, in_=src_tile, func=AF.Square, accum_out=m[:, 2:3])
    v.tensor_scalar(m[:, 3:4], m[:, 2:3], 1.0 / D, None, ALU.mult)
    v.tensor_tensor(r[:, 0:1], m[:, 1:2], m[:, 1:2], ALU.mult)
    v.tensor_tensor(r[:, 1:2], m[:, 3:4], r[:, 0:1], ALU.subtract)
    v.tensor_scalar(r[:, 1:2], r[:, 1:2], eps, None, ALU.add)
    s.sqrt(r[:, 2:3], r[:, 1:2])
    v.reciprocal(r[:, 3:4], r[:, 2:3])
    v.tensor_tensor(m[:, 0:1], m[:, 1:2], r[:, 3:4], ALU.mult)
    v.tensor_scalar(m[:, 0:1], m[:, 0:1], -1.0, None, ALU.mult)
    v.tensor_scalar(dst_tile, src_tile, r[:, 3:4], m[:, 0:1], ALU.mult, ALU.add)



def emit_tm_ln(block, sync_waits, x_src, out_dst, n_tiles, buffers, eps=1e-5,
               ld=None, vs=None, sv=None, st=None):
    pass


def build_prep(nc=None, io=None):
    if nc is None:
        nc = bass.Bass("TRN2", target_bir_lowering=False, debug=False)
    DT = _mk_dt(nc, io)
    x = DT("x", [2, CHH, D], F32, "ExternalInput")
    xn = DT("xn", [2, CHH, D], BF16, "ExternalOutput")
    NT = CHH // 128
    n_tiles = 2 * NT
    with (
        nc.sbuf_tensor("xt", [128, 2 * D], F32) as xt,
        nc.sbuf_tensor("ot", [128, 2 * D], BF16) as ot,
        nc.sbuf_tensor("scr", [128, D], F32) as scr,
        nc.sbuf_tensor("mm_", [128, 4], F32) as mc,
        nc.sbuf_tensor("rr_", [128, 4], F32) as rc,
        nc.semaphore("ld") as ld,
        nc.semaphore("ldo") as ldo,
        nc.semaphore("sv") as sv,
        nc.semaphore("vs") as vs,
        nc.semaphore("vch") as vch,
        nc.semaphore("st") as st,
        nc.Block() as block,
    ):
        @block.sync
        def _(sync):
            for i in range(min(2, n_tiles)):
                ch, t = divmod(i, NT)
                sync.dma_start(out=xt[:, (i % 2) * D:(i % 2 + 1) * D],
                               in_=x[ch, t * 128:(t + 1) * 128, :]).then_inc(ld if i % 2 == 0 else ldo, 16)
            for i in range(n_tiles):
                ch, t = divmod(i, NT)
                buf = i % 2
                sync.wait_ge(vs, 2 * i + 2)
                sync.dma_start(out=xn[ch, t * 128:(t + 1) * 128, :],
                               in_=ot[:, buf * D:(buf + 1) * D]).then_inc(st, 16)
                if i + 2 < n_tiles:
                    ch2, t2 = divmod(i + 2, NT)
                    sync.dma_start(out=xt[:, buf * D:(buf + 1) * D],
                                   in_=x[ch2, t2 * 128:(t2 + 1) * 128, :]).then_inc(ld if (i + 2) % 2 == 0 else ldo, 16)
            sync.wait_ge(st, 16 * n_tiles)

        @block.vector
        def _(v):
            vc = [0]
            def ch(inst):
                vc[0] += 1
                inst.then_inc(vch, 1)
                v.wait_ge(vch, vc[0])
            for i in range(n_tiles):
                buf = i % 2
                src = xt[:, buf * D:(buf + 1) * D]
                v.wait_ge(ld if i % 2 == 0 else ldo, 16 * (i // 2 + 1))
                if i >= 1:
                    v.wait_ge(vs, 2 * i)
                ch(v.reduce_sum(mc[:, 0:1], src, axis=AX.X))
                ch(v.tensor_scalar(mc[:, 1:2], mc[:, 0:1], 1.0 / D, None, ALU.mult))
                v.wait_ge(sv, 2 * i + 1)
                ch(v.tensor_scalar(mc[:, 3:4], mc[:, 2:3], 1.0 / D, None, ALU.mult))
                ch(v.tensor_tensor(rc[:, 0:1], mc[:, 1:2], mc[:, 1:2], ALU.mult))
                ch(v.tensor_tensor(rc[:, 1:2], mc[:, 3:4], rc[:, 0:1], ALU.subtract))
                ch(v.tensor_scalar(rc[:, 1:2], rc[:, 1:2], 1e-5, None, ALU.add))
                v.sem_inc(vs, 1)
                v.wait_ge(sv, 2 * i + 2)
                ch(v.reciprocal(rc[:, 3:4], rc[:, 2:3]))
                ch(v.tensor_tensor(mc[:, 0:1], mc[:, 1:2], rc[:, 3:4], ALU.mult))
                ch(v.tensor_scalar(mc[:, 0:1], mc[:, 0:1], -1.0, None, ALU.mult))
                if i >= 2:
                    v.wait_ge(st, 16 * (i - 1))
                v.tensor_scalar(ot[:, buf * D:(buf + 1) * D], src, rc[:, 3:4],
                                mc[:, 0:1], ALU.mult, ALU.add).then_inc(vs, 1)

        @block.scalar
        def _(s):
            for i in range(n_tiles):
                buf = i % 2
                src = xt[:, buf * D:(buf + 1) * D]
                s.wait_ge(ld if i % 2 == 0 else ldo, 16 * (i // 2 + 1))
                if i >= 1:
                    s.wait_ge(vs, 2 * (i - 1) + 2)
                s.activation(out=scr[:], in_=src, func=AF.Square,
                             accum_out=mc[:, 2:3]).then_inc(sv, 1)
                s.wait_ge(vs, 2 * i + 1)
                s.sqrt(rc[:, 2:3], rc[:, 1:2]).then_inc(sv, 1)
                s.wait_ge(sv, 2 * i + 2)
    return nc


def host_col8(vec):
    return np.ascontiguousarray(vec.reshape(8, 128).T).astype(np.float32)


def host_conv_weights(conv1_w, conv1_b, conv2_w, conv2_b):
    w1g = conv1_w.reshape(1024, 4, 5).reshape(1024, 20)
    b1g = np.ascontiguousarray(conv1_b.reshape(1024, 4))
    w2g = conv2_w.reshape(1024, 20)
    return w1g.astype(np.float32), b1g.astype(np.float32), w2g.astype(np.float32), host_col8(conv2_b)


def host_conv_diag(conv1_w, conv2_w):
    w1 = conv1_w.reshape(1024, 4, 5)
    w2 = conv2_w
    W1D = np.zeros((4, 5, 8, 128, 128), np.float32)
    W2D = np.zeros((4, 5, 8, 128, 128), np.float32)
    idx = np.arange(128)
    for j in range(4):
        for k in range(5):
            for ct in range(8):
                W1D[j, k, ct, idx, idx] = w1[ct * 128:(ct + 1) * 128, j, k]
                W2D[j, k, ct, idx, idx] = w2[ct * 128:(ct + 1) * 128, j, k]
    W1D = np.ascontiguousarray(W1D.reshape(160, 128, 128).transpose(1, 0, 2).reshape(128, 160 * 128))
    W2D = np.ascontiguousarray(W2D.reshape(160, 128, 128).transpose(1, 0, 2).reshape(128, 160 * 128))
    return W1D, W2D


def build_conv(nc=None, io=None):
    WI = 1040
    WG = 1028
    NCH1 = [(0, 512), (512, 512), (1024, 4)]
    if nc is None:
        nc = bass.Bass("TRN2", target_bir_lowering=False, debug=False)
    DT = _mk_dt(nc, io)
    xn = DT("xn", [2, CHH, D], BF16, "ExternalInput")
    w1d = DT("w1d", [128, 160 * 128], F32, "ExternalInput")
    w2d = DT("w2d", [128, 160 * 128], F32, "ExternalInput")
    b1gt = DT("b1g", [D, 4], F32, "ExternalInput")
    b2 = DT("b2", [128, 8], F32, "ExternalInput")
    g0w = DT("g0w", [128, 8], F32, "ExternalInput")
    b0w = DT("b0w", [128, 8], F32, "ExternalInput")
    h0 = DT("h0", [D, T], BF16, "ExternalOutput")

    with contextlib.ExitStack() as ctx:
        E = ctx.enter_context
        E(nc.allow_low_precision(reason="bf16 pipeline; tolerance measured for bf16 matmuls"))
        xnf = E(nc.sbuf_tensor("xnf", [128, 8 * WI], BF16))
        gbuf = E(nc.sbuf_tensor("gbuf", [128, 32 * WG], BF16))
        w1ds = E(nc.sbuf_tensor("w1ds", [128, 160 * 128], BF16))
        w2ds = E(nc.sbuf_tensor("w2ds", [128, 160 * 128], BF16))
        rsb = E(nc.sbuf_tensor("rsb", [128, 8 * 1024], BF16))
        sqb = E(nc.sbuf_tensor("sqb", [128, 2 * 1024], BF16))
        b1s = E(nc.sbuf_tensor("b1s", [128, 32], F32))
        b2s = E(nc.sbuf_tensor("b2s", [128, 8], F32))
        g0s = E(nc.sbuf_tensor("g0s", [128, 8], F32))
        b0s = E(nc.sbuf_tensor("b0s", [128, 8], F32))
        onesb = E(nc.sbuf_tensor("onesb", [128, 1], BF16))
        onesr = E(nc.sbuf_tensor("onesr", [1, 128], F32))
        stat = E(nc.sbuf_tensor("stat", [1, 5 * 1024], F32))
        mbs = E(nc.sbuf_tensor("mbs", [128, 1024], BF16))
        rbs = E(nc.sbuf_tensor("rbs", [128, 1024], BF16))
        ps = E(nc.psum_tensor("cps", [128, 8 * 512], F32))
        ini = E(nc.semaphore("ini"))
        ld = E(nc.semaphore("ld"))
        ldc = E(nc.semaphore("ldc"))
        ldw = E(nc.semaphore("ldw"))
        mm1 = E(nc.semaphore("mm1"))
        gl = E(nc.semaphore("gl"))
        mm2 = E(nc.semaphore("mm2"))
        rdn = E(nc.semaphore("rdn"))
        sqs = E(nc.semaphore("sqs"))
        mms = E(nc.semaphore("mms"))
        mmd = E(nc.semaphore("mmd"))
        std = E(nc.semaphore("std"))
        bcd = E(nc.semaphore("bcd"))
        apd = E(nc.semaphore("apd"))
        vch = E(nc.semaphore("vch"))
        st = E(nc.semaphore("st"))
        block = E(nc.Block())

        vch_n = [0]
        NCONST = 8 + 3

        def wslice(ws, j, k, ct):
            base = ((j * 5 + k) * 8 + ct) * 128
            return ws[:, base:base + 128]

        @block.gpsimd
        def _(g):
            g.memset(onesb[:], 1.0)
            g.memset(onesr[:], 1.0).then_inc(ini, 1)
            g.dma_start(out=w1ds[:], in_=w1d).then_inc(ldw, 16)
            g.dma_start(out=w2ds[:], in_=w2d).then_inc(ldw, 16)

        @block.sync
        def _(sync):
            for ct in range(8):
                sync.dma_start(out=b1s[:, ct * 4:(ct + 1) * 4], in_=b1gt[ct * 128:(ct + 1) * 128, :]).then_inc(ldc, 16)
            sync.dma_start(out=b2s[:], in_=b2).then_inc(ldc, 16)
            sync.dma_start(out=g0s[:], in_=g0w).then_inc(ldc, 16)
            sync.dma_start(out=b0s[:], in_=b0w).then_inc(ldc, 16)
            for ch_idx in range(2):
                for ct in range(8):
                    sync.dma_start_transpose(xnf[:, ct * WI:(ct + 1) * WI],
                                             xn[ch_idx, 56:56 + WI, ct * 128:(ct + 1) * 128]).then_inc(ld, 16)
                for ct in range(8):
                    sync.wait_ge(apd, 8 * ch_idx + ct + 1)
                    sync.dma_start(out=h0[ct * 128:(ct + 1) * 128, ch_idx * 1024:(ch_idx + 1) * 1024],
                                   in_=gbuf[:, ct * WG:ct * WG + 1024]).then_inc(st, 16)
            sync.wait_ge(st, 16 * 16)

        @block.tensor
        def _(t):
            t.wait_ge(ini, 1)
            t.wait_ge(ldc, 16 * NCONST)
            t.wait_ge(ldw, 32)
            n1 = 12 * 8
            n2 = 16
            for ch_idx in range(2):
                t.wait_ge(ld, 16 * 8 * (ch_idx + 1))
                gidx = 0
                if ch_idx > 0:
                    t.wait_ge(bcd, ch_idx)
                for ct in range(8):
                    for j in range(4):
                        for (off, w) in NCH1:
                            bank = gidx % 2
                            if gidx >= 2:
                                t.wait_ge(gl, 96 * ch_idx + gidx - 1)
                            last = None
                            for k in range(5):
                                last = t.matmul(ps[:, bank * 512:bank * 512 + w],
                                                wslice(w1ds, j, k, ct),
                                                xnf[:, ct * WI + 4 + k + off:ct * WI + 4 + k + off + w],
                                                start=(k == 0), stop=(k == 4))
                            last.then_inc(mm1, 1)
                            gidx += 1
                for ct in range(8):
                    t.wait_ge(gl, 96 * ch_idx + 12 * (ct + 1))
                    for ni, (off, w) in enumerate([(0, 512), (512, 512)]):
                        grp = ct * 2 + ni
                        bank = 2 + grp % 2
                        if grp >= 2:
                            t.wait_ge(rdn, 8 * ch_idx * 2 + grp - 1)
                        last = None
                        mmi = 0
                        for j in range(4):
                            for k in range(5):
                                last = t.matmul(ps[:, bank * 512:bank * 512 + w],
                                                wslice(w2ds, j, k, ct),
                                                gbuf[:, (j * 8 + ct) * WG + off + k:(j * 8 + ct) * WG + off + k + w],
                                                start=(mmi == 0), stop=(mmi == 19))
                                mmi += 1
                        last.then_inc(mm2, 1)
                if ch_idx > 0:
                    t.wait_ge(std, 3 * (ch_idx - 1) + 1)
                for ct in range(8):
                    t.wait_ge(rdn, 16 * ch_idx + 2 * (ct + 1))
                    for half in range(2):
                        t.matmul(ps[0:1, (4 + half) * 512:(5 + half) * 512], onesb[:, 0:1],
                                 rsb[:, ct * 1024 + half * 512:ct * 1024 + (half + 1) * 512],
                                 start=(ct == 0), stop=(ct == 7))
                    t.wait_ge(sqs, 8 * ch_idx + ct + 1)
                    last = None
                    for half in range(2):
                        last = t.matmul(ps[0:1, (6 + half) * 512:(7 + half) * 512], onesb[:, 0:1],
                                        sqb[:, (ct % 2) * 1024 + half * 512:(ct % 2) * 1024 + (half + 1) * 512],
                                        start=(ct == 0), stop=(ct == 7))
                    if ct < 7:
                        last.then_inc(mms, 1)
                    else:
                        last.then_inc(mmd, 1)
                t.wait_ge(std, 3 * ch_idx + 3)
                if ch_idx > 0:
                    t.wait_ge(bcd, ch_idx)
                last = None
                for half in range(2):
                    t.matmul(ps[:, (0 + half) * 512:(1 + half) * 512], onesr[0:1, :],
                             stat[0:1, half * 512:512 + half * 512], start=True, stop=True)
                    last = t.matmul(ps[:, (2 + half) * 512:(3 + half) * 512], onesr[0:1, :],
                                    stat[0:1, 3072 + half * 512:3584 + half * 512], start=True, stop=True)
                last.then_inc(mmd, 1)

        @block.scalar
        def _(s):
            s.wait_ge(ldc, 16 * NCONST)
            for ch_idx in range(2):
                if ch_idx > 0:
                    s.wait_ge(st, 16 * 8)
                gidx = 0
                for ct in range(8):
                    for j in range(4):
                        for (off, w) in NCH1:
                            s.wait_ge(mm1, 96 * ch_idx + gidx + 1)
                            bank = gidx % 2
                            s.activation(out=gbuf[:, (j * 8 + ct) * WG + off:(j * 8 + ct) * WG + off + w],
                                         in_=ps[:, bank * 512:bank * 512 + w],
                                         func=AF.Gelu, bias=b1s[:, ct * 4 + j:ct * 4 + j + 1]).then_inc(gl, 1)
                            gidx += 1
                for ct in range(8):
                    s.wait_ge(rdn, 16 * ch_idx + 2 * (ct + 1))
                    if ct >= 2:
                        s.wait_ge(mms, 7 * ch_idx + ct - 1)
                    s.activation(out=sqb[:, (ct % 2) * 1024:(ct % 2 + 1) * 1024],
                                 in_=rsb[:, ct * 1024:(ct + 1) * 1024], func=AF.Square).then_inc(sqs, 1)
                s.wait_ge(std, 3 * ch_idx + 1)
                s.sqrt(stat[0:1, 4096:4608], stat[0:1, 2048:2560])
                s.sqrt(stat[0:1, 4608:5120], stat[0:1, 2560:3072]).then_inc(std, 1)
                s.wait_ge(mmd, 2 * ch_idx + 2)
                s.copy(mbs[:, 0:512], ps[:, 0:512])
                s.copy(mbs[:, 512:1024], ps[:, 512:1024])
                s.copy(rbs[:, 0:512], ps[:, 1024:1536])
                s.copy(rbs[:, 512:1024], ps[:, 1536:2048]).then_inc(bcd, 1)

        @block.vector
        def _(v):
            def chv(inst):
                vch_n[0] += 1
                inst.then_inc(vch, 1)
                v.wait_ge(vch, vch_n[0])
            v.wait_ge(ldc, 16 * NCONST)
            for ch_idx in range(2):
                for ct in range(8):
                    for ni, (off, w) in enumerate([(0, 512), (512, 512)]):
                        grp = ct * 2 + ni
                        bank = 2 + grp % 2
                        v.wait_ge(mm2, 16 * ch_idx + grp + 1)
                        xin = xnf[:, ct * WI + 8 + off:ct * WI + 8 + off + w]
                        v.scalar_tensor_tensor(rsb[:, ct * 1024 + off:ct * 1024 + off + w],
                                               ps[:, bank * 512:bank * 512 + w],
                                               b2s[:, ct:ct + 1], xin,
                                               ALU.add, ALU.add).then_inc(rdn, 1)
                v.wait_ge(mmd, 2 * ch_idx + 1)
                for half in range(2):
                    sm = ps[0:1, (4 + half) * 512:(5 + half) * 512]
                    sq_ = ps[0:1, (6 + half) * 512:(7 + half) * 512]
                    m_ = stat[0:1, half * 512:512 + half * 512]
                    msq = stat[0:1, 1024 + half * 512:1536 + half * 512]
                    var = stat[0:1, 2048 + half * 512:2560 + half * 512]
                    chv(v.tensor_scalar(m_, sm, 1.0 / 1024, None, ALU.mult))
                    chv(v.tensor_scalar(msq, sq_, 1.0 / 1024, None, ALU.mult))
                    chv(v.tensor_tensor(var, m_, m_, ALU.mult))
                    chv(v.tensor_tensor(var, msq, var, ALU.subtract))
                    chv(v.tensor_scalar(var, var, 1e-12, None, ALU.add))
                v.sem_inc(std, 1)
                v.wait_ge(std, 3 * ch_idx + 2)
                for half in range(2):
                    chv(v.reciprocal(stat[0:1, 3072 + half * 512:3584 + half * 512],
                                     stat[0:1, 4096 + half * 512:4608 + half * 512]))
                v.sem_inc(std, 1)
                v.wait_ge(bcd, ch_idx + 1)
                for ct in range(8):
                    r_ = rsb[:, ct * 1024:(ct + 1) * 1024]
                    dst = gbuf[:, ct * WG:ct * WG + 1024]
                    chv(v.tensor_tensor(dst, r_, mbs[:], ALU.subtract))
                    chv(v.tensor_tensor(dst, dst, rbs[:], ALU.mult))
                    v.tensor_scalar(dst, dst, g0s[:, ct:ct + 1], b0s[:, ct:ct + 1],
                                    ALU.mult, ALU.add).then_inc(apd, 1)
    return nc


def build_la_a(nc=None, io=None):
    if nc is None:
        nc = bass.Bass("TRN2", target_bir_lowering=False, debug=False)
    DT = _mk_dt(nc, io)
    h = DT("h", [D, T], BF16, "ExternalInput")
    wq = DT("wq", [D, D], F32, "ExternalInput")
    wk = DT("wk", [D, D], F32, "ExternalInput")
    wv = DT("wv", [D, D], F32, "ExternalInput")
    bq = DT("bq", [128, 8], F32, "ExternalInput")
    bk = DT("bk", [128, 8], F32, "ExternalInput")
    bvr = DT("bvr", [1, D], F32, "ExternalInput")
    cosd = DT("cosd", [D, T], BF16, "ExternalInput")
    sind = DT("sind", [D, T], BF16, "ExternalInput")
    swm = DT("swm", [128, 128], BF16, "ExternalInput")
    kvp = DT("kvp", [128, 1032], F32, "ExternalOutput")
    qo = DT("q", [D, T], BF16, "ExternalOutput")
    qro = DT("qr", [D, T], BF16, "ExternalOutput")
    dkr = DT("dkr", [D, T], BF16, "Internal")

    NCHK = 4
    with contextlib.ExitStack() as ctx:
        E = ctx.enter_context
        E(nc.allow_low_precision(reason="bf16 pipeline; tolerance measured for bf16 matmuls"))
        hs = E(nc.sbuf_tensor("hs", [128, 8 * T], BF16))
        wqs = E(nc.sbuf_tensor("wqs", [128, 8 * D], BF16))
        wks = E(nc.sbuf_tensor("wks", [128, 8 * D], BF16))
        wvs = E(nc.sbuf_tensor("wvs", [128, 8 * D], BF16))
        bqs = E(nc.sbuf_tensor("bqs", [128, 8], F32))
        bks = E(nc.sbuf_tensor("bks", [128, 8], F32))
        bvb = E(nc.sbuf_tensor("bvb", [128, D], BF16))
        cs = E(nc.sbuf_tensor("cs", [128, 2 * 8 * 512], BF16))
        ss = E(nc.sbuf_tensor("ss", [128, 2 * 8 * 512], BF16))
        sws = E(nc.sbuf_tensor("sws", [128, 128], BF16))
        tmp = E(nc.sbuf_tensor("tmp", [128, 8 * 512], BF16))
        kqb = E(nc.sbuf_tensor("kqb", [128, 8 * 512], BF16))
        krb = E(nc.sbuf_tensor("krb", [128, 8 * 512], BF16))
        t1b = E(nc.sbuf_tensor("t1b", [128, 8 * 512], BF16))
        ksum = E(nc.sbuf_tensor("ksum", [128, 10], F32))
        vtm = E(nc.sbuf_tensor("vtm", [128, 2 * 4 * 1024], BF16))
        ktm = E(nc.sbuf_tensor("ktm", [128, 2 * 4 * 1024], BF16))
        kvo = E(nc.sbuf_tensor("kvo", [128, 1032], F32))
        onesr = E(nc.sbuf_tensor("onesr", [1, 128], F32))
        ps = E(nc.psum_tensor("laps", [128, 8 * 512], F32))
        ini = E(nc.semaphore("ini"))
        lh = E(nc.semaphore("lh"))
        ldw = E(nc.semaphore("ldw"))
        ldc = E(nc.semaphore("ldc"))
        ldre = E(nc.semaphore("ldre"))
        ldro = E(nc.semaphore("ldro"))
        mmp = E(nc.semaphore("mmp"))
        evp = E(nc.semaphore("evp"))
        vk1 = E(nc.semaphore("vk1"))
        mmsw = E(nc.semaphore("mmsw"))
        rp1 = E(nc.semaphore("rp1"))
        vkr = E(nc.semaphore("vkr"))
        stkr = E(nc.semaphore("stkr"))
        stq = E(nc.semaphore("stq"))
        ltr = E(nc.semaphore("ltr"))
        mmv = E(nc.semaphore("mmv"))
        vv = E(nc.semaphore("vv"))
        mmkv = E(nc.semaphore("mmkv"))
        kvev = E(nc.semaphore("kvev"))
        evv = E(nc.semaphore("evv"))
        vch = E(nc.semaphore("vch"))
        stf = E(nc.semaphore("stf"))
        block = E(nc.Block())

        vch_n = [0]

        @block.gpsimd
        def _(g):
            g.memset(onesr[:], 1.0).then_inc(ini, 1)
            for wsrc, wdst in ((wq, wqs), (wk, wks), (wv, wvs)):
                for kt in range(8):
                    g.dma_start(out=wdst[:, kt * D:(kt + 1) * D],
                                in_=wsrc[kt * 128:(kt + 1) * 128, :]).then_inc(ldw, 16)

        @block.sync
        def _(sync):
            for kt in range(8):
                sync.dma_start(out=hs[:, kt * T:(kt + 1) * T], in_=h[kt * 128:(kt + 1) * 128, :]).then_inc(lh, 16)
            sync.dma_start(out=bqs[:], in_=bq).then_inc(ldc, 16)
            sync.dma_start(out=bks[:], in_=bk).then_inc(ldc, 16)
            sync.dma_start(out=sws[:], in_=swm).then_inc(ldc, 16)
            sync.dma_start(out=kvo[0:1, 0:1024], in_=bvr).then_inc(ldc, 16)
            for tc in range(min(2, NCHK)):
                sem = ldre if tc % 2 == 0 else ldro
                for mt in range(8):
                    buf = (tc % 2) * 8
                    sync.dma_start(out=cs[:, (buf + mt) * 512:(buf + mt + 1) * 512],
                                   in_=cosd[mt * 128:(mt + 1) * 128, tc * 512:(tc + 1) * 512]).then_inc(sem, 16)
                    sync.dma_start(out=ss[:, (buf + mt) * 512:(buf + mt + 1) * 512],
                                   in_=sind[mt * 128:(mt + 1) * 128, tc * 512:(tc + 1) * 512]).then_inc(sem, 16)
            for tc in range(NCHK):
                for mt in range(8):
                    sync.wait_ge(vkr, 16 * tc + mt + 1)
                    sync.dma_start(out=dkr[mt * 128:(mt + 1) * 128, tc * 512:(tc + 1) * 512],
                                   in_=krb[:, mt * 512:(mt + 1) * 512]).then_inc(stkr, 16)
                for tb in range(4):
                    sync.wait_ge(stkr, 16 * 8 * (tc + 1))
                    if tc >= 2:
                        sync.wait_ge(mmkv, 4 * (tc - 1))
                    sync.dma_start_transpose(ktm[:, ((tc % 2) * 4 + tb) * 1024:((tc % 2) * 4 + tb + 1) * 1024],
                                             dkr[:, tc * 512 + tb * 128:tc * 512 + (tb + 1) * 128]).then_inc(ltr, 16)
                for mt in range(8):
                    sync.wait_ge(vkr, 16 * tc + 8 + mt + 1)
                    sync.dma_start(out=qo[mt * 128:(mt + 1) * 128, tc * 512:(tc + 1) * 512],
                                   in_=kqb[:, mt * 512:(mt + 1) * 512]).then_inc(stq, 16)
                    sync.dma_start(out=qro[mt * 128:(mt + 1) * 128, tc * 512:(tc + 1) * 512],
                                   in_=krb[:, mt * 512:(mt + 1) * 512]).then_inc(stq, 16)
                if tc + 2 < NCHK:
                    sem = ldre if tc % 2 == 0 else ldro
                    for mt in range(8):
                        buf = (tc % 2) * 8
                        sync.dma_start(out=cs[:, (buf + mt) * 512:(buf + mt + 1) * 512],
                                       in_=cosd[mt * 128:(mt + 1) * 128, (tc + 2) * 512:(tc + 3) * 512]).then_inc(sem, 16)
                        sync.dma_start(out=ss[:, (buf + mt) * 512:(buf + mt + 1) * 512],
                                       in_=sind[mt * 128:(mt + 1) * 128, (tc + 2) * 512:(tc + 3) * 512]).then_inc(sem, 16)
            sync.wait_ge(evv, 2)
            sync.dma_start(out=kvp, in_=kvo[:]).then_inc(stf, 16)
            sync.wait_ge(stf, 16)
            sync.wait_ge(stq, 16 * 16 * NCHK)
            sync.wait_ge(stkr, 16 * 8 * NCHK)
            sync.wait_ge(ltr, 64 * NCHK)
            sync.wait_ge(lh, 16 * 8)
            sync.wait_ge(ldw, 16 * 24)
            sync.wait_ge(ldc, 16 * 4)
            sync.wait_ge(ldre, 16 * 32)
            sync.wait_ge(ldro, 16 * 32)

        @block.tensor
        def _(t):
            t.wait_ge(ini, 1)
            t.wait_ge(lh, 16 * 8)
            t.wait_ge(ldw, 16 * 24)
            t.wait_ge(ldc, 16 * 4)
            t.matmul(ps[:, 3 * 512:4 * 512], onesr[0:1, :], kvo[0:1, 0:512], start=True, stop=True)
            t.matmul(ps[:, 2 * 512:3 * 512], onesr[0:1, :], kvo[0:1, 512:1024],
                     start=True, stop=True).then_inc(mmsw, 1)
            pgrp = 0
            sgrp = 0
            for tc in range(NCHK):
                for phase in range(2):
                    wsrc = wks if phase == 0 else wqs
                    for mt in range(8):
                        bank = pgrp % 2
                        if pgrp >= 2:
                            t.wait_ge(evp, pgrp - 1)
                        last = None
                        for kt in range(8):
                            last = t.matmul(ps[:, bank * 512:(bank + 1) * 512],
                                            wsrc[:, kt * D + mt * 128:kt * D + (mt + 1) * 128],
                                            hs[:, kt * T + tc * 512:kt * T + (tc + 1) * 512],
                                            start=(kt == 0), stop=(kt == 7))
                        last.then_inc(mmp, 1)
                        pgrp += 1
                    for mt in range(8):
                        bank = 2 + sgrp % 2
                        if sgrp == 0:
                            t.wait_ge(evv, 1)
                        if sgrp >= 2:
                            t.wait_ge(rp1, sgrp - 1)
                        t.wait_ge(vk1, 16 * tc + phase * 8 + mt + 1)
                        t.matmul(ps[:, bank * 512:(bank + 1) * 512], sws[:],
                                 kqb[:, mt * 512:(mt + 1) * 512], start=True, stop=True).then_inc(mmsw, 1)
                        sgrp += 1
                    if phase == 0:
                        for tb in range(4):
                            g = tc * 4 + tb
                            if g >= 1:
                                t.wait_ge(vv, g)
                            last = None
                            for half in range(2):
                                bank = 6 + half
                                for kt in range(8):
                                    last = t.matmul(ps[:, bank * 512:(bank + 1) * 512],
                                                    hs[:, kt * T + tc * 512 + tb * 128:kt * T + tc * 512 + (tb + 1) * 128],
                                                    wvs[:, kt * D + half * 512:kt * D + (half + 1) * 512],
                                                    start=(kt == 0), stop=(kt == 7))
                                last.then_inc(mmv, 1)
                        for tb in range(4):
                            g = tc * 4 + tb
                            t.wait_ge(ltr, 64 * (tc + 1))
                            t.wait_ge(vv, g + 1)
                            if g >= 1:
                                t.wait_ge(kvev, g)
                            ring = (tc % 2) * 4 + tb
                            last = None
                            for hp in range(8):
                                bank = 4 + hp // 4
                                reg = bank * 512 + (hp % 4) * 128
                                last = t.matmul(ps[:, reg:reg + 128],
                                                ktm[:, ring * 1024 + hp * 128:ring * 1024 + (hp + 1) * 128],
                                                vtm[:, ring * 1024 + hp * 128:ring * 1024 + (hp + 1) * 128],
                                                start=True, stop=True)
                            last.then_inc(mmkv, 1)

        @block.scalar
        def _(s):
            s.wait_ge(ldc, 16 * 4)
            s.wait_ge(mmsw, 1)
            s.copy(bvb[:, 0:512], ps[:, 3 * 512:4 * 512])
            s.copy(bvb[:, 512:1024], ps[:, 2 * 512:3 * 512]).then_inc(evv, 1)
            pgrp = 0
            for tc in range(NCHK):
                for phase in range(2):
                    bias = bks if phase == 0 else bqs
                    for mt in range(8):
                        bank = pgrp % 2
                        s.wait_ge(mmp, pgrp + 1)
                        if pgrp >= 2:
                            s.wait_ge(vk1, pgrp - 1)
                        s.activation(out=tmp[:, mt * 512:(mt + 1) * 512],
                                     in_=ps[:, bank * 512:(bank + 1) * 512],
                                     func=AF.Gelu, bias=bias[:, mt:mt + 1]).then_inc(evp, 1)
                        pgrp += 1


        @block.vector
        def _(v):
            def chv(inst):
                vch_n[0] += 1
                inst.then_inc(vch, 1)
                v.wait_ge(vch, vch_n[0])
            v.wait_ge(ldc, 16 * 4)
            v.wait_ge(mmsw, 1)
            chv(v.memset(kvo[:, 0:1024], 0.0))
            pgrp = 0
            sgrp = 0
            for tc in range(NCHK):
                for phase in range(2):
                    for mt in range(8):
                        v.wait_ge(evp, pgrp + 1)
                        if phase == 0 and tc >= 1:
                            v.wait_ge(stq, 16 * 16 * tc)
                        if phase == 1:
                            v.wait_ge(mmsw, 1 + 16 * tc + mt + 1)
                        v.tensor_scalar(kqb[:, mt * 512:(mt + 1) * 512], tmp[:, mt * 512:(mt + 1) * 512],
                                        0.21, None, ALU.add).then_inc(vk1, 1)
                        pgrp += 1
                    if phase == 0:
                        for mt in range(8):
                            chv(v.reduce_sum(ksum[:, 8:9], kqb[:, mt * 512:(mt + 1) * 512], axis=AX.X))
                            if tc == 0:
                                chv(v.tensor_copy(ksum[:, mt:mt + 1], ksum[:, 8:9]))
                            else:
                                chv(v.tensor_tensor(ksum[:, mt:mt + 1], ksum[:, mt:mt + 1], ksum[:, 8:9], ALU.add))
                    for mt in range(8):
                        v.wait_ge(mmsw, 1 + sgrp + 1)
                        bank = 2 + sgrp % 2
                        buf = (tc % 2) * 8
                        v.wait_ge(ldre if tc % 2 == 0 else ldro, 16 * 16 * (tc // 2 + 1))
                        if phase == 0 and tc >= 1:
                            v.wait_ge(stq, 16 * 16 * tc)
                        if phase == 1:
                            v.wait_ge(stkr, 128 * (tc + 1))
                        t1 = v.tensor_tensor(t1b[:, mt * 512:(mt + 1) * 512],
                                             ps[:, bank * 512:(bank + 1) * 512],
                                             ss[:, (buf + mt) * 512:(buf + mt + 1) * 512], ALU.mult)
                        vch_n[0] += 1
                        t1.then_inc(vch, 1)
                        v.wait_ge(vch, vch_n[0])
                        v.sem_inc(rp1, 1)
                        chv(v.tensor_tensor(krb[:, mt * 512:(mt + 1) * 512],
                                            kqb[:, mt * 512:(mt + 1) * 512],
                                            cs[:, (buf + mt) * 512:(buf + mt + 1) * 512], ALU.mult))
                        v.tensor_tensor(krb[:, mt * 512:(mt + 1) * 512],
                                        krb[:, mt * 512:(mt + 1) * 512],
                                        t1b[:, mt * 512:(mt + 1) * 512], ALU.add).then_inc(vkr, 1)
                        sgrp += 1
                    if phase == 0:
                        for tb in range(4):
                            g = tc * 4 + tb
                            ring = (tc % 2) * 4 + tb
                            v.wait_ge(mmv, 2 * g + 2)
                            if tc >= 2:
                                v.wait_ge(mmkv, 4 * (tc - 1))
                            chv(v.tensor_tensor(vtm[:, ring * 1024:ring * 1024 + 512],
                                                ps[:, 6 * 512:7 * 512], bvb[:, 0:512], ALU.add))
                            v.tensor_tensor(vtm[:, ring * 1024 + 512:ring * 1024 + 1024],
                                            ps[:, 7 * 512:8 * 512], bvb[:, 512:1024],
                                            ALU.add).then_inc(vv, 1)
                        for tb in range(4):
                            g = tc * 4 + tb
                            v.wait_ge(mmkv, g + 1)
                            chv(v.tensor_tensor(kvo[:, 0:512], kvo[:, 0:512], ps[:, 4 * 512:5 * 512], ALU.add))
                            v.tensor_tensor(kvo[:, 512:1024], kvo[:, 512:1024], ps[:, 5 * 512:6 * 512],
                                            ALU.add).then_inc(kvev, 1)
                            v.wait_ge(kvev, g + 1)
            v.wait_ge(evv, 1)
            v.tensor_copy(kvo[:, 1024:1032], ksum[:, 0:8]).then_inc(evv, 1)
    return nc


def build_la_b(nc=None, io=None):
    if nc is None:
        nc = bass.Bass("TRN2", target_bir_lowering=False, debug=False)
    DT = _mk_dt(nc, io)
    kvp4 = DT("kvp4", [4, 128, 1032], F32, "ExternalInput")
    qd = DT("q", [D, T], BF16, "ExternalInput")
    qrd = DT("qr", [D, T], BF16, "ExternalInput")
    hd = DT("h", [D, T], BF16, "ExternalInput")
    lnag = DT("lnag", [128, 8], F32, "ExternalInput")
    lnab = DT("lnab", [128, 8], F32, "ExternalInput")
    sel2d = DT("sel2", [2, 128], BF16, "ExternalInput")
    hn = DT("hn", [D, T], BF16, "ExternalOutput")

    NCHK = 4
    with contextlib.ExitStack() as ctx:
        E = ctx.enter_context
        E(nc.allow_low_precision(reason="bf16 pipeline; tolerance measured for bf16 matmuls"))
        qs = E(nc.sbuf_tensor("qs", [128, 8 * T], BF16))
        qrs = E(nc.sbuf_tensor("qrs", [128, 8 * T], BF16))
        hs = E(nc.sbuf_tensor("hs", [128, 8 * T], BF16))
        rsb = E(nc.sbuf_tensor("rsb", [128, 8 * T], BF16))
        kv4 = E(nc.sbuf_tensor("kv4", [128, 4 * 1032], F32))
        kvs = E(nc.sbuf_tensor("kvs", [128, 1032], F32))
        kvb = E(nc.sbuf_tensor("kvb", [128, 1024], BF16))
        kmb = E(nc.sbuf_tensor("kmb", [128, 8], BF16))
        zlh = E(nc.sbuf_tensor("zlh", [128, 16], BF16))
        sel2s = E(nc.sbuf_tensor("sel2s", [2, 128], BF16))
        zrow = E(nc.sbuf_tensor("zrow", [2, 512], F32))
        zrow2 = E(nc.sbuf_tensor("zrow2", [2, 512], BF16))
        zbs = E(nc.sbuf_tensor("zbs", [128, 2 * 512], BF16))
        o1s = E(nc.sbuf_tensor("o1s", [128, 512], BF16))
        sqb = E(nc.sbuf_tensor("sqb", [128, 2 * T], BF16))
        gs = E(nc.sbuf_tensor("gs", [128, 8], F32))
        bs = E(nc.sbuf_tensor("bs", [128, 8], F32))
        onesb = E(nc.sbuf_tensor("onesb", [128, 1], BF16))
        onesrb = E(nc.sbuf_tensor("onesrb", [1, 128], BF16))
        stat = E(nc.sbuf_tensor("stat", [1, 5 * T], BF16))
        mbs = E(nc.sbuf_tensor("mbs", [128, T], BF16))
        rbs = E(nc.sbuf_tensor("rbs", [128, T], BF16))
        ps = E(nc.psum_tensor("lbps", [128, 8 * 512], F32))
        ini = E(nc.semaphore("ini"))
        lkv = E(nc.semaphore("lkv"))
        lq = E(nc.semaphore("lq"))
        lqr = E(nc.semaphore("lqr"))
        lhm = E(nc.semaphore("lhm"))
        ldc = E(nc.semaphore("ldc"))
        vkv = E(nc.semaphore("vkv"))
        szl = E(nc.semaphore("szl"))
        mo = E(nc.semaphore("mo"))
        mz = E(nc.semaphore("mz"))
        vz = E(nc.semaphore("vz"))
        mzb = E(nc.semaphore("mzb"))
        ezb = E(nc.semaphore("ezb"))
        vr = E(nc.semaphore("vr"))
        sqs = E(nc.semaphore("sqs"))
        mms = E(nc.semaphore("mms"))
        mmd = E(nc.semaphore("mmd"))
        std = E(nc.semaphore("std"))
        bcd = E(nc.semaphore("bcd"))
        apd = E(nc.semaphore("apd"))
        vch = E(nc.semaphore("vch"))
        st = E(nc.semaphore("st"))
        block = E(nc.Block())
        vch_n = [0]

        def chv_of(v):
            def chv(inst):
                vch_n[0] += 1
                inst.then_inc(vch, 1)
                v.wait_ge(vch, vch_n[0])
            return chv

        @block.gpsimd
        def _(g):
            g.memset(onesb[:], 1.0)
            g.memset(onesrb[:], 1.0)
            g.memset(zlh[:], 0.0).then_inc(ini, 1)

        @block.sync
        def _(sync):
            for i in range(4):
                sync.dma_start(out=kv4[:, i * 1032:(i + 1) * 1032], in_=kvp4[i]).then_inc(lkv, 16)
            for kt in range(8):
                sync.dma_start(out=qs[:, kt * T:(kt + 1) * T], in_=qd[kt * 128:(kt + 1) * 128, :]).then_inc(lq, 16)
                sync.dma_start(out=qrs[:, kt * T:(kt + 1) * T], in_=qrd[kt * 128:(kt + 1) * 128, :]).then_inc(lqr, 16)
                sync.dma_start(out=hs[:, kt * T:(kt + 1) * T], in_=hd[kt * 128:(kt + 1) * 128, :]).then_inc(lhm, 16)
            sync.dma_start(out=gs[:], in_=lnag).then_inc(ldc, 16)
            sync.dma_start(out=bs[:], in_=lnab).then_inc(ldc, 16)
            sync.dma_start(out=sel2s[:], in_=sel2d).then_inc(ldc, 16)
            for hp in range(8):
                sync.wait_ge(apd, hp + 1)
                sync.dma_start(out=hn[hp * 128:(hp + 1) * 128, :], in_=rsb[:, hp * T:(hp + 1) * T]).then_inc(st, 16)
            sync.wait_ge(st, 16 * 8)

        @block.vector
        def _(v):
            chv = chv_of(v)
            v.wait_ge(lkv, 16 * 4)
            chv(v.tensor_tensor(kvs[:], kv4[:, 0:1032], kv4[:, 1032:2064], ALU.add))
            chv(v.tensor_tensor(kvs[:], kvs[:], kv4[:, 2064:3096], ALU.add))
            chv(v.tensor_tensor(kvs[:], kvs[:], kv4[:, 3096:4128], ALU.add))
            chv(v.tensor_scalar(kvb[:], kvs[:, 0:1024], 1.0 / S, None, ALU.mult))
            chv(v.tensor_scalar(kmb[:], kvs[:, 1024:1032], 1.0 / S, None, ALU.mult))
            for hp in range(8):
                chv(v.memset(kvb[0:64, hp * 128 + 64:hp * 128 + 128], 0.0))
                chv(v.memset(kvb[64:128, hp * 128:hp * 128 + 64], 0.0))
            v.sem_inc(vkv, 1)
            v.wait_ge(lhm, 16 * 8)
            for hp in range(8):
                for tc in range(NCHK):
                    g = hp * NCHK + tc
                    bank = g % 2
                    v.wait_ge(mz, g + 1)
                    chv(v.tensor_scalar(zrow[0:2, :], ps[0:2, 6 * 512:7 * 512], 1e-6, None, ALU.add))
                    v.reciprocal(zrow2[0:2, :], zrow[0:2, :]).then_inc(vz, 1)
                    v.wait_ge(vz, g + 1)
                    v.wait_ge(mo, g + 1)
                    v.wait_ge(ezb, g + 1)
                    chv(v.tensor_tensor(o1s[:], ps[:, bank * 512:(bank + 1) * 512],
                                        zbs[:, (g % 2) * 512:(g % 2 + 1) * 512], ALU.mult))
                    v.tensor_tensor(rsb[:, hp * T + tc * 512:hp * T + (tc + 1) * 512], o1s[:],
                                    hs[:, hp * T + tc * 512:hp * T + (tc + 1) * 512],
                                    ALU.add).then_inc(vr, 1)
                    v.wait_ge(vr, g + 1)
            v.wait_ge(mmd, 1)
            for qh in range(4):
                sm = ps[0:1, qh * 512:(qh + 1) * 512]
                sq_ = ps[0:1, (4 + qh) * 512:(5 + qh) * 512]
                m_ = stat[0:1, qh * 512:(qh + 1) * 512]
                msq = stat[0:1, T + qh * 512:T + (qh + 1) * 512]
                var = stat[0:1, 2 * T + qh * 512:2 * T + (qh + 1) * 512]
                chv(v.tensor_scalar(m_, sm, 1.0 / 1024, None, ALU.mult))
                chv(v.tensor_scalar(msq, sq_, 1.0 / 1024, None, ALU.mult))
                chv(v.tensor_tensor(var, m_, m_, ALU.mult))
                chv(v.tensor_tensor(var, msq, var, ALU.subtract))
                chv(v.tensor_scalar(var, var, 1e-12, None, ALU.add))
            v.sem_inc(std, 1)
            v.wait_ge(std, 2)
            for qh in range(4):
                chv(v.reciprocal(stat[0:1, 3 * T + qh * 512:3 * T + (qh + 1) * 512],
                                 stat[0:1, 4 * T + qh * 512:4 * T + (qh + 1) * 512]))
            v.sem_inc(std, 1)
            v.wait_ge(bcd, 1)
            for hp in range(8):
                r_ = rsb[:, hp * T:(hp + 1) * T]
                chv(v.tensor_tensor(r_, r_, mbs[:], ALU.subtract))
                chv(v.tensor_tensor(r_, r_, rbs[:], ALU.mult))
                v.tensor_scalar(r_, r_, gs[:, hp:hp + 1], bs[:, hp:hp + 1],
                                ALU.mult, ALU.add).then_inc(apd, 1)
                v.wait_ge(apd, hp + 1)

        @block.scalar
        def _(s):
            s.wait_ge(ldc, 16 * 3)
            s.wait_ge(ini, 1)
            s.wait_ge(vkv, 1)
            for hp in range(8):
                s.copy(zlh[0:64, 2 * hp:2 * hp + 1], kmb[0:64, hp:hp + 1]).then_inc(szl, 1)
                s.copy(zlh[64:128, 2 * hp + 1:2 * hp + 2], kmb[64:128, hp:hp + 1]).then_inc(szl, 1)
            for hp in range(8):
                for tc in range(NCHK):
                    g = hp * NCHK + tc
                    s.wait_ge(mzb, g + 1)
                    if g >= 2:
                        s.wait_ge(vr, g - 1)
                    s.copy(zbs[:, (g % 2) * 512:(g % 2 + 1) * 512], ps[:, 7 * 512:8 * 512]).then_inc(ezb, 1)
            for hp in range(8):
                s.wait_ge(vr, NCHK * (hp + 1))
                if hp >= 2:
                    s.wait_ge(mms, hp - 1)
                s.activation(out=sqb[:, (hp % 2) * T:(hp % 2 + 1) * T],
                             in_=rsb[:, hp * T:(hp + 1) * T], func=AF.Square).then_inc(sqs, 1)
            s.wait_ge(std, 1)
            for qh in range(4):
                inst = s.sqrt(stat[0:1, 4 * T + qh * 512:4 * T + (qh + 1) * 512],
                              stat[0:1, 2 * T + qh * 512:2 * T + (qh + 1) * 512])
            inst.then_inc(std, 1)
            s.wait_ge(mmd, 2)
            for qh in range(4):
                s.copy(mbs[:, qh * 512:(qh + 1) * 512], ps[:, qh * 512:(qh + 1) * 512])
                inst = s.copy(rbs[:, qh * 512:(qh + 1) * 512], ps[:, (4 + qh) * 512:(5 + qh) * 512])
            inst.then_inc(bcd, 1)

        @block.tensor
        def _(t):
            t.wait_ge(ini, 1)
            t.wait_ge(lq, 16 * 8)
            t.wait_ge(lqr, 16 * 8)
            t.wait_ge(ldc, 16 * 3)
            t.wait_ge(vkv, 1)
            t.wait_ge(szl, 16)
            for hp in range(8):
                for tc in range(NCHK):
                    g = hp * NCHK + tc
                    bank = g % 2
                    if g >= 2:
                        t.wait_ge(vr, g - 1)
                    t.matmul(ps[:, bank * 512:(bank + 1) * 512], kvb[:, hp * 128:(hp + 1) * 128],
                             qrs[:, hp * T + tc * 512:hp * T + (tc + 1) * 512],
                             start=True, stop=True).then_inc(mo, 1)
                    if g >= 1:
                        t.wait_ge(vz, g)
                    t.matmul(ps[0:2, 6 * 512:7 * 512], zlh[:, 2 * hp:2 * hp + 2],
                             qs[:, hp * T + tc * 512:hp * T + (tc + 1) * 512],
                             start=True, stop=True).then_inc(mz, 1)
                    t.wait_ge(vz, g + 1)
                    if g >= 1:
                        t.wait_ge(ezb, g)
                    t.matmul(ps[:, 7 * 512:8 * 512], sel2s[:], zrow2[0:2, :],
                             start=True, stop=True).then_inc(mzb, 1)
            t.wait_ge(vr, 32)
            for hp in range(8):
                for qh in range(4):
                    t.matmul(ps[0:1, qh * 512:(qh + 1) * 512], onesb[:, 0:1],
                             rsb[:, hp * T + qh * 512:hp * T + (qh + 1) * 512],
                             start=(hp == 0), stop=(hp == 7))
                t.wait_ge(sqs, hp + 1)
                last = None
                for qh in range(4):
                    last = t.matmul(ps[0:1, (4 + qh) * 512:(5 + qh) * 512], onesb[:, 0:1],
                                    sqb[:, (hp % 2) * T + qh * 512:(hp % 2) * T + (qh + 1) * 512],
                                    start=(hp == 0), stop=(hp == 7))
                if hp < 7:
                    last.then_inc(mms, 1)
                else:
                    last.then_inc(mmd, 1)
            t.wait_ge(std, 3)
            last = None
            for qh in range(4):
                t.matmul(ps[:, qh * 512:(qh + 1) * 512], onesrb[0:1, :],
                         stat[0:1, qh * 512:(qh + 1) * 512], start=True, stop=True)
                last = t.matmul(ps[:, (4 + qh) * 512:(5 + qh) * 512], onesrb[0:1, :],
                                stat[0:1, 3 * T + qh * 512:3 * T + (qh + 1) * 512], start=True, stop=True)
            last.then_inc(mmd, 1)
    return nc


def build_ffn0(nc=None, io=None):
    if nc is None:
        nc = bass.Bass("TRN2", target_bir_lowering=False, debug=False)
    DT = _mk_dt(nc, io)
    hd = DT("h", [D, T], BF16, "ExternalInput")
    w1d = DT("w1", [D, 4096], F32, "ExternalInput")
    b1d = DT("b1c", [128, 32], F32, "ExternalInput")
    w2d = DT("w2", [4096, D], F32, "ExternalInput")
    b2d = DT("b2c", [128, 8], F32, "ExternalInput")
    lngd = DT("lng", [128, 8], F32, "ExternalInput")
    lnbd = DT("lnb", [128, 8], F32, "ExternalInput")
    so = DT("s0", [D, T], BF16, "ExternalOutput")
    dt1 = DT("dt1", [4096, T], BF16, "Internal")

    NCHK = 4
    with contextlib.ExitStack() as ctx:
        E = ctx.enter_context
        E(nc.allow_low_precision(reason="bf16 pipeline; tolerance measured for bf16 matmuls"))
        hs = E(nc.sbuf_tensor("hs", [128, 8 * T], BF16))
        wbuf = E(nc.sbuf_tensor("wbuf", [128, 32 * 1024], BF16))
        t1s = E(nc.sbuf_tensor("t1s", [128, 32 * 512], BF16))
        rsb = E(nc.sbuf_tensor("rsb", [128, 8 * T], BF16))
        sqb = E(nc.sbuf_tensor("sqb", [128, 2 * T], BF16))
        b1s = E(nc.sbuf_tensor("b1s", [128, 32], F32))
        b2s = E(nc.sbuf_tensor("b2s", [128, 8], F32))
        gs = E(nc.sbuf_tensor("gs", [128, 8], F32))
        bs = E(nc.sbuf_tensor("bs", [128, 8], F32))
        onesb = E(nc.sbuf_tensor("onesb", [128, 1], BF16))
        onesrb = E(nc.sbuf_tensor("onesrb", [1, 128], BF16))
        stat = E(nc.sbuf_tensor("stat", [1, 5 * T], BF16))
        mbs = E(nc.sbuf_tensor("mbs", [128, T], BF16))
        rbs = E(nc.sbuf_tensor("rbs", [128, T], BF16))
        ps = E(nc.psum_tensor("ffps", [128, 8 * 512], F32))
        ini = E(nc.semaphore("ini"))
        lhm = E(nc.semaphore("lhm"))
        ldw1 = E(nc.semaphore("ldw1"))
        ldw2 = E(nc.semaphore("ldw2"))
        ldc = E(nc.semaphore("ldc"))
        lt1 = E(nc.semaphore("lt1"))
        mm1 = E(nc.semaphore("mm1"))
        ev1 = E(nc.semaphore("ev1"))
        st8 = [E(nc.semaphore(f"st8_{i}")) for i in range(8)]
        mm2 = E(nc.semaphore("mm2"))
        vr = E(nc.semaphore("vr"))
        sqs = E(nc.semaphore("sqs"))
        mms = E(nc.semaphore("mms"))
        mmd = E(nc.semaphore("mmd"))
        std = E(nc.semaphore("std"))
        bcd = E(nc.semaphore("bcd"))
        apd = E(nc.semaphore("apd"))
        vch = E(nc.semaphore("vch"))
        st = E(nc.semaphore("st"))
        block = E(nc.Block())
        vch_n = [0]

        @block.gpsimd
        def _(g):
            g.memset(onesb[:], 1.0)
            g.memset(onesrb[:], 1.0).then_inc(ini, 1)
            for kt in range(8):
                g.dma_start(out=wbuf[:, kt * 4096:(kt + 1) * 4096],
                            in_=w1d[kt * 128:(kt + 1) * 128, :]).then_inc(ldw1, 16)
            g.wait_ge(mm1, 128)
            for kt in range(32):
                g.dma_start(out=wbuf[:, kt * D:(kt + 1) * D],
                            in_=w2d[kt * 128:(kt + 1) * 128, :]).then_inc(ldw2, 16)

        @block.sync
        def _(sync):
            for kt in range(8):
                sync.dma_start(out=hs[:, kt * T:(kt + 1) * T], in_=hd[kt * 128:(kt + 1) * 128, :]).then_inc(lhm, 16)
            for nm, dst in ((b1d, b1s), (b2d, b2s), (lngd, gs), (lnbd, bs)):
                sync.dma_start(out=dst[:], in_=nm).then_inc(ldc, 16)
            for m in range(32):
                for tc in range(NCHK):
                    g1 = m * 4 + tc
                    sync.wait_ge(ev1, g1 + 1)
                    sync.dma_start(out=dt1[m * 128:(m + 1) * 128, tc * 512:(tc + 1) * 512],
                                   in_=t1s[:, ((m % 8) * 4 + tc) * 512:((m % 8) * 4 + tc + 1) * 512]).then_inc(st8[m % 8], 16)
            for i in range(8):
                sync.wait_ge(st8[i], 16 * 16)
            for tc in range(NCHK):
                if tc >= 1:
                    sync.wait_ge(mm2, 8 * tc)
                for kt in range(32):
                    sync.dma_start(out=t1s[:, kt * 512:(kt + 1) * 512],
                                   in_=dt1[kt * 128:(kt + 1) * 128, tc * 512:(tc + 1) * 512]).then_inc(lt1, 16)
            for hp in range(8):
                sync.wait_ge(apd, hp + 1)
                sync.dma_start(out=so[hp * 128:(hp + 1) * 128, :], in_=rsb[:, hp * T:(hp + 1) * T]).then_inc(st, 16)
            sync.wait_ge(st, 16 * 8)
            sync.wait_ge(lt1, 16 * 128)

        @block.tensor
        def _(t):
            t.wait_ge(ini, 1)
            t.wait_ge(lhm, 16 * 8)
            t.wait_ge(ldw1, 16 * 8)
            t.wait_ge(ldc, 16 * 4)
            for m in range(32):
                grp = m % 2
                if m >= 2:
                    t.wait_ge(ev1, 4 * (m - 1))
                for kt in range(8):
                    for tc in range(NCHK):
                        bank = grp * 4 + tc
                        last = t.matmul(ps[:, bank * 512:(bank + 1) * 512],
                                        wbuf[:, kt * 4096 + m * 128:kt * 4096 + (m + 1) * 128],
                                        hs[:, kt * T + tc * 512:kt * T + (tc + 1) * 512],
                                        start=(kt == 0), stop=(kt == 7))
                        if kt == 7:
                            last.then_inc(mm1, 1)
            t.wait_ge(ldw2, 16 * 32)
            for tc in range(NCHK):
                t.wait_ge(lt1, 16 * 32 * (tc + 1))
                for m2 in range(8):
                    g2 = tc * 8 + m2
                    bank = 2 + g2 % 2
                    if g2 >= 2:
                        t.wait_ge(vr, g2 - 1)
                    last = None
                    for kt in range(32):
                        last = t.matmul(ps[:, bank * 512:(bank + 1) * 512],
                                        wbuf[:, kt * D + m2 * 128:kt * D + (m2 + 1) * 128],
                                        t1s[:, kt * 512:(kt + 1) * 512],
                                        start=(kt == 0), stop=(kt == 31))
                    last.then_inc(mm2, 1)
            t.wait_ge(vr, 32)
            for hp in range(8):
                for qh in range(4):
                    t.matmul(ps[0:1, qh * 512:(qh + 1) * 512], onesb[:, 0:1],
                             rsb[:, hp * T + qh * 512:hp * T + (qh + 1) * 512],
                             start=(hp == 0), stop=(hp == 7))
                t.wait_ge(sqs, hp + 1)
                last = None
                for qh in range(4):
                    last = t.matmul(ps[0:1, (4 + qh) * 512:(5 + qh) * 512], onesb[:, 0:1],
                                    sqb[:, (hp % 2) * T + qh * 512:(hp % 2) * T + (qh + 1) * 512],
                                    start=(hp == 0), stop=(hp == 7))
                if hp < 7:
                    last.then_inc(mms, 1)
                else:
                    last.then_inc(mmd, 1)
            t.wait_ge(std, 3)
            last = None
            for qh in range(4):
                t.matmul(ps[:, qh * 512:(qh + 1) * 512], onesrb[0:1, :],
                         stat[0:1, qh * 512:(qh + 1) * 512], start=True, stop=True)
                last = t.matmul(ps[:, (4 + qh) * 512:(5 + qh) * 512], onesrb[0:1, :],
                                stat[0:1, 3 * T + qh * 512:3 * T + (qh + 1) * 512], start=True, stop=True)
            last.then_inc(mmd, 1)

        @block.scalar
        def _(s):
            s.wait_ge(ldc, 16 * 4)
            for m in range(32):
                grp = m % 2
                for tc in range(NCHK):
                    g1 = m * 4 + tc
                    bank = grp * 4 + tc
                    s.wait_ge(mm1, g1 + 1)
                    if m >= 8:
                        s.wait_ge(st8[m % 8], 64 * (m // 8))
                    s.activation(out=t1s[:, ((m % 8) * 4 + tc) * 512:((m % 8) * 4 + tc + 1) * 512],
                                 in_=ps[:, bank * 512:(bank + 1) * 512],
                                 func=AF.Gelu, bias=b1s[:, m:m + 1]).then_inc(ev1, 1)
            for hp in range(8):
                s.wait_ge(vr, 24 + hp + 1)
                if hp >= 2:
                    s.wait_ge(mms, hp - 1)
                s.activation(out=sqb[:, (hp % 2) * T:(hp % 2 + 1) * T],
                             in_=rsb[:, hp * T:(hp + 1) * T], func=AF.Square).then_inc(sqs, 1)
            s.wait_ge(std, 1)
            inst = None
            for qh in range(4):
                inst = s.sqrt(stat[0:1, 4 * T + qh * 512:4 * T + (qh + 1) * 512],
                              stat[0:1, 2 * T + qh * 512:2 * T + (qh + 1) * 512])
            inst.then_inc(std, 1)
            s.wait_ge(mmd, 2)
            inst = None
            for qh in range(4):
                s.copy(mbs[:, qh * 512:(qh + 1) * 512], ps[:, qh * 512:(qh + 1) * 512])
                inst = s.copy(rbs[:, qh * 512:(qh + 1) * 512], ps[:, (4 + qh) * 512:(5 + qh) * 512])
            inst.then_inc(bcd, 1)

        @block.vector
        def _(v):
            def chv(inst):
                vch_n[0] += 1
                inst.then_inc(vch, 1)
                v.wait_ge(vch, vch_n[0])
            v.wait_ge(ldc, 16 * 4)
            v.wait_ge(lhm, 16 * 8)
            for tc in range(NCHK):
                for m2 in range(8):
                    g2 = tc * 8 + m2
                    bank = 2 + g2 % 2
                    v.wait_ge(mm2, g2 + 1)
                    v.scalar_tensor_tensor(rsb[:, m2 * T + tc * 512:m2 * T + (tc + 1) * 512],
                                           ps[:, bank * 512:(bank + 1) * 512], b2s[:, m2:m2 + 1],
                                           hs[:, m2 * T + tc * 512:m2 * T + (tc + 1) * 512],
                                           ALU.add, ALU.add).then_inc(vr, 1)
                    v.wait_ge(vr, g2 + 1)
            v.wait_ge(mmd, 1)
            for qh in range(4):
                sm = ps[0:1, qh * 512:(qh + 1) * 512]
                sq_ = ps[0:1, (4 + qh) * 512:(5 + qh) * 512]
                m_ = stat[0:1, qh * 512:(qh + 1) * 512]
                msq = stat[0:1, T + qh * 512:T + (qh + 1) * 512]
                var = stat[0:1, 2 * T + qh * 512:2 * T + (qh + 1) * 512]
                chv(v.tensor_scalar(m_, sm, 1.0 / 1024, None, ALU.mult))
                chv(v.tensor_scalar(msq, sq_, 1.0 / 1024, None, ALU.mult))
                chv(v.tensor_tensor(var, m_, m_, ALU.mult))
                chv(v.tensor_tensor(var, msq, var, ALU.subtract))
                chv(v.tensor_scalar(var, var, 1e-12, None, ALU.add))
            v.sem_inc(std, 1)
            v.wait_ge(std, 2)
            for qh in range(4):
                chv(v.reciprocal(stat[0:1, 3 * T + qh * 512:3 * T + (qh + 1) * 512],
                                 stat[0:1, 4 * T + qh * 512:4 * T + (qh + 1) * 512]))
            v.sem_inc(std, 1)
            v.wait_ge(bcd, 1)
            for hp in range(8):
                r_ = rsb[:, hp * T:(hp + 1) * T]
                chv(v.tensor_tensor(r_, r_, mbs[:], ALU.subtract))
                chv(v.tensor_tensor(r_, r_, rbs[:], ALU.mult))
                v.tensor_scalar(r_, r_, gs[:, hp:hp + 1], bs[:, hp:hp + 1],
                                ALU.mult, ALU.add).then_inc(apd, 1)
                v.wait_ge(apd, hp + 1)
    return nc


def build_comb0(nc=None, io=None):
    if nc is None:
        nc = bass.Bass("TRN2", target_bir_lowering=False, debug=False)
    DT = _mk_dt(nc, io)
    xd = DT("x", [2, CHH, D], F32, "ExternalInput")
    s0d = DT("s0", [D, T], BF16, "ExternalInput")
    g0d = DT("g0", [D, T], BF16, "ExternalInput")
    wkvd = DT("wkv", [D, 128], F32, "ExternalInput")
    wqd = DT("wq", [D, D], F32, "ExternalInput")
    x1d = DT("x1", [T, D], F32, "ExternalOutput")
    xn1d = DT("xn1", [T, D], BF16, "ExternalOutput")
    latd = DT("lat", [T, 128], BF16, "ExternalOutput")
    qfmd = DT("qfm", [D, T], BF16, "ExternalOutput")

    with contextlib.ExitStack() as ctx:
        E = ctx.enter_context
        E(nc.allow_low_precision(reason="bf16 pipeline; tolerance measured for bf16 matmuls"))
        xt = E(nc.sbuf_tensor("xt", [128, 2 * D], F32))
        sgt = E(nc.sbuf_tensor("sgt", [128, 2 * 2 * D], BF16))
        x1t = E(nc.sbuf_tensor("x1t", [128, 2 * D], F32))
        ot = E(nc.sbuf_tensor("ot", [128, 2 * D], BF16))
        scr = E(nc.sbuf_tensor("scr", [128, D], F32))
        scr2 = E(nc.sbuf_tensor("scr2", [128, D], F32))
        mc = E(nc.sbuf_tensor("mc", [128, 4], F32))
        rc = E(nc.sbuf_tensor("rc", [128, 4], F32))
        xnf = E(nc.sbuf_tensor("xnf", [128, 8 * T], BF16))
        wkvs = E(nc.sbuf_tensor("wkvs", [128, 8 * 128], BF16))
        wqs = E(nc.sbuf_tensor("wqs", [128, 8 * D], BF16))
        obl = E(nc.sbuf_tensor("obl", [128, 4 * 128], BF16))
        ob = E(nc.sbuf_tensor("ob", [128, 4 * 512], BF16))
        ps = E(nc.psum_tensor("cbps", [128, 8 * 512], F32))
        ldx = E(nc.semaphore("ldx"))
        ldxo = E(nc.semaphore("ldxo"))
        ldsg = E(nc.semaphore("ldsg"))
        ldsgo = E(nc.semaphore("ldsgo"))
        ldw = E(nc.semaphore("ldw"))
        lxf = E(nc.semaphore("lxf"))
        xav = E(nc.semaphore("xav"))
        vs = E(nc.semaphore("vs"))
        sv = E(nc.semaphore("sv"))
        mml = E(nc.semaphore("mml"))
        evl = E(nc.semaphore("evl"))
        vch = E(nc.semaphore("vch"))
        st = E(nc.semaphore("st"))
        sto = E(nc.semaphore("sto"))
        stq = E(nc.semaphore("stq"))
        block = E(nc.Block())
        vch_n = [0]
        NTT = 16

        @block.gpsimd
        def _(g):
            for kt in range(8):
                g.dma_start(out=wkvs[:, kt * 128:(kt + 1) * 128],
                            in_=wkvd[kt * 128:(kt + 1) * 128, :]).then_inc(ldw, 16)
                g.dma_start(out=wqs[:, kt * D:(kt + 1) * D],
                            in_=wqd[kt * 128:(kt + 1) * 128, :]).then_inc(ldw, 16)

        @block.sync
        def _(sync):
            def tile_loads(j):
                ch, tt = divmod(j, 8)
                sem = ldx if j % 2 == 0 else ldxo
                semg = ldsg if j % 2 == 0 else ldsgo
                sync.dma_start(out=xt[:, (j % 2) * D:(j % 2 + 1) * D],
                               in_=xd[ch, HALO + tt * 128:HALO + (tt + 1) * 128, :]).then_inc(sem, 16)
                sync.dma_start_transpose(sgt[:, (j % 2) * 2 * D:(j % 2) * 2 * D + D],
                                         s0d[:, j * 128:(j + 1) * 128]).then_inc(semg, 16)
                sync.dma_start_transpose(sgt[:, (j % 2) * 2 * D + D:(j % 2 + 1) * 2 * D],
                                         g0d[:, j * 128:(j + 1) * 128]).then_inc(semg, 16)
            for i in range(min(2, NTT)):
                tile_loads(i)
            for i in range(NTT):
                buf = i % 2
                sync.wait_ge(vs, 2 * i + 2)
                ssem = st if i % 2 == 0 else sto
                sync.dma_start(out=x1d[i * 128:(i + 1) * 128, :], in_=x1t[:, buf * D:(buf + 1) * D]).then_inc(ssem, 16)
                sync.dma_start(out=xn1d[i * 128:(i + 1) * 128, :], in_=ot[:, buf * D:(buf + 1) * D]).then_inc(ssem, 16)
                if i + 2 < NTT:
                    tile_loads(i + 2)
            sync.wait_ge(st, 16 * NTT)
            sync.wait_ge(sto, 16 * NTT)
            for kt in range(8):
                sync.dma_start_transpose(xnf[:, kt * T:(kt + 1) * T],
                                         xn1d[:, kt * 128:(kt + 1) * 128]).then_inc(lxf, 16)
            for tb in range(16):
                sync.wait_ge(evl, tb + 1)
                sync.dma_start(out=latd[tb * 128:(tb + 1) * 128, :],
                               in_=obl[:, (tb % 4) * 128:(tb % 4 + 1) * 128]).then_inc(stq, 16)
            for g2 in range(32):
                mt, tc = divmod(g2, 4)
                sync.wait_ge(evl, 16 + g2 + 1)
                sync.dma_start(out=qfmd[mt * 128:(mt + 1) * 128, tc * 512:(tc + 1) * 512],
                               in_=ob[:, (g2 % 4) * 512:(g2 % 4 + 1) * 512]).then_inc(stq, 16)
            sync.wait_ge(stq, 16 * 48)
            sync.wait_ge(ldw, 16 * 16)

        @block.vector
        def _(v):
            def chv(inst):
                vch_n[0] += 1
                inst.then_inc(vch, 1)
                v.wait_ge(vch, vch_n[0])
            for i in range(NTT):
                buf = i % 2
                v.wait_ge(ldx if i % 2 == 0 else ldxo, 16 * (i // 2 + 1))
                v.wait_ge(ldsg if i % 2 == 0 else ldsgo, 32 * (i // 2 + 1))
                xsrc = xt[:, buf * D:(buf + 1) * D]
                x1dst = x1t[:, buf * D:(buf + 1) * D]
                s_ = sgt[:, buf * 2 * D:buf * 2 * D + D]
                g_ = sgt[:, buf * 2 * D + D:(buf + 1) * 2 * D]
                chv(v.tensor_tensor(scr[:], s_, g_, ALU.mult))
                if i >= 2:
                    v.wait_ge(st if i % 2 == 0 else sto, 32 * (i // 2))
                v.tensor_tensor(x1dst, xsrc, scr[:], ALU.add).then_inc(xav, 1)
                v.wait_ge(xav, i + 1)
                chv(v.reduce_sum(mc[:, 0:1], x1dst, axis=AX.X))
                chv(v.tensor_scalar(mc[:, 1:2], mc[:, 0:1], 1.0 / D, None, ALU.mult))
                v.wait_ge(sv, 2 * i + 1)
                chv(v.tensor_scalar(mc[:, 3:4], mc[:, 2:3], 1.0 / D, None, ALU.mult))
                chv(v.tensor_tensor(rc[:, 0:1], mc[:, 1:2], mc[:, 1:2], ALU.mult))
                chv(v.tensor_tensor(rc[:, 1:2], mc[:, 3:4], rc[:, 0:1], ALU.subtract))
                chv(v.tensor_scalar(rc[:, 1:2], rc[:, 1:2], 1e-5, None, ALU.add))
                v.sem_inc(vs, 1)
                v.wait_ge(sv, 2 * i + 2)
                chv(v.reciprocal(rc[:, 3:4], rc[:, 2:3]))
                chv(v.tensor_tensor(mc[:, 0:1], mc[:, 1:2], rc[:, 3:4], ALU.mult))
                chv(v.tensor_scalar(mc[:, 0:1], mc[:, 0:1], -1.0, None, ALU.mult))
                v.tensor_scalar(ot[:, buf * D:(buf + 1) * D], x1dst, rc[:, 3:4],
                                mc[:, 0:1], ALU.mult, ALU.add).then_inc(vs, 1)

        @block.scalar
        def _(s):
            for i in range(NTT):
                buf = i % 2
                s.wait_ge(xav, i + 1)
                s.activation(out=scr2[:], in_=x1t[:, buf * D:(buf + 1) * D], func=AF.Square,
                             accum_out=mc[:, 2:3]).then_inc(sv, 1)
                s.wait_ge(vs, 2 * i + 1)
                s.sqrt(rc[:, 2:3], rc[:, 1:2]).then_inc(sv, 1)
                s.wait_ge(sv, 2 * i + 2)
            for tb in range(16):
                s.wait_ge(mml, tb + 1)
                if tb >= 4:
                    s.wait_ge(stq, 16 * tb)
                s.copy(obl[:, (tb % 4) * 128:(tb % 4 + 1) * 128],
                       ps[:, (tb % 4) * 512:(tb % 4) * 512 + 128]).then_inc(evl, 1)
            for g2 in range(32):
                s.wait_ge(mml, 16 + g2 + 1)
                if g2 >= 4:
                    s.wait_ge(stq, 16 * (16 + g2))
                s.copy(ob[:, (g2 % 4) * 512:(g2 % 4 + 1) * 512],
                       ps[:, (4 + g2 % 4) * 512:(5 + g2 % 4) * 512]).then_inc(evl, 1)

        @block.tensor
        def _(t):
            t.wait_ge(lxf, 16 * 8)
            t.wait_ge(ldw, 16 * 16)
            for tb in range(16):
                bank = tb % 4
                if tb >= 4:
                    t.wait_ge(evl, tb - 3)
                last = None
                for kt in range(8):
                    last = t.matmul(ps[:, bank * 512:bank * 512 + 128],
                                    xnf[:, kt * T + tb * 128:kt * T + (tb + 1) * 128],
                                    wkvs[:, kt * 128:(kt + 1) * 128],
                                    start=(kt == 0), stop=(kt == 7))
                last.then_inc(mml, 1)
            for g2 in range(32):
                mt, tc = divmod(g2, 4)
                bank = 4 + g2 % 4
                if g2 >= 4:
                    t.wait_ge(evl, 16 + g2 - 3)
                last = None
                for kt in range(8):
                    last = t.matmul(ps[:, bank * 512:(bank + 1) * 512],
                                    wqs[:, kt * D + mt * 128:kt * D + (mt + 1) * 128],
                                    xnf[:, kt * T + tc * 512:kt * T + (tc + 1) * 512],
                                    start=(kt == 0), stop=(kt == 7))
                last.then_inc(mml, 1)
    return nc


def build_l1ffn_a(nc=None, io=None):
    if nc is None:
        nc = bass.Bass("TRN2", target_bir_lowering=False, debug=False)
    DT = _mk_dt(nc, io)
    xnd = DT("xn1", [T, D], BF16, "ExternalInput")
    w1d = DT("w1t", [D, 4096], F32, "ExternalInput")
    w2d = DT("w2t", [D, 4096], F32, "ExternalInput")
    s1d = DT("s1c", [128, 32], F32, "ExternalInput")
    s2d = DT("s2c", [128, 32], F32, "ExternalInput")
    t1d = DT("t1", [4096, T], BF16, "ExternalOutput")

    with contextlib.ExitStack() as ctx:
        E = ctx.enter_context
        E(nc.allow_low_precision(reason="bf16 pipeline; tolerance measured for bf16 matmuls"))
        xnf = E(nc.sbuf_tensor("xnf", [128, 8 * T], BF16))
        w1s = E(nc.sbuf_tensor("w1s", [128, 8 * 4096], BF16))
        w2s = E(nc.sbuf_tensor("w2s", [128, 8 * 4096], BF16))
        s1s = E(nc.sbuf_tensor("s1s", [128, 32], F32))
        s2s = E(nc.sbuf_tensor("s2s", [128, 32], F32))
        sg = E(nc.sbuf_tensor("sg", [128, 8 * 512], BF16))
        oc = E(nc.sbuf_tensor("oc", [128, 4 * 512], BF16))
        ps = E(nc.psum_tensor("faps", [128, 8 * 512], F32))
        lx = E(nc.semaphore("lx"))
        ldw = E(nc.semaphore("ldw"))
        ldc = E(nc.semaphore("ldc"))
        mmg = E(nc.semaphore("mmg"))
        mmu = E(nc.semaphore("mmu"))
        evg = E(nc.semaphore("evg"))
        vt = E(nc.semaphore("vt"))
        st4 = [E(nc.semaphore(f"st{i}")) for i in range(4)]
        block = E(nc.Block())

        @block.gpsimd
        def _(g):
            for kt in range(8):
                g.dma_start(out=w1s[:, kt * 4096:(kt + 1) * 4096],
                            in_=w1d[kt * 128:(kt + 1) * 128, :]).then_inc(ldw, 16)
                g.dma_start(out=w2s[:, kt * 4096:(kt + 1) * 4096],
                            in_=w2d[kt * 128:(kt + 1) * 128, :]).then_inc(ldw, 16)

        @block.sync
        def _(sync):
            for kt in range(8):
                sync.dma_start_transpose(xnf[:, kt * T:(kt + 1) * T],
                                         xnd[:, kt * 128:(kt + 1) * 128]).then_inc(lx, 16)
            sync.dma_start(out=s1s[:], in_=s1d).then_inc(ldc, 16)
            sync.dma_start(out=s2s[:], in_=s2d).then_inc(ldc, 16)
            for g2 in range(128):
                m, tc = divmod(g2, 4)
                sync.wait_ge(vt, g2 + 1)
                sync.dma_start(out=t1d[m * 128:(m + 1) * 128, tc * 512:(tc + 1) * 512],
                               in_=oc[:, (g2 % 4) * 512:(g2 % 4 + 1) * 512]).then_inc(st4[g2 % 4], 16)
            for i in range(4):
                sync.wait_ge(st4[i], 16 * 32)

        @block.tensor
        def _(t):
            t.wait_ge(lx, 16 * 8)
            t.wait_ge(ldw, 16 * 16)
            t.wait_ge(ldc, 16 * 2)
            for m in range(32):
                if m >= 1:
                    t.wait_ge(evg, 4 * m)
                for kt in range(8):
                    for tc in range(4):
                        last = t.matmul(ps[:, tc * 512:(tc + 1) * 512],
                                        w1s[:, kt * 4096 + m * 128:kt * 4096 + (m + 1) * 128],
                                        xnf[:, kt * T + tc * 512:kt * T + (tc + 1) * 512],
                                        start=(kt == 0), stop=(kt == 7))
                        if kt == 7:
                            last.then_inc(mmg, 1)
                if m >= 1:
                    t.wait_ge(vt, 4 * m)
                for kt in range(8):
                    for tc in range(4):
                        last = t.matmul(ps[:, (4 + tc) * 512:(4 + tc + 1) * 512],
                                        w2s[:, kt * 4096 + m * 128:kt * 4096 + (m + 1) * 128],
                                        xnf[:, kt * T + tc * 512:kt * T + (tc + 1) * 512],
                                        start=(kt == 0), stop=(kt == 7))
                        if kt == 7:
                            last.then_inc(mmu, 1)

        @block.scalar
        def _(s):
            s.wait_ge(ldc, 16 * 2)
            for g2 in range(128):
                m, tc = divmod(g2, 4)
                s.wait_ge(mmg, g2 + 1)
                if m >= 2:
                    s.wait_ge(vt, 4 * (m - 1))
                s.activation(out=sg[:, ((m % 2) * 4 + tc) * 512:((m % 2) * 4 + tc + 1) * 512],
                             in_=ps[:, tc * 512:(tc + 1) * 512],
                             func=AF.Silu, scale=s1s[:, m:m + 1]).then_inc(evg, 1)

        @block.vector
        def _(v):
            for g2 in range(128):
                m, tc = divmod(g2, 4)
                v.wait_ge(mmu, g2 + 1)
                v.wait_ge(evg, g2 + 1)
                if g2 >= 4:
                    v.wait_ge(st4[g2 % 4], 16 * (g2 // 4))
                v.scalar_tensor_tensor(oc[:, (g2 % 4) * 512:(g2 % 4 + 1) * 512],
                                       ps[:, (4 + tc) * 512:(4 + tc + 1) * 512], s2s[:, m:m + 1],
                                       sg[:, ((m % 2) * 4 + tc) * 512:((m % 2) * 4 + tc + 1) * 512],
                                       ALU.mult, ALU.mult).then_inc(vt, 1)
                v.wait_ge(vt, g2 + 1)
    return nc


def build_l1ffn_h(nc=None, io=None):
    if nc is None:
        nc = bass.Bass("TRN2", target_bir_lowering=False, debug=False)
    DT = _mk_dt(nc, io)
    tind = DT("tin", [4096, T], BF16, "ExternalInput")
    wd = DT("wt", [4096, 4096], F32, "ExternalInput")
    scd = DT("sc", [128, 32], F32, "ExternalInput")
    tod = DT("tout", [4096, T], BF16, "ExternalOutput")

    KT, MT, MC, NMC = 32, 32, 512, 8
    with contextlib.ExitStack() as ctx:
        E = ctx.enter_context
        E(nc.allow_low_precision(reason="bf16 pipeline; tolerance measured for bf16 matmuls"))
        tins = E(nc.sbuf_tensor("tins", [128, KT * T], BF16))
        ws = E(nc.sbuf_tensor("ws", [128, 2 * KT * MC], BF16))
        scs = E(nc.sbuf_tensor("scs", [128, 32], F32))
        oc = E(nc.sbuf_tensor("oc", [128, 4 * 512], BF16))
        ps = E(nc.psum_tensor("fhps", [128, 8 * 512], F32))
        lx = E(nc.semaphore("lx"))
        ldwe = E(nc.semaphore("ldwe"))
        ldwo = E(nc.semaphore("ldwo"))
        ldc = E(nc.semaphore("ldc"))
        wfree = E(nc.semaphore("wfree"))
        mm = E(nc.semaphore("mm"))
        ev = E(nc.semaphore("ev"))
        st4 = [E(nc.semaphore(f"st{i}")) for i in range(4)]
        block = E(nc.Block())

        @block.sync
        def _(sync):
            for kt in range(KT):
                sync.dma_start(out=tins[:, kt * T:(kt + 1) * T],
                               in_=tind[kt * 128:(kt + 1) * 128, :]).then_inc(lx, 16)
            sync.dma_start(out=scs[:], in_=scd).then_inc(ldc, 16)
            for g2 in range(128):
                mt, tc = divmod(g2, 4)
                sync.wait_ge(ev, g2 + 1)
                sync.dma_start(out=tod[mt * 128:(mt + 1) * 128, tc * 512:(tc + 1) * 512],
                               in_=oc[:, (g2 % 4) * 512:(g2 % 4 + 1) * 512]).then_inc(st4[g2 % 4], 16)
            for i in range(4):
                sync.wait_ge(st4[i], 16 * 32)
            sync.wait_ge(ldwe, 16 * KT * NMC // 2)
            sync.wait_ge(ldwo, 16 * KT * NMC // 2)

        @block.gpsimd
        def _(g):
            for c in range(NMC):
                buf = c % 2
                if c >= 2:
                    g.wait_ge(ev, 16 * (c - 1))
                for kt in range(KT):
                    g.dma_start(out=ws[:, (buf * KT + kt) * MC:(buf * KT + kt + 1) * MC],
                                in_=wd[kt * 128:(kt + 1) * 128, c * MC:(c + 1) * MC]).then_inc(ldwe if c % 2 == 0 else ldwo, 16)

        @block.tensor
        def _(t):
            t.wait_ge(lx, 16 * KT)
            t.wait_ge(ldc, 16)
            for c in range(NMC):
                buf = c % 2
                t.wait_ge(ldwe if c % 2 == 0 else ldwo, 16 * KT * (c // 2 + 1))
                for j in range(4):
                    mt = c * 4 + j
                    grp = (mt % 2) * 4
                    if mt >= 2:
                        t.wait_ge(ev, 4 * (mt - 1))
                    for kt in range(KT):
                        for tc in range(4):
                            last = t.matmul(ps[:, (grp + tc) * 512:(grp + tc + 1) * 512],
                                            ws[:, (buf * KT + kt) * MC + j * 128:(buf * KT + kt) * MC + (j + 1) * 128],
                                            tins[:, kt * T + tc * 512:kt * T + (tc + 1) * 512],
                                            start=(kt == 0), stop=(kt == KT - 1))
                            if kt == KT - 1:
                                last.then_inc(mm, 1)

        @block.scalar
        def _(s):
            s.wait_ge(ldc, 16)
            for g2 in range(128):
                mt, tc = divmod(g2, 4)
                bank = (mt % 2) * 4 + tc
                s.wait_ge(mm, g2 + 1)
                if g2 >= 4:
                    s.wait_ge(st4[g2 % 4], 16 * (g2 // 4))
                s.activation(out=oc[:, (g2 % 4) * 512:(g2 % 4 + 1) * 512],
                             in_=ps[:, bank * 512:(bank + 1) * 512],
                             func=AF.Gelu, scale=scs[:, mt:mt + 1]).then_inc(ev, 1)
    return nc


def build_l1ffn_c(nc=None, io=None):
    if nc is None:
        nc = bass.Bass("TRN2", target_bir_lowering=False, debug=False)
    DT = _mk_dt(nc, io)
    t3d = DT("t3", [4096, T], BF16, "ExternalInput")
    w3d = DT("w3t", [4096, D], F32, "ExternalInput")
    s3d = DT("s3c", [128, 8], F32, "ExternalInput")
    x1d = DT("x1", [T, D], F32, "ExternalInput")
    s1d = DT("s1f", [D, T], BF16, "ExternalInput")
    yd = DT("y", [T, D], F32, "ExternalOutput")
    dg1 = DT("dg1", [D, T], BF16, "Internal")

    KT = 32
    with contextlib.ExitStack() as ctx:
        E = ctx.enter_context
        E(nc.allow_low_precision(reason="bf16 pipeline; tolerance measured for bf16 matmuls"))
        t3s = E(nc.sbuf_tensor("t3s", [128, KT * T], BF16))
        w3s = E(nc.sbuf_tensor("w3s", [128, KT * 512], BF16))
        s3s = E(nc.sbuf_tensor("s3s", [128, 8], F32))
        oc = E(nc.sbuf_tensor("oc", [128, 4 * 512], BF16))
        x1t = E(nc.sbuf_tensor("x1t", [128, 2 * D], F32))
        sgt = E(nc.sbuf_tensor("sgt", [128, 2 * 2 * D], BF16))
        yt = E(nc.sbuf_tensor("yt", [128, 2 * D], F32))
        scr = E(nc.sbuf_tensor("scr", [128, D], F32))
        ps = E(nc.psum_tensor("fcps", [128, 8 * 512], F32))
        lx = E(nc.semaphore("lx"))
        ldw = E(nc.semaphore("ldw"))
        ldc = E(nc.semaphore("ldc"))
        mm = E(nc.semaphore("mm"))
        ev = E(nc.semaphore("ev"))
        stg4 = [E(nc.semaphore(f"stg{i}")) for i in range(4)]
        lxe = E(nc.semaphore("lxe"))
        lxo = E(nc.semaphore("lxo"))
        vt = E(nc.semaphore("vt"))
        vch = E(nc.semaphore("vch"))
        st = E(nc.semaphore("st"))
        sto = E(nc.semaphore("sto"))
        block = E(nc.Block())
        vch_n = [0]

        @block.gpsimd
        def _(g):
            for half in range(2):
                if half == 1:
                    g.wait_ge(mm, 16)
                for kt in range(KT):
                    g.dma_start(out=w3s[:, kt * 512:(kt + 1) * 512],
                                in_=w3d[kt * 128:(kt + 1) * 128, half * 512:(half + 1) * 512]).then_inc(ldw, 16)

        @block.sync
        def _(sync):
            for kt in range(KT):
                sync.dma_start(out=t3s[:, kt * T:(kt + 1) * T],
                               in_=t3d[kt * 128:(kt + 1) * 128, :]).then_inc(lx, 16)
            sync.dma_start(out=s3s[:], in_=s3d).then_inc(ldc, 16)
            for g2 in range(32):
                mt, tc = divmod(g2, 4)
                sync.wait_ge(ev, g2 + 1)
                sync.dma_start(out=dg1[mt * 128:(mt + 1) * 128, tc * 512:(tc + 1) * 512],
                               in_=oc[:, (g2 % 4) * 512:(g2 % 4 + 1) * 512]).then_inc(stg4[g2 % 4], 16)
            for i in range(4):
                sync.wait_ge(stg4[i], 16 * 8)
            def tile_loads(j):
                sem = lxe if j % 2 == 0 else lxo
                sync.dma_start(out=x1t[:, (j % 2) * D:(j % 2 + 1) * D],
                               in_=x1d[j * 128:(j + 1) * 128, :]).then_inc(sem, 16)
                sync.dma_start_transpose(sgt[:, (j % 2) * 2 * D:(j % 2) * 2 * D + D],
                                         s1d[:, j * 128:(j + 1) * 128]).then_inc(sem, 16)
                sync.dma_start_transpose(sgt[:, (j % 2) * 2 * D + D:(j % 2 + 1) * 2 * D],
                                         dg1[:, j * 128:(j + 1) * 128]).then_inc(sem, 16)
            for j in range(2):
                tile_loads(j)
            for i in range(16):
                buf = i % 2
                sync.wait_ge(vt, i + 1)
                sync.dma_start(out=yd[i * 128:(i + 1) * 128, :],
                               in_=yt[:, buf * D:(buf + 1) * D]).then_inc(st if i % 2 == 0 else sto, 16)
                if i + 2 < 16:
                    tile_loads(i + 2)
            sync.wait_ge(st, 16 * 8)
            sync.wait_ge(sto, 16 * 8)
            sync.wait_ge(ldw, 16 * KT * 2)

        @block.tensor
        def _(t):
            t.wait_ge(lx, 16 * KT)
            t.wait_ge(ldc, 16)
            for g2 in range(32):
                mt, tc = divmod(g2, 4)
                half, mloc = divmod(mt, 4)
                t.wait_ge(ldw, 16 * KT * (half + 1))
                bank = g2 % 4
                if g2 >= 4:
                    t.wait_ge(ev, g2 - 3)
                last = None
                for kt in range(KT):
                    last = t.matmul(ps[:, bank * 512:(bank + 1) * 512],
                                    w3s[:, kt * 512 + mloc * 128:kt * 512 + (mloc + 1) * 128],
                                    t3s[:, kt * T + tc * 512:kt * T + (tc + 1) * 512],
                                    start=(kt == 0), stop=(kt == KT - 1))
                last.then_inc(mm, 1)

        @block.scalar
        def _(s):
            s.wait_ge(ldc, 16)
            for g2 in range(32):
                mt, tc = divmod(g2, 4)
                bank = g2 % 4
                s.wait_ge(mm, g2 + 1)
                if g2 >= 4:
                    s.wait_ge(stg4[g2 % 4], 16 * (g2 // 4))
                s.activation(out=oc[:, (g2 % 4) * 512:(g2 % 4 + 1) * 512],
                             in_=ps[:, bank * 512:(bank + 1) * 512],
                             func=AF.Silu, scale=s3s[:, mt:mt + 1]).then_inc(ev, 1)

        @block.vector
        def _(v):
            def chv(inst):
                vch_n[0] += 1
                inst.then_inc(vch, 1)
                v.wait_ge(vch, vch_n[0])
            for i in range(16):
                buf = i % 2
                v.wait_ge(lxe if i % 2 == 0 else lxo, 48 * (i // 2 + 1))
                s_ = sgt[:, buf * 2 * D:buf * 2 * D + D]
                g_ = sgt[:, buf * 2 * D + D:(buf + 1) * 2 * D]
                chv(v.tensor_tensor(scr[:], s_, g_, ALU.mult))
                if i >= 2:
                    v.wait_ge(st if i % 2 == 0 else sto, 16 * (i // 2))
                v.tensor_tensor(yt[:, buf * D:(buf + 1) * D], x1t[:, buf * D:(buf + 1) * D],
                                scr[:], ALU.add).then_inc(vt, 1)
                v.wait_ge(vt, i + 1)
    return nc


def host_pm(w, ktiles):
    K_, M_ = w.shape
    assert K_ == ktiles * 128
    return np.ascontiguousarray(w.reshape(ktiles, 128, M_).transpose(1, 0, 2).reshape(128, ktiles * M_)).astype(np.float32)


def host_sel8():
    m = np.zeros((8, 8 * 128), np.float32)
    for e in range(8):
        m[e, e * 128:(e + 1) * 128] = 1.0
    return m.astype(BF)


def build_moe(nc=None, io=None):
    if nc is None:
        nc = bass.Bass("TRN2", target_bir_lowering=False, debug=False)
    DT = _mk_dt(nc, io)
    xnd = DT("xn0", [2, CHH, D], BF16, "ExternalInput")
    gwd = DT("gw", [8, D], F32, "ExternalInput")
    cbd = DT("cbr", [128, 8], F32, "ExternalInput")
    idd = DT("identd", [128, 128], F32, "ExternalInput")
    sel8d = DT("sel8", [8, 1024], BF16, "ExternalInput")
    wgp = DT("wgp", [9, 128, 8 * 512], F32, "ExternalInput")
    wup = DT("wup", [9, 128, 8 * 512], F32, "ExternalInput")
    wdp = DT("wdp", [9, 128, 4 * 1024], F32, "ExternalInput")
    g0d = DT("g0", [D, T], BF16, "ExternalOutput")
    dte = DT("dte", [9, 512, T], BF16, "Internal")

    with contextlib.ExitStack() as ctx:
        E = ctx.enter_context
        E(nc.allow_low_precision(reason="bf16 pipeline; tolerance measured for bf16 matmuls"))
        xnf = E(nc.sbuf_tensor("xnf", [128, 8 * T], BF16))
        wgs = E(nc.sbuf_tensor("wgs", [128, 2 * 4096], BF16))
        wus = E(nc.sbuf_tensor("wus", [128, 2 * 4096], BF16))
        wds = E(nc.sbuf_tensor("wds", [128, 2 * 4096], BF16))
        teb = E(nc.sbuf_tensor("teb", [128, 9 * 4 * 512], BF16))
        tsc = E(nc.sbuf_tensor("tsc", [128, 2 * 4 * 512], BF16))
        sgb = E(nc.sbuf_tensor("sgb", [128, 2 * 512], BF16))
        teA = E(nc.sbuf_tensor("teA", [128, 2 * 512], BF16))
        gws = E(nc.sbuf_tensor("gws", [8, D], F32))
        gwt = E(nc.sbuf_tensor("gwt", [128, 8 * 8], BF16))
        ids = E(nc.sbuf_tensor("ids", [128, 128], F32))
        sel8s = E(nc.sbuf_tensor("sel8s", [8, 1024], BF16))
        cbb = E(nc.sbuf_tensor("cbb", [128, 8], F32))
        lgf = E(nc.sbuf_tensor("lgf", [8, T], F32))
        sgm = E(nc.sbuf_tensor("sgm", [128, 128], F32))
        scp = E(nc.sbuf_tensor("scp", [128, 128], F32))
        gsm = E(nc.sbuf_tensor("gsm", [128, 64], F32))
        gtm = E(nc.sbuf_tensor("gtm", [128, 64], F32))
        t2b = E(nc.sbuf_tensor("t2b", [128, 16], F32))
        t2c = E(nc.sbuf_tensor("t2c", [128, 32], F32))
        gmk = E(nc.sbuf_tensor("gmk", [128, 64], F32))
        mek = E(nc.sbuf_tensor("mek", [128, 128], F32))
        vbuf = E(nc.sbuf_tensor("vbuf", [128, 128], F32))
        selp = E(nc.sbuf_tensor("selp", [128, 128], F32))
        cntb = E(nc.sbuf_tensor("cntb", [128, 16], F32))
        dbuf = E(nc.sbuf_tensor("dbuf", [128, 16], F32))
        cmb = E(nc.sbuf_tensor("cmb", [128, 128], F32))
        rpb = E(nc.sbuf_tensor("rpb", [128, 128], F32))
        flb = E(nc.sbuf_tensor("flb", [128, 128], F32))
        wbv = E(nc.sbuf_tensor("wbv", [128, 128], F32))
        denb = E(nc.sbuf_tensor("denb", [128, 16], F32))
        cwt = E(nc.sbuf_tensor("cwt", [128, 128], F32))
        cwf = E(nc.sbuf_tensor("cwf", [8, T], BF16))
        cws = E(nc.sbuf_tensor("cws", [128, 32 * 512], BF16))
        g0t = E(nc.sbuf_tensor("g0t", [128, 4 * 512], BF16))
        ps = E(nc.psum_tensor("moeps", [128, 8 * 512], F32))
        lxn = E(nc.semaphore("lxn"))
        lc = E(nc.semaphore("lc"))
        lwe = E(nc.semaphore("lwe"))
        lwo = E(nc.semaphore("lwo"))
        lwd = [E(nc.semaphore(f"lwd{i}")) for i in range(2)]
        ltb = E(nc.semaphore("ltb"))
        mmA = E(nc.semaphore("mmA"))
        evg = E(nc.semaphore("evg"))
        vte = E(nc.semaphore("vte"))
        s4 = [E(nc.semaphore(f"sp{i}")) for i in range(2)]
        mmR = E(nc.semaphore("mmR"))
        esg = E(nc.semaphore("esg"))
        vrt = E(nc.semaphore("vrt"))
        ecw = E(nc.semaphore("ecw"))
        mmB = E(nc.semaphore("mmB"))
        mmBe = E(nc.semaphore("mmBe"))
        vsc = E(nc.semaphore("vsc"))
        evB = E(nc.semaphore("evB"))
        vch = E(nc.semaphore("vch"))
        st = E(nc.semaphore("st"))
        st4m = [E(nc.semaphore(f"stm{i}")) for i in range(4)]
        block = E(nc.Block())
        vch_n = [0]

        def chv_of(v):
            def chv(inst):
                vch_n[0] += 1
                inst.then_inc(vch, 1)
                v.wait_ge(vch, vch_n[0])
            return chv

        @block.gpsimd
        def _(g):
            for e in range(9):
                sem = lwe if e % 2 == 0 else lwo
                if e >= 2:
                    g.wait_ge(mmA, 32 * (e - 1))
                g.dma_start(out=wgs[:, (e % 2) * 4096:(e % 2 + 1) * 4096], in_=wgp[e]).then_inc(sem, 16)
                g.dma_start(out=wus[:, (e % 2) * 4096:(e % 2 + 1) * 4096], in_=wup[e]).then_inc(sem, 16)
            for tc in range(4):
                for e in range(9):
                    le = tc * 9 + e
                    if le >= 2:
                        g.wait_ge(mmBe, le - 1)
                    g.dma_start(out=wds[:, (le % 2) * 4096:(le % 2 + 1) * 4096],
                                in_=wdp[e]).then_inc(lwd[le % 2], 16)

        @block.sync
        def _(sync):
            for ch in range(2):
                for ct in range(8):
                    sync.dma_start_transpose(xnf[:, ct * T + ch * 1024:ct * T + (ch + 1) * 1024],
                                             xnd[ch, HALO:HALO + 1024, ct * 128:(ct + 1) * 128]).then_inc(lxn, 16)
            sync.dma_start(out=gws[:], in_=gwd).then_inc(lc, 16)
            sync.dma_start(out=cbb[:], in_=cbd).then_inc(lc, 16)
            sync.dma_start(out=ids[:], in_=idd).then_inc(lc, 16)
            sync.dma_start(out=sel8s[:], in_=sel8d).then_inc(lc, 16)
            pc = 0
            for e in range(9):
                for it in range(4):
                    for tc in range(4):
                        sync.wait_ge(vte, pc + 1)
                        sync.dma_start(out=dte[e, it * 128:(it + 1) * 128, tc * 512:(tc + 1) * 512],
                                       in_=teA[:, (pc % 2) * 512:(pc % 2 + 1) * 512]).then_inc(s4[pc % 2], 16)
                        pc += 1
            def pb_loads(tc):
                if tc == 0:
                    for i in range(2):
                        sync.wait_ge(s4[i], 16 * 72)
                else:
                    sync.wait_ge(mmBe, 9 * tc)
                for e in range(9):
                    for kt in range(4):
                        sync.dma_start(out=teb[:, (e * 4 + kt) * 512:(e * 4 + kt + 1) * 512],
                                       in_=dte[e, kt * 128:(kt + 1) * 128, tc * 512:(tc + 1) * 512]).then_inc(ltb, 16)
            def pb_stores(tc):
                for m in range(8):
                    sync.wait_ge(evB, tc * 8 + m + 1)
                    sync.dma_start(out=g0d[m * 128:(m + 1) * 128, tc * 512:(tc + 1) * 512],
                                   in_=g0t[:, (m % 4) * 512:(m % 4 + 1) * 512]).then_inc(st4m[m % 4], 16)
            pb_loads(0)
            pb_loads(1)
            pb_stores(0)
            pb_loads(2)
            pb_stores(1)
            pb_loads(3)
            pb_stores(2)
            pb_stores(3)
            for i in range(4):
                sync.wait_ge(st4m[i], 16 * 8)
            sync.wait_ge(lwd[0], 16 * 18)
            sync.wait_ge(lwd[1], 16 * 18)

        @block.tensor
        def _(t):
            t.wait_ge(lxn, 16 * 16)
            t.wait_ge(lc, 16 * 4)
            for kt in range(8):
                if kt >= 1:
                    t.wait_ge(esg, kt)
                t.transpose(ps[:, 7 * 512:7 * 512 + 8], gws[0:8, kt * 128:(kt + 1) * 128], ids[0:8, 0:8]).then_inc(mmR, 1)
            t.wait_ge(esg, 8)
            for tc in range(4):
                if tc >= 1:
                    t.wait_ge(esg, 8 + tc)
                last = None
                for kt in range(8):
                    last = t.matmul(ps[0:8, 4 * 512:4 * 512 + 512], gwt[:, kt * 8:(kt + 1) * 8],
                                    xnf[:, kt * T + tc * 512:kt * T + (tc + 1) * 512],
                                    start=(kt == 0), stop=(kt == 7))
                last.then_inc(mmR, 1)
            t.wait_ge(esg, 12)
            for tt in range(16):
                if tt >= 1:
                    t.wait_ge(esg, 12 + tt)
                t.transpose(ps[:, 7 * 512:7 * 512 + 8], lgf[0:8, tt * 128:(tt + 1) * 128], ids[0:8, 0:8]).then_inc(mmR, 1)
            ga = 0
            for e in range(9):
                t.wait_ge(lwe if e % 2 == 0 else lwo, 32 * (e // 2 + 1))
                for it in range(4):
                    ei = e * 4 + it
                    if ei >= 1:
                        t.wait_ge(evg, 4 * ei)
                    for kt in range(8):
                        for tc in range(4):
                            last = t.matmul(ps[:, tc * 512:(tc + 1) * 512],
                                            wgs[:, (e % 2) * 4096 + kt * 512 + it * 128:(e % 2) * 4096 + kt * 512 + (it + 1) * 128],
                                            xnf[:, kt * T + tc * 512:kt * T + (tc + 1) * 512],
                                            start=(kt == 0), stop=(kt == 7))
                            if kt == 7:
                                last.then_inc(mmA, 1)
                    if ei == 0:
                        t.wait_ge(esg, 28)
                    else:
                        t.wait_ge(vte, 4 * ei)
                    for kt in range(8):
                        for tc in range(4):
                            last = t.matmul(ps[:, (4 + tc) * 512:(4 + tc + 1) * 512],
                                            wus[:, (e % 2) * 4096 + kt * 512 + it * 128:(e % 2) * 4096 + kt * 512 + (it + 1) * 128],
                                            xnf[:, kt * T + tc * 512:kt * T + (tc + 1) * 512],
                                            start=(kt == 0), stop=(kt == 7))
                            if kt == 7:
                                last.then_inc(mmA, 1)
            t.wait_ge(vrt, 1)
            t.wait_ge(vte, 144)
            for tt in range(16):
                if tt >= 1:
                    t.wait_ge(ecw, tt)
                t.transpose(ps[0:8, 7 * 512:7 * 512 + 128], cwt[:, tt * 8:(tt + 1) * 8], ids[:]).then_inc(mmR, 1)
            t.wait_ge(ecw, 16)
            for e in range(8):
                for tc in range(4):
                    i2 = e * 4 + tc
                    if i2 >= 1:
                        t.wait_ge(ecw, 16 + i2)
                    t.matmul(ps[:, 7 * 512:8 * 512], sel8s[0:8, e * 128:(e + 1) * 128],
                             cwf[0:8, tc * 512:(tc + 1) * 512], start=True, stop=True).then_inc(mmR, 1)
            t.wait_ge(ecw, 48)
            for tc in range(4):
                t.wait_ge(ltb, 16 * 36 * (tc + 1))
                if tc >= 1:
                    t.wait_ge(evB, 8 * tc)
                for e in range(9):
                    le = tc * 9 + e
                    t.wait_ge(lwd[le % 2], 16 * (le // 2 + 1))
                    if e < 8:
                        t.wait_ge(vsc, tc * 8 + e + 1)
                        rhsbuf = tsc
                        rbase = (e % 2) * 4 * 512
                    else:
                        rhsbuf = teb
                        rbase = 8 * 4 * 512
                    last = None
                    for m in range(8):
                        for kt in range(4):
                            last = t.matmul(ps[:, m * 512:(m + 1) * 512],
                                            wds[:, (le % 2) * 4096 + kt * 1024 + m * 128:(le % 2) * 4096 + kt * 1024 + (m + 1) * 128],
                                            rhsbuf[:, rbase + kt * 512:rbase + (kt + 1) * 512],
                                            start=(e == 0 and kt == 0), stop=(e == 8 and kt == 3))
                    last.then_inc(mmBe, 1)

        @block.scalar
        def _(s):
            s.wait_ge(lc, 16 * 4)
            for kt in range(8):
                s.wait_ge(mmR, kt + 1)
                s.copy(gwt[:, kt * 8:(kt + 1) * 8], ps[:, 7 * 512:7 * 512 + 8]).then_inc(esg, 1)
            for tc in range(4):
                s.wait_ge(mmR, 8 + tc + 1)
                s.copy(lgf[0:8, tc * 512:(tc + 1) * 512], ps[0:8, 4 * 512:4 * 512 + 512]).then_inc(esg, 1)
            for tt in range(16):
                s.wait_ge(mmR, 12 + tt + 1)
                s.activation(out=sgm[:, tt * 8:(tt + 1) * 8], in_=ps[:, 7 * 512:7 * 512 + 8],
                             func=AF.Sigmoid).then_inc(esg, 1)
            ga = 0
            for e in range(9):
                for it in range(4):
                    ei = e * 4 + it
                    for tc in range(4):
                        s.wait_ge(mmA, ei * 8 + tc + 1)
                        if ga >= 2:
                            s.wait_ge(vte, ga - 1)
                        s.activation(out=sgb[:, (ga % 2) * 512:(ga % 2 + 1) * 512],
                                     in_=ps[:, tc * 512:(tc + 1) * 512], func=AF.Silu).then_inc(evg, 1)
                        ga += 1
            for tt in range(16):
                s.wait_ge(mmR, 28 + tt + 1)
                s.copy(cwf[0:8, tt * 128:(tt + 1) * 128], ps[0:8, 7 * 512:7 * 512 + 128]).then_inc(ecw, 1)
            for i2 in range(32):
                s.wait_ge(mmR, 44 + i2 + 1)
                s.copy(cws[:, i2 * 512:(i2 + 1) * 512], ps[:, 7 * 512:8 * 512]).then_inc(ecw, 1)
            for tc in range(4):
                for m in range(8):
                    s.wait_ge(mmBe, tc * 9 + 9)
                    nprev = tc * 2 + m // 4
                    if nprev >= 1:
                        s.wait_ge(st4m[m % 4], 16 * nprev)
                    s.activation(out=g0t[:, (m % 4) * 512:(m % 4 + 1) * 512],
                                 in_=ps[:, m * 512:(m + 1) * 512], func=AF.Silu).then_inc(evB, 1)

        @block.vector
        def _(v):
            chv = chv_of(v)
            v.wait_ge(lc, 16 * 4)
            pass
            ga = 0
            for e in range(9):
                for it in range(4):
                    ei = e * 4 + it
                    for tc in range(4):
                        v.wait_ge(mmA, ei * 8 + 4 + tc + 1)
                        v.wait_ge(evg, ga + 1)
                        if ga >= 2:
                            v.wait_ge(s4[ga % 2], 16 * (ga // 2))
                        v.tensor_tensor(teA[:, (ga % 2) * 512:(ga % 2 + 1) * 512],
                                        sgb[:, (ga % 2) * 512:(ga % 2 + 1) * 512],
                                        ps[:, (4 + tc) * 512:(4 + tc + 1) * 512], ALU.mult).then_inc(vte, 1)
                        v.wait_ge(vte, ga + 1)
                        ga += 1
            v.wait_ge(esg, 28)
            sc3i0 = scp[:].rearrange("p (t e) -> p t e", e=8)
            sg3 = sgm[:].rearrange("p (t e) -> p t e", e=8)
            g3 = gsm[:].rearrange("p (t j) -> p t j", j=4)
            gt3 = gtm[:].rearrange("p (t j) -> p t j", j=4)
            gm3 = gmk[:].rearrange("p (t j) -> p t j", j=4)
            me3 = mek[:].rearrange("p (t e) -> p t e", e=8)
            v3 = vbuf[:].rearrange("p (t e) -> p t e", e=8)
            sp3 = selp[:].rearrange("p (t e) -> p t e", e=8)
            cm3 = cmb[:].rearrange("p (t e) -> p t e", e=8)
            r3 = rpb[:].rearrange("p (t e) -> p t e", e=8)
            fl3 = flb[:].rearrange("p (t e) -> p t e", e=8)
            w3 = wbv[:].rearrange("p (t e) -> p t e", e=8)
            cw3 = cwt[:].rearrange("p (t e) -> p t e", e=8)
            for tt in range(16):
                chv(v.tensor_tensor(scp[:, tt * 8:(tt + 1) * 8], sgm[:, tt * 8:(tt + 1) * 8], cbb[:], ALU.add))
            chv(v.tensor_tensor(g3[:, :, :], sc3i0[:, :, 0:8:2], sc3i0[:, :, 1:8:2], ALU.add))
            chv(v.tensor_tensor(gt3[:, :, 0:4:2], g3[:, :, 0:4:2], g3[:, :, 1:4:2], ALU.max))
            chv(v.tensor_tensor(gt3[:, :, 1:4:2], g3[:, :, 0:4:2], g3[:, :, 1:4:2], ALU.min))
            chv(v.tensor_tensor(t2c[:, 0:16], gt3[:, :, 0], gt3[:, :, 2], ALU.min))
            chv(v.tensor_tensor(t2c[:, 16:32], gt3[:, :, 1], gt3[:, :, 3], ALU.max))
            chv(v.tensor_tensor(t2b[:], t2c[:, 0:16], t2c[:, 16:32], ALU.max))
            for j in range(4):
                chv(v.tensor_tensor(gm3[:, :, j], g3[:, :, j], t2b[:], ALU.is_ge))
            for e in range(8):
                chv(v.tensor_copy(me3[:, :, e], gm3[:, :, e // 2]))
            chv(v.tensor_tensor(vbuf[:], scp[:], mek[:], ALU.mult))
            chv(v.tensor_scalar(selp[:], vbuf[:], 0.0, None, ALU.is_gt))
            chv(v.reduce_sum(cntb[:], sp3[:, :, :], axis=AX.X))
            chv(v.tensor_scalar(dbuf[:], cntb[:], -1.0, 4.0, ALU.mult, ALU.add))
            chv(v.tensor_scalar(cmb[:], mek[:], -1.0, 1.0, ALU.mult, ALU.add))
            chv(v.tensor_copy(r3[:, :, 0], cm3[:, :, 0]))
            for e in range(1, 8):
                chv(v.tensor_tensor(r3[:, :, e], r3[:, :, e - 1], cm3[:, :, e], ALU.add))
            for e in range(8):
                chv(v.tensor_tensor(fl3[:, :, e], r3[:, :, e], dbuf[:], ALU.is_le))
            chv(v.tensor_tensor(flb[:], flb[:], cmb[:], ALU.mult))
            chv(v.tensor_tensor(selp[:], selp[:], flb[:], ALU.add))
            chv(v.tensor_tensor(wbv[:], sgm[:], selp[:], ALU.mult))
            chv(v.reduce_sum(denb[:], w3[:, :, :], axis=AX.X))
            chv(v.tensor_scalar(denb[:], denb[:], 1e-20, None, ALU.add))
            chv(v.reciprocal(denb[:], denb[:]))
            for e in range(8):
                chv(v.tensor_tensor(cw3[:, :, e], w3[:, :, e], denb[:], ALU.mult))
            v.tensor_scalar(cwt[:], cwt[:], 2.5, None, ALU.mult).then_inc(vrt, 1)
            for tc in range(4):
                for e in range(8):
                    i2 = tc * 8 + e
                    v.wait_ge(ltb, 16 * 36 * (tc + 1))
                    v.wait_ge(ecw, 16 + e * 4 + tc + 1)
                    if i2 >= 2:
                        v.wait_ge(mmBe, 9 * tc + e - 1)
                    last = None
                    for kt in range(4):
                        last = v.tensor_tensor(tsc[:, (e % 2) * 2048 + kt * 512:(e % 2) * 2048 + (kt + 1) * 512],
                                               teb[:, (e * 4 + kt) * 512:(e * 4 + kt + 1) * 512],
                                               cws[:, (e * 4 + tc) * 512:(e * 4 + tc + 1) * 512], ALU.mult)
                        vch_n[0] += 1
                        last.then_inc(vch, 1)
                        v.wait_ge(vch, vch_n[0])
                    v.sem_inc(vsc, 1)
    return nc


def host_latp(lat_full_b, c):
    perm = host_kperm(c)
    lp = lat_full_b[perm]
    lk = lp
    lv = np.concatenate([lp[:, 64:128], lp[:, 0:64]], 1)
    return np.ascontiguousarray(lk), np.ascontiguousarray(lv)


def _l1_slots(qc, j):
    if qc == 0:
        full = list(range(16, 16 + 8 * j))
    elif qc == 1:
        full = list(range(16, 16 + 8 * j)) + [0, 1, 2, 3]
    elif qc == 2:
        full = list(range(16, 16 + 8 * (6 - j))) + list(range(0, 8))
    else:
        full = list(range(16, 16 + 8 * (6 - j))) + list(range(0, 12))
    diag = [4 * qc + i for i in range(4)]
    return full + diag, len(full)


def build_l1attn(nc=None, io=None):
    if nc is None:
        nc = bass.Bass("TRN2", target_bir_lowering=False, debug=False)
    DT = _mk_dt(nc, io)
    latkd = DT("latk", [S, 128], BF16, "ExternalInput")
    latvd = DT("latv", [S, 128], BF16, "ExternalInput")
    qd = DT("qfm", [D, T], BF16, "ExternalInput")
    wkud = DT("wku", [32, 128], F32, "ExternalInput")
    wvud = DT("wvu", [32, 128], F32, "ExternalInput")
    wod = DT("wo", [D, D], F32, "ExternalInput")
    btd = DT("biastab", [128, 256], F32, "ExternalInput")
    trid = DT("tri", [4, 128, 512], BF16, "ExternalInput")
    jd = DT("jidx", [1, 1], mybir.dt.int32, "ExternalInput")
    s1d = DT("s1", [D, T], BF16, "ExternalOutput")

    SCALE = 1.0 / float(np.sqrt(128.0))
    with contextlib.ExitStack() as ctx:
        E = ctx.enter_context
        E(nc.allow_low_precision(reason="bf16 pipeline; tolerance measured for bf16 matmuls"))
        latk = E(nc.sbuf_tensor("latks", [128, S], BF16))
        latv = E(nc.sbuf_tensor("latvs", [128, S], BF16))
        ksb = E(nc.sbuf_tensor("ksb", [128, 2 * S], BF16))
        vsb = E(nc.sbuf_tensor("vsb", [128, 2 * S], BF16))
        qsb = E(nc.sbuf_tensor("qsb", [128, 8 * T], BF16))
        ysb = E(nc.sbuf_tensor("ysb", [128, 8 * T], BF16))
        wku = E(nc.sbuf_tensor("wkus", [64, 128], BF16))
        wvu = E(nc.sbuf_tensor("wvus", [64, 128], BF16))
        wos = E(nc.sbuf_tensor("wos", [128, 8 * D], BF16))
        bts = E(nc.sbuf_tensor("bts", [128, 256], F32))
        btf = E(nc.sbuf_tensor("btf", [128, 256], F32))
        tris = E(nc.sbuf_tensor("tris", [128, 4 * 512], BF16))
        onesb = E(nc.sbuf_tensor("onesb", [128, 1], BF16))
        onesrf = E(nc.sbuf_tensor("onesrf", [1, 128], F32))
        nacc = E(nc.sbuf_tensor("nacc", [1, 8], F32))
        ccol = E(nc.sbuf_tensor("ccol", [128, 1], F32))
        pbuf = E(nc.sbuf_tensor("pbuf", [128, 2 * 512], BF16))
        rws = E(nc.sbuf_tensor("rws", [1, 512], F32))
        rbc = E(nc.sbuf_tensor("rbc", [128, 512], BF16))
        js = E(nc.sbuf_tensor("js", [1, 1], mybir.dt.int32))
        oc = E(nc.sbuf_tensor("oc", [128, 4 * 512], BF16))
        ps = E(nc.psum_tensor("atps", [128, 8 * 512], F32))
        ini = E(nc.semaphore("ini"))
        ldj = E(nc.semaphore("ldj"))
        llat = E(nc.semaphore("llat"))
        lq = E(nc.semaphore("lq"))
        lw = E(nc.semaphore("lw"))
        lcst = E(nc.semaphore("lcst"))
        mmU = E(nc.semaphore("mmU"))
        evK = E(nc.semaphore("evK"))
        evV = E(nc.semaphore("evV"))
        sqd = E(nc.semaphore("sqd"))
        mmN = E(nc.semaphore("mmN"))
        vnm = E(nc.semaphore("vnm"))
        scd = E(nc.semaphore("scd"))
        vbf = E(nc.semaphore("vbf"))
        mmS = E(nc.semaphore("mmS"))
        evS = E(nc.semaphore("evS"))
        msk = E(nc.semaphore("msk"))
        mmP = E(nc.semaphore("mmP"))
        mmD = E(nc.semaphore("mmD"))
        vrc = E(nc.semaphore("vrc"))
        mmBC = E(nc.semaphore("mmBC"))
        evBC = E(nc.semaphore("evBC"))
        vy = E(nc.semaphore("vy"))
        mmO = E(nc.semaphore("mmO"))
        evO = E(nc.semaphore("evO"))
        st4 = [E(nc.semaphore(f"sa{i}")) for i in range(4)]
        vch = E(nc.semaphore("vch"))
        block = E(nc.Block())
        vch_n = [0]

        NQSQ = 8 * 4 + 2 * 16

        @block.gpsimd
        def _(g):
            g.memset(onesb[:], 1.0)
            g.memset(onesrf[:], 1.0).then_inc(ini, 1)
            for kt in range(8):
                g.dma_start(out=wos[:, kt * D:(kt + 1) * D],
                            in_=wod[kt * 128:(kt + 1) * 128, :]).then_inc(lw, 16)
            g.dma_start(out=wku[0:32, :], in_=wkud).then_inc(lw, 16)
            g.dma_start(out=wku[32:64, :], in_=wkud).then_inc(lw, 16)
            g.dma_start(out=wvu[0:32, :], in_=wvud).then_inc(lw, 16)
            g.dma_start(out=wvu[32:64, :], in_=wvud).then_inc(lw, 16)

        @block.sync
        def _(sync):
            sync.dma_start_transpose(latk[:], latkd).then_inc(llat, 16)
            sync.dma_start_transpose(latv[:], latvd).then_inc(llat, 16)
            for kt in range(8):
                sync.dma_start(out=qsb[:, kt * T:(kt + 1) * T], in_=qd[kt * 128:(kt + 1) * 128, :]).then_inc(lq, 16)
            sync.dma_start(out=bts[:], in_=btd).then_inc(lcst, 16)
            sync.dma_start(out=js[0:1, :], in_=jd).then_inc(ldj, 16)
            for i in range(4):
                sync.dma_start(out=tris[:, i * 512:(i + 1) * 512], in_=trid[i]).then_inc(lcst, 16)
            for g2 in range(32):
                mt, tc = divmod(g2, 4)
                sync.wait_ge(evO, g2 + 1)
                sync.dma_start(out=s1d[mt * 128:(mt + 1) * 128, tc * 512:(tc + 1) * 512],
                               in_=oc[:, (g2 % 4) * 512:(g2 % 4 + 1) * 512]).then_inc(st4[g2 % 4], 16)
            for i in range(4):
                sync.wait_ge(st4[i], 16 * 8)
            sync.wait_ge(lw, 16 * 12)

        @block.tensor
        def _(t):
            t.wait_ge(ini, 1)
            t.wait_ge(llat, 16 * 2)
            t.wait_ge(lw, 16 * 12)
            gk = 0
            for kvh in range(2):
                for cchunk in range(16):
                    bank = 2 + gk % 2
                    if gk >= 2:
                        t.wait_ge(evK, gk - 1)
                    t.matmul(ps[:, bank * 512:(bank + 1) * 512], wku[kvh * 32:(kvh + 1) * 32, :],
                             latk[kvh * 32:(kvh + 1) * 32, cchunk * 512:(cchunk + 1) * 512],
                             start=True, stop=True).then_inc(mmU, 1)
                    gk += 1
            for g2 in range(32):
                kvh, qd4 = divmod(g2, 16)
                bank = 2 + (gk % 2)
                if g2 >= 2:
                    t.wait_ge(evV, g2 - 1)
                else:
                    t.wait_ge(evK, 31 + g2)
                last = None
                for j in range(4):
                    tb = qd4 * 4 + j
                    last = t.matmul(ps[:, bank * 512 + j * 128:bank * 512 + (j + 1) * 128],
                                    latv[kvh * 32:(kvh + 1) * 32, tb * 128:(tb + 1) * 128],
                                    wvu[kvh * 32:(kvh + 1) * 32, :], start=True, stop=True)
                last.then_inc(mmU, 1)
                gk += 1
            t.wait_ge(lq, 16 * 8)
            nn = 0
            for i in range(10):
                t.wait_ge(sqd, i + 1)
                nch = 4 if i < 8 else 16
                for cchunk in range(nch):
                    if nn >= 1:
                        t.wait_ge(vnm, nn)
                    t.matmul(ps[0:1, 4 * 512:4 * 512 + 512], onesb[:, 0:1],
                             ysb[:, cchunk * 512:(cchunk + 1) * 512],
                             start=True, stop=True).then_inc(mmN, 1)
                    nn += 1
            t.wait_ge(scd, 3)
            t.matmul(ps[:, 5 * 512:5 * 512 + 1], onesrf[0:1, :], nacc[0:1, 0:1],
                     start=True, stop=True).then_inc(mmN, 1)
            t.wait_ge(vbf, 1)
            t.wait_ge(ldj, 16)
            treg = t.alloc_register()
            t.reg_load(treg, js[0:1, 0:1])
            trv = t.snap(treg, min_val=0, max_val=3)
            for j in t.Switch(trv, n=4):
                it = 0
                ss = 0
                for kvh in range(2):
                    for rep in range(4):
                        qt = kvh * 4 + rep
                        for qc in range(4):
                            LS, nfull = _l1_slots(qc, j)
                            N = len(LS)
                            if it >= 1:
                                t.wait_ge(vy, it)
                            ssb = ss
                            for idx in range(N + 1):
                                if idx < N:
                                    slot = LS[idx]
                                    g = ssb + idx
                                    sbank = 6 + g % 2
                                    if g >= 2:
                                        t.wait_ge(evS, g - 1)
                                    t.matmul(ps[:, sbank * 512:(sbank + 1) * 512],
                                             ksb[:, kvh * S + slot * 128:kvh * S + (slot + 1) * 128],
                                             qsb[:, qt * T + qc * 512:qt * T + (qc + 1) * 512],
                                             start=True, stop=True).then_inc(mmS, 1)
                                if idx >= 1:
                                    k = idx - 1
                                    s_ = LS[k]
                                    gs_ = ssb + k
                                    if k >= nfull:
                                        t.wait_ge(msk, it * 4 + (k - nfull) + 1)
                                    else:
                                        t.wait_ge(evS, gs_ + 1)
                                    t.matmul(ps[:, 0:512], vsb[:, kvh * S + s_ * 128:kvh * S + (s_ + 1) * 128],
                                             pbuf[:, (gs_ % 2) * 512:(gs_ % 2 + 1) * 512],
                                             start=(k == 0), stop=(k == N - 1)).then_inc(mmP, 1)
                                    last = t.matmul(ps[0:1, 1 * 512:1 * 512 + 512], onesb[:, 0:1],
                                                    pbuf[:, (gs_ % 2) * 512:(gs_ % 2 + 1) * 512],
                                                    start=(k == 0), stop=(k == N - 1))
                                    if k == N - 1:
                                        last.then_inc(mmD, 1)
                            ss = ssb + N
                            t.wait_ge(vrc, it + 1)
                            if it == 0:
                                t.wait_ge(vbf, 1)
                            else:
                                t.wait_ge(vy, it)
                            t.matmul(ps[:, 5 * 512:5 * 512 + 512], onesrf[0:1, :], rws[0:1, :],
                                     start=True, stop=True).then_inc(mmBC, 1)
                            it += 1
            t.wait_ge(vy, 32)
            for g2 in range(32):
                mt, tc = divmod(g2, 4)
                bank = g2 % 4
                if g2 >= 4:
                    t.wait_ge(evO, g2 - 3)
                last = None
                for kt in range(8):
                    last = t.matmul(ps[:, bank * 512:(bank + 1) * 512],
                                    wos[:, kt * D + mt * 128:kt * D + (mt + 1) * 128],
                                    ysb[:, kt * T + tc * 512:kt * T + (tc + 1) * 512],
                                    start=(kt == 0), stop=(kt == 7))
                last.then_inc(mmO, 1)

        @block.scalar
        def _(s):
            s.wait_ge(lcst, 16 * 5)
            gk = 0
            for kvh in range(2):
                for cchunk in range(16):
                    bank = 2 + gk % 2
                    s.wait_ge(mmU, gk + 1)
                    s.copy(ksb[:, kvh * S + cchunk * 512:kvh * S + (cchunk + 1) * 512],
                           ps[:, bank * 512:(bank + 1) * 512]).then_inc(evK, 1)
                    gk += 1
            for g2 in range(32):
                kvh, qd4 = divmod(g2, 16)
                bank = 2 + (gk % 2)
                s.wait_ge(mmU, gk + 1)
                s.copy(vsb[:, kvh * S + qd4 * 512:kvh * S + (qd4 + 1) * 512],
                       ps[:, bank * 512:(bank + 1) * 512]).then_inc(evV, 1)
                gk += 1
            s.wait_ge(lq, 16 * 8)
            for i in range(10):
                if i >= 1:
                    prev = 4 * i if i <= 8 else 48
                    s.wait_ge(vnm, (4 * i) if i <= 8 else 48)
                if i < 8:
                    s.activation(out=ysb[:, 0:T], in_=qsb[:, i * T:(i + 1) * T],
                                 func=AF.Square).then_inc(sqd, 1)
                else:
                    kvh = i - 8
                    s.activation(out=ysb[:, 0:S], in_=ksb[:, kvh * S:(kvh + 1) * S],
                                 func=AF.Square).then_inc(sqd, 1)
            s.wait_ge(scd, 1)
            s.sqrt(nacc[0:1, 3:4], nacc[0:1, 2:3]).then_inc(scd, 1)
            s.wait_ge(mmN, NQSQ + 1)
            s.copy(ccol[:], ps[:, 5 * 512:5 * 512 + 1]).then_inc(vbf, 1)
            s.wait_ge(vbf, 2)
            s.wait_ge(ldj, 16)
            sreg = s.alloc_register()
            s.reg_load(sreg, js[0:1, 0:1])
            srv = s.snap(sreg, min_val=0, max_val=3)
            for j in s.Switch(srv, n=4):
                ss = 0
                for it in range(32):
                    qc = it % 4
                    LS, nfull = _l1_slots(qc, j)
                    for slot in LS:
                        s.wait_ge(mmS, ss + 1)
                        if ss >= 2:
                            s.wait_ge(mmP, ss - 1)
                        sbank = 6 + ss % 2
                        s.activation(out=pbuf[:, (ss % 2) * 512:(ss % 2 + 1) * 512],
                                     in_=ps[:, sbank * 512:(sbank + 1) * 512], func=AF.Exp,
                                     scale=SCALE, bias=btf[:, qc * 64 + slot:qc * 64 + slot + 1]).then_inc(evS, 1)
                        ss += 1
                    s.wait_ge(mmBC, it + 1)
                    s.copy(rbc[:], ps[:, 5 * 512:6 * 512]).then_inc(evBC, 1)
            for g2 in range(32):
                bank = g2 % 4
                s.wait_ge(mmO, g2 + 1)
                if g2 >= 4:
                    s.wait_ge(st4[g2 % 4], 16 * (g2 // 4))
                s.copy(oc[:, (g2 % 4) * 512:(g2 % 4 + 1) * 512],
                       ps[:, bank * 512:(bank + 1) * 512]).then_inc(evO, 1)

        @block.vector
        def _(v):
            def chv(inst):
                vch_n[0] += 1
                inst.then_inc(vch, 1)
                v.wait_ge(vch, vch_n[0])
            v.wait_ge(lcst, 16 * 5)
            nn = 0
            for i in range(10):
                nch = 4 if i < 8 else 16
                for cchunk in range(nch):
                    v.wait_ge(mmN, nn + 1)
                    chv(v.reduce_max(nacc[0:1, 4:5], ps[0:1, 4 * 512:4 * 512 + 512], axis=AX.X))
                    dst = nacc[0:1, 0:1] if i < 8 else nacc[0:1, 1:2]
                    if nn == 0 or (i == 8 and cchunk == 0):
                        chv(v.tensor_copy(dst, nacc[0:1, 4:5]))
                    else:
                        chv(v.tensor_tensor(dst, dst, nacc[0:1, 4:5], ALU.max))
                    v.sem_inc(vnm, 1)
                    nn += 1
            chv(v.tensor_tensor(nacc[0:1, 2:3], nacc[0:1, 0:1], nacc[0:1, 1:2], ALU.mult))
            v.sem_inc(scd, 1)
            v.wait_ge(scd, 2)
            chv(v.tensor_scalar(nacc[0:1, 0:1], nacc[0:1, 3:4], SCALE, None, ALU.mult))
            v.sem_inc(scd, 1)
            v.wait_ge(vbf, 1)
            v.tensor_scalar(btf[:], bts[:], ccol[:, 0:1], None, ALU.subtract).then_inc(vbf, 1)
            v.wait_ge(ldj, 16)
            vreg = v.alloc_register()
            v.reg_load(vreg, js[0:1, 0:1])
            vrv = v.snap(vreg, min_val=0, max_val=3)
            for j in v.Switch(vrv, n=4):
                ss = 0
                for it in range(32):
                    qt = it // 4
                    qc = it % 4
                    LS, nfull = _l1_slots(qc, j)
                    for i in range(4):
                        gs_ = ss + nfull + i
                        v.wait_ge(evS, gs_ + 1)
                        v.tensor_tensor(pbuf[:, (gs_ % 2) * 512:(gs_ % 2 + 1) * 512],
                                        pbuf[:, (gs_ % 2) * 512:(gs_ % 2 + 1) * 512],
                                        tris[:, i * 512:(i + 1) * 512], ALU.mult).then_inc(msk, 1)
                        v.wait_ge(msk, it * 4 + i + 1)
                    ss += len(LS)
                    v.wait_ge(mmD, it + 1)
                    v.reciprocal(rws[0:1, :], ps[0:1, 1 * 512:1 * 512 + 512]).then_inc(vrc, 1)
                    v.wait_ge(vrc, it + 1)
                    v.wait_ge(evBC, it + 1)
                    v.tensor_tensor(ysb[:, qt * T + qc * 512:qt * T + (qc + 1) * 512],
                                    ps[:, 0:512], rbc[:], ALU.mult).then_inc(vy, 1)
                    v.wait_ge(vy, it + 1)
    return nc



RG4 = [[0, 1, 2, 3], [4, 5, 6, 7]]
STAGE_MARKS = []


def emit_ag(nc, cin, cout):
    with contextlib.ExitStack() as c:
        cs = c.enter_context(nc.semaphore("ccs"))
        blk = c.enter_context(nc.Block())

        @blk.gpsimd
        def _(g):
            g.collective_compute("AllGather", ALU.bypass, replica_groups=RG4,
                                 ins=[cin.opt()], outs=[cout.opt()]).then_inc(cs)
            g.wait_ge(cs, 1)


def emit_latx(nc, lat, latg, latk, latv, csrc):
    with contextlib.ExitStack() as c:
        ob = c.enter_context(nc.sbuf_tensor("obx", [1, 8], mybir.dt.int32))
        cs = c.enter_context(nc.semaphore("ccs"))
        lo = c.enter_context(nc.semaphore("lox"))
        sp = c.enter_context(nc.semaphore("spx"))
        blk = c.enter_context(nc.Block())

        @blk.gpsimd
        def _(g):
            g.collective_compute("AllGather", ALU.bypass, replica_groups=RG4,
                                 ins=[lat.opt()], outs=[latg.opt()]).then_inc(cs)
            g.wait_ge(cs, 1)
            g.dma_start(out=ob[0:1, :], in_=csrc).then_inc(lo, 16)
            g.wait_ge(lo, 16)
            for j in range(8):
                reg = g.alloc_register()
                g.reg_load(reg, ob[0:1, j:j + 1])
                r = g.snap(reg, min_val=0, max_val=S - 1024)
                g.dma_start(out=latk[j * 1024:(j + 1) * 1024, :],
                            in_=latg[bass.ds(r, 1024), :]).then_inc(sp, 16)
            g.wait_ge(sp, 16 * 8)
            g.dma_start(out=latv[:, 0:64], in_=latk[:, 64:128]).then_inc(sp, 16)
            g.dma_start(out=latv[:, 64:128], in_=latk[:, 0:64]).then_inc(sp, 16)
            g.wait_ge(sp, 16 * 10)


def build_fused():
    nc = bass.Bass("TRN2", target_bir_lowering=False, debug=False)
    ext = {}

    def EI(name, shape, dtype):
        ext[name] = nc.dram_tensor(name, shape, dtype, kind="ExternalInput").ap()
        return ext[name]

    EI("x", [2, CHH, D], F32)
    EI("w1d", [128, 160 * 128], F32); EI("w2d", [128, 160 * 128], F32)
    EI("b1g", [D, 4], F32); EI("b2", [128, 8], F32); EI("g0w", [128, 8], F32); EI("b0w", [128, 8], F32)
    for i in range(4):
        EI(f"wq{i}", [D, D], F32); EI(f"wk{i}", [D, D], F32); EI(f"wv{i}", [D, D], F32)
        EI(f"bq{i}", [128, 8], F32); EI(f"bk{i}", [128, 8], F32); EI(f"bvr{i}", [1, D], F32)
        EI(f"lnag{i}", [128, 8], F32); EI(f"lnab{i}", [128, 8], F32)
    EI("cosd", [D, T], BF16); EI("sind", [D, T], BF16)
    EI("swm", [128, 128], BF16); EI("sel2", [2, 128], BF16)
    EI("fw1", [D, 4096], F32); EI("fb1c", [128, 32], F32); EI("fw2", [4096, D], F32)
    EI("fb2c", [128, 8], F32); EI("flng", [128, 8], F32); EI("flnb", [128, 8], F32)
    EI("gw", [8, D], F32); EI("cbr", [128, 8], F32); EI("identd", [128, 128], F32)
    EI("sel8", [8, 1024], BF16)
    EI("wgp", [9, 128, 8 * 512], F32); EI("wup", [9, 128, 8 * 512], F32); EI("wdp", [9, 128, 4 * 1024], F32)
    EI("wkv", [D, 128], F32); EI("wqL1", [D, D], F32)
    EI("wku", [32, 128], F32); EI("wvu", [32, 128], F32); EI("wo", [D, D], F32)
    EI("biastab", [128, 256], F32); EI("tri", [4, 128, 512], BF16); EI("csrc", [1, 8], mybir.dt.int32)
    EI("jidx", [1, 1], mybir.dt.int32)
    EI("fa_w1t", [D, 4096], F32); EI("fa_w2t", [D, 4096], F32)
    EI("fa_s1c", [128, 32], F32); EI("fa_s2c", [128, 32], F32)
    EI("h1t", [4096, 4096], F32); EI("h2t", [4096, 4096], F32)
    EI("h1s", [128, 32], F32); EI("h2s", [128, 32], F32)
    EI("w3t", [4096, D], F32); EI("s3c", [128, 8], F32)
    y = nc.dram_tensor("y", [T, D], F32, kind="ExternalOutput").ap()

    INT = lambda n, s, d: nc.dram_tensor(n, s, d, kind="Internal").ap()
    xn0 = INT("i_xn0", [2, CHH, D], BF16)
    hA = INT("i_hA", [D, T], BF16)
    hB = INT("i_hB", [D, T], BF16)
    kvp = INT("i_kvp", [128, 1032], F32)
    kvp4g = INT("i_kvp4", [512, 1032], F32)
    qi = INT("i_q", [D, T], BF16)
    qri = INT("i_qr", [D, T], BF16)
    s0i = INT("i_s0", [D, T], BF16)
    g0i = INT("i_g0", [D, T], BF16)
    x1i = INT("i_x1", [T, D], F32)
    xn1i = INT("i_xn1", [T, D], BF16)
    lati = INT("i_lat", [T, 128], BF16)
    latg = INT("i_latg", [S, 128], BF16)
    latki = INT("i_latk", [S, 128], BF16)
    latvi = INT("i_latv", [S, 128], BF16)
    s1i = INT("i_s1", [D, T], BF16)
    t1i = INT("i_t1", [4096, T], BF16)
    t2i = INT("i_t2", [4096, T], BF16)
    t3i = INT("i_t3", [4096, T], BF16)

    bases = dict(sb=nc.sbuf_base, st=nc.sbuf_top, pb=nc.psum_base, pt=nc.psum_top,
                 ld=nc.local_dram_base, sd=nc.shared_dram_base)

    stage_n = [0]
    orig_sbuf, orig_psum, orig_sem = nc.sbuf_tensor, nc.psum_tensor, nc.semaphore

    def _mangle(args):
        if args and isinstance(args[0], str):
            return (f"{args[0]}_f{stage_n[0]}",) + tuple(args[1:])
        return args

    nc.sbuf_tensor = lambda *a, **k: orig_sbuf(*_mangle(a), **k)
    nc.psum_tensor = lambda *a, **k: orig_psum(*_mangle(a), **k)
    nc.semaphore = lambda *a, **k: orig_sem(*_mangle(a), **k)

    def R():
        STAGE_MARKS.append((stage_n[0], nc.next_id()))
        stage_n[0] += 1
        nc.reset(previous_sbuf_base=bases["sb"], previous_sbuf_top=bases["st"],
                 previous_psum_base=bases["pb"], previous_psum_top=bases["pt"],
                 previous_local_dram_base=nc.local_dram_base,
                 previous_shared_dram_base=nc.shared_dram_base)

    build_prep(nc, {"x": ext["x"], "xn": xn0}); R()
    build_conv(nc, {"xn": xn0, "w1d": ext["w1d"], "w2d": ext["w2d"], "b1g": ext["b1g"],
                    "b2": ext["b2"], "g0w": ext["g0w"], "b0w": ext["b0w"], "h0": hA}); R()
    h_cur, h_nxt = hA, hB
    for i in range(4):
        build_la_a(nc, {"h": h_cur, "wq": ext[f"wq{i}"], "wk": ext[f"wk{i}"], "wv": ext[f"wv{i}"],
                        "bq": ext[f"bq{i}"], "bk": ext[f"bk{i}"], "bvr": ext[f"bvr{i}"],
                        "cosd": ext["cosd"], "sind": ext["sind"], "swm": ext["swm"],
                        "kvp": kvp, "q": qi, "qr": qri}); R()
        emit_ag(nc, kvp, kvp4g); R()
        build_la_b(nc, {"kvp4": kvp4g.rearrange("(r p) n -> r p n", p=128),
                        "q": qi, "qr": qri, "h": h_cur, "lnag": ext[f"lnag{i}"],
                        "lnab": ext[f"lnab{i}"], "sel2": ext["sel2"], "hn": h_nxt}); R()
        h_cur, h_nxt = h_nxt, h_cur
    build_ffn0(nc, {"h": h_cur, "w1": ext["fw1"], "b1c": ext["fb1c"], "w2": ext["fw2"],
                    "b2c": ext["fb2c"], "lng": ext["flng"], "lnb": ext["flnb"], "s0": s0i}); R()
    build_moe(nc, {"xn0": xn0, "gw": ext["gw"], "cbr": ext["cbr"], "identd": ext["identd"],
                   "sel8": ext["sel8"], "wgp": ext["wgp"], "wup": ext["wup"], "wdp": ext["wdp"],
                   "g0": g0i}); R()
    build_comb0(nc, {"x": ext["x"], "s0": s0i, "g0": g0i, "wkv": ext["wkv"], "wq": ext["wqL1"],
                     "x1": x1i, "xn1": xn1i, "lat": lati, "qfm": qi}); R()
    emit_latx(nc, lati, latg, latki, latvi, ext["csrc"]); R()
    build_l1attn(nc, {"latk": latki, "latv": latvi, "qfm": qi, "wku": ext["wku"],
                      "wvu": ext["wvu"], "wo": ext["wo"], "biastab": ext["biastab"],
                      "tri": ext["tri"], "jidx": ext["jidx"], "s1": s1i}); R()
    build_l1ffn_a(nc, {"xn1": xn1i, "w1t": ext["fa_w1t"], "w2t": ext["fa_w2t"],
                       "s1c": ext["fa_s1c"], "s2c": ext["fa_s2c"], "t1": t1i}); R()
    build_l1ffn_h(nc, {"tin": t1i, "wt": ext["h1t"], "sc": ext["h1s"], "tout": t2i}); R()
    build_l1ffn_h(nc, {"tin": t2i, "wt": ext["h2t"], "sc": ext["h2s"], "tout": t3i}); R()
    build_l1ffn_c(nc, {"t3": t3i, "w3t": ext["w3t"], "s3c": ext["s3c"], "x1": x1i,
                       "s1f": s1i, "y": y})
    STAGE_MARKS.append((stage_n[0], nc.next_id()))
    return nc


def host_csrc(c):
    b, _ = core_chunks(c)
    perm = host_kperm(c)
    rows = []
    for j in range(8):
        gpos = int(perm[j * 1024])
        gchunk = gpos // 1024
        for r in range(4):
            rb, (rc0, rc1) = core_chunks((c // 4) * 4 + r)
            if rc0 == gchunk:
                rows.append(r * 2048 + 0)
                break
            if rc1 == gchunk:
                rows.append(r * 2048 + 1024)
                break
    assert len(rows) == 8
    return np.asarray(rows, np.int32).reshape(1, 8)



_PROGS = {}


def _prog(name, builder):
    if name not in _PROGS:
        _PROGS[name] = builder()
    return _PROGS[name]


def _col32(v):
    return np.ascontiguousarray(v.reshape(32, 128).T).astype(np.float32)


def host_fused_maps(I, c):
    x = I["L0_attn_x"].astype(np.float32)
    w1g, b1g, w2g, b2c = host_conv_weights(I["L0_attn_conv1_w"], I["L0_attn_conv1_b"],
                                           I["L0_attn_conv2_w"], I["L0_attn_conv2_b"])
    W1D, W2D = host_conv_diag(I["L0_attn_conv1_w"], I["L0_attn_conv2_w"])
    cos, sin = host_rope_tables(c)
    wg = np.concatenate([I["L0_ffn_Wg"], I["L0_ffn_Wg_s"][None]], 0)
    wu = np.concatenate([I["L0_ffn_Wu"], I["L0_ffn_Wu_s"][None]], 0)
    wd = np.concatenate([I["L0_ffn_Wd"], I["L0_ffn_Wd_s"][None]], 0)
    m = {
        "x": host_x_slice(x, c),
        "w1d": W1D, "w2d": W2D, "b1g": b1g, "b2": b2c,
        "g0w": host_col8(I["L0_attn_ln0_g"]), "b0w": host_col8(I["L0_attn_ln0_b"]),
        "cosd": cos, "sind": sin, "swm": host_swap_matrix(), "sel2": host_sel2(),
        "fw1": I["L0_attn_W1"].astype(np.float32), "fb1c": _col32(I["L0_attn_b1"]),
        "fw2": I["L0_attn_W2"].astype(np.float32), "fb2c": host_col8(I["L0_attn_b2"]),
        "flng": host_col8(I["L0_attn_ln2_g"]), "flnb": host_col8(I["L0_attn_ln2_b"]),
        "gw": I["L0_ffn_gate_w"].astype(np.float32),
        "cbr": np.broadcast_to(I["L0_ffn_correction_bias"][None, :], (128, 8)).copy().astype(np.float32),
        "identd": np.eye(128, dtype=np.float32), "sel8": host_sel8(),
        "wgp": np.stack([host_pm(wg[e], 8) for e in range(9)]),
        "wup": np.stack([host_pm(wu[e], 8) for e in range(9)]),
        "wdp": np.stack([host_pm(wd[e], 4) for e in range(9)]),
        "wkv": np.concatenate([I["L1_attn_wk_down"], I["L1_attn_wv_down"]], 1).astype(np.float32),
        "wqL1": I["L1_attn_wq"].astype(np.float32),
        "wku": I["L1_attn_wk_up"].astype(np.float32), "wvu": I["L1_attn_wv_up"].astype(np.float32),
        "wo": I["L1_attn_wo"].astype(np.float32),
        "biastab": host_bias_tab(c), "tri": host_tri_masks(), "csrc": host_csrc(c),
        "jidx": np.array([[c % 4]], np.int32),
        "fa_w1t": np.ascontiguousarray(I["L1_ffn_w1_w"].T).astype(np.float32),
        "fa_w2t": np.ascontiguousarray(I["L1_ffn_w2_w"].T).astype(np.float32),
        "fa_s1c": _col32(I["L1_ffn_w1_s"]), "fa_s2c": _col32(I["L1_ffn_w2_s"]),
        "h1t": np.ascontiguousarray(I["L1_ffn_h1_w"].T).astype(np.float32),
        "h2t": np.ascontiguousarray(I["L1_ffn_h2_w"].T).astype(np.float32),
        "h1s": _col32(I["L1_ffn_h1_s"]), "h2s": _col32(I["L1_ffn_h2_s"]),
        "w3t": np.ascontiguousarray(I["L1_ffn_w3_w"].T).astype(np.float32),
        "s3c": host_col8(I["L1_ffn_w3_s"]),
    }
    for i in range(4):
        m[f"wq{i}"] = np.ascontiguousarray(I["L0_attn_Wq"][i]).astype(np.float32)
        m[f"wk{i}"] = np.ascontiguousarray(I["L0_attn_Wk"][i]).astype(np.float32)
        m[f"wv{i}"] = np.ascontiguousarray(I["L0_attn_Wv"][i]).astype(np.float32)
        m[f"bq{i}"] = host_col8(I["L0_attn_bq"][i])
        m[f"bk{i}"] = host_col8(I["L0_attn_bk"][i])
        m[f"bvr{i}"] = I["L0_attn_bv"][i].reshape(1, D).astype(np.float32)
        m[f"lnag{i}"] = host_col8(I["L0_attn_lnA_g"][i])
        m[f"lnab{i}"] = host_col8(I["L0_attn_lnA_b"][i])
    return m


def kernel(**I):
    nc = _prog("fused", build_fused)
    cores = list(range(NCORE))
    maps = [host_fused_maps(I, c) for c in cores]
    res = run_bass_kernel_spmd(nc, maps, core_ids=cores).results
    out = np.zeros((2, S, D), np.float32)
    for c in cores:
        b, (c0, c1) = core_chunks(c)
        y = np.asarray(res[c]["y"])
        out[b, c0 * CH:(c0 + 1) * CH] = y[0:CH]
        out[b, c1 * CH:(c1 + 1) * CH] = y[CH:2 * CH]
    return out


def kernel_unfused(**I):
    x = I["L0_attn_x"].astype(np.float32)
    cores = list(range(NCORE))

    def run(name, builder, maps):
        nc = _prog(name, builder)
        return run_bass_kernel_spmd(nc, maps, core_ids=cores).results

    def fm_of(res, key):
        return [np.asarray(r[key]) for r in res]

    res = run("prep", build_prep, [{"x": host_x_slice(x, c)} for c in cores])
    xn0 = fm_of(res, "xn")

    w1g, b1g, w2g, b2c = host_conv_weights(I["L0_attn_conv1_w"], I["L0_attn_conv1_b"],
                                           I["L0_attn_conv2_w"], I["L0_attn_conv2_b"])
    W1D, W2D = host_conv_diag(I["L0_attn_conv1_w"], I["L0_attn_conv2_w"])
    g0c = host_col8(I["L0_attn_ln0_g"])
    b0c = host_col8(I["L0_attn_ln0_b"])
    res = run("conv", build_conv, [{"xn": xn0[c], "w1d": W1D, "w2d": W2D, "b1g": b1g,
                                    "b2": b2c, "g0w": g0c, "b0w": b0c} for c in cores])
    h = fm_of(res, "h0")

    swm = host_swap_matrix()
    sel2 = host_sel2()
    ropes = [host_rope_tables(c) for c in cores]
    for layer in range(4):
        Wq = np.ascontiguousarray(I["L0_attn_Wq"][layer])
        Wk = np.ascontiguousarray(I["L0_attn_Wk"][layer])
        Wv = np.ascontiguousarray(I["L0_attn_Wv"][layer])
        bq = host_col8(I["L0_attn_bq"][layer])
        bk = host_col8(I["L0_attn_bk"][layer])
        bvr = I["L0_attn_bv"][layer].reshape(1, D).astype(np.float32)
        res = run("la_a", build_la_a,
                  [{"h": h[c], "wq": Wq, "wk": Wk, "wv": Wv, "bq": bq, "bk": bk,
                    "bvr": bvr, "cosd": ropes[c][0], "sind": ropes[c][1], "swm": swm}
                   for c in cores])
        kvp = fm_of(res, "kvp")
        qs = fm_of(res, "q")
        qrs = fm_of(res, "qr")
        lnag = host_col8(I["L0_attn_lnA_g"][layer])
        lnab = host_col8(I["L0_attn_lnA_b"][layer])
        kvp_g = [np.stack([kvp[g * 4 + i] for i in range(4)]).astype(np.float32) for g in range(2)]
        res = run("la_b", build_la_b,
                  [{"kvp4": kvp_g[c // 4], "q": qs[c], "qr": qrs[c], "h": h[c],
                    "lnag": lnag, "lnab": lnab, "sel2": sel2} for c in cores])
        h = fm_of(res, "hn")

    b1c32 = _col32(I["L0_attn_b1"])
    res = run("ffn0", build_ffn0,
              [{"h": h[c], "w1": I["L0_attn_W1"], "b1c": b1c32, "w2": I["L0_attn_W2"],
                "b2c": host_col8(I["L0_attn_b2"]), "lng": host_col8(I["L0_attn_ln2_g"]),
                "lnb": host_col8(I["L0_attn_ln2_b"])} for c in cores])
    s0 = fm_of(res, "s0")

    wg = np.concatenate([I["L0_ffn_Wg"], I["L0_ffn_Wg_s"][None]], 0)
    wu = np.concatenate([I["L0_ffn_Wu"], I["L0_ffn_Wu_s"][None]], 0)
    wd = np.concatenate([I["L0_ffn_Wd"], I["L0_ffn_Wd_s"][None]], 0)
    wgp = np.stack([host_pm(wg[e], 8) for e in range(9)])
    wup = np.stack([host_pm(wu[e], 8) for e in range(9)])
    wdp = np.stack([host_pm(wd[e], 4) for e in range(9)])
    cbb = np.broadcast_to(I["L0_ffn_correction_bias"][None, :], (128, 8)).copy().astype(np.float32)
    identd = np.eye(128, dtype=np.float32)
    sel8 = host_sel8()
    res = run("moe", build_moe,
              [{"xn0": xn0[c], "gw": I["L0_ffn_gate_w"], "cbr": cbb, "identd": identd,
                "sel8": sel8, "wgp": wgp, "wup": wup, "wdp": wdp} for c in cores])
    g0 = fm_of(res, "g0")

    wkv = np.concatenate([I["L1_attn_wk_down"], I["L1_attn_wv_down"]], 1).astype(np.float32)
    res = run("comb0", build_comb0,
              [{"x": host_x_slice(x, c), "s0": s0[c], "g0": g0[c], "wkv": wkv,
                "wq": I["L1_attn_wq"]} for c in cores])
    x1 = fm_of(res, "x1")
    lat = fm_of(res, "lat")
    qfm = fm_of(res, "qfm")
    xn1 = fm_of(res, "xn1")

    lat_full = np.zeros((2, S, 128), lat[0].dtype)
    for c in cores:
        b, (c0, c1) = core_chunks(c)
        lat_full[b, c0 * CH:(c0 + 1) * CH] = lat[c][0:CH]
        lat_full[b, c1 * CH:(c1 + 1) * CH] = lat[c][CH:2 * CH]

    tri = host_tri_masks()
    maps = []
    for c in cores:
        b, _ = core_chunks(c)
        lk, lv = host_latp(lat_full[b], c)
        maps.append({"latk": lk, "latv": lv, "qfm": qfm[c],
                     "wku": I["L1_attn_wk_up"], "wvu": I["L1_attn_wv_up"],
                     "wo": I["L1_attn_wo"], "biastab": host_bias_tab(c), "tri": tri})
    res = run("l1attn", build_l1attn, maps)
    s1 = fm_of(res, "s1")

    w1t = np.ascontiguousarray(I["L1_ffn_w1_w"].T).astype(np.float32)
    w2t = np.ascontiguousarray(I["L1_ffn_w2_w"].T).astype(np.float32)
    h1t = np.ascontiguousarray(I["L1_ffn_h1_w"].T).astype(np.float32)
    h2t = np.ascontiguousarray(I["L1_ffn_h2_w"].T).astype(np.float32)
    w3t = np.ascontiguousarray(I["L1_ffn_w3_w"].T).astype(np.float32)
    res = run("l1ffn_a", build_l1ffn_a,
              [{"xn1": xn1[c], "w1t": w1t, "w2t": w2t, "s1c": _col32(I["L1_ffn_w1_s"]),
                "s2c": _col32(I["L1_ffn_w2_s"])} for c in cores])
    t1 = fm_of(res, "t1")
    res = run("l1ffn_h", build_l1ffn_h,
              [{"tin": t1[c], "wt": h1t, "sc": _col32(I["L1_ffn_h1_s"])} for c in cores])
    t2 = fm_of(res, "tout")
    res = run("l1ffn_h", build_l1ffn_h,
              [{"tin": t2[c], "wt": h2t, "sc": _col32(I["L1_ffn_h2_s"])} for c in cores])
    t3 = fm_of(res, "tout")
    res = run("l1ffn_c", build_l1ffn_c,
              [{"t3": t3[c], "w3t": w3t, "s3c": host_col8(I["L1_ffn_w3_s"]),
                "x1": x1[c], "s1f": s1[c]} for c in cores])
    y = fm_of(res, "y")

    out = np.zeros((2, S, D), np.float32)
    for c in cores:
        b, (c0, c1) = core_chunks(c)
        out[b, c0 * CH:(c0 + 1) * CH] = y[c][0:CH]
        out[b, c1 * CH:(c1 + 1) * CH] = y[c][CH:2 * CH]
    return out


# version 07 (final)
# speedup vs baseline: 1.0205x; 1.0205x over previous
import contextlib
import numpy as np
import ml_dtypes
import concourse.bass as bass
import concourse.mybir as mybir
from concourse.bass_utils import run_bass_kernel_spmd

F32 = mybir.dt.float32
BF16 = mybir.dt.bfloat16
AF = mybir.ActivationFunctionType
ALU = mybir.AluOpType
AX = mybir.AxisListType

BF = ml_dtypes.bfloat16

B, S, D = 2, 8192, 1024
NCORE = 8
CH = 1024
HALO = 64
CHH = CH + 2 * HALO
T = 2 * CH
NEG = -60000.0


def core_chunks(c):
    b, j = c // 4, c % 4
    return b, (j, 7 - j)



def host_x_slice(x, c):
    b, (c0, c1) = core_chunks(c)
    out = np.zeros((2, CHH, D), np.float32)
    for i, ch in enumerate((c0, c1)):
        lo, hi = ch * CH - HALO, (ch + 1) * CH + HALO
        slo, shi = max(lo, 0), min(hi, S)
        out[i, slo - lo:shi - lo, :] = x[b, slo:shi, :]
    return out


def host_rope_tables(c):
    b, (c0, c1) = core_chunks(c)
    pos = np.concatenate([np.arange(c0 * CH, (c0 + 1) * CH),
                          np.arange(c1 * CH, (c1 + 1) * CH)]).astype(np.float64)
    inv = 1.0 / 10000.0 ** (np.arange(0, D, 2, dtype=np.float64) / D)
    inv2 = np.repeat(inv, 2)
    ang = inv2[:, None] * pos[None, :]
    return (np.cos(ang).astype(np.float32).astype(BF),
            np.sin(ang).astype(np.float32).astype(BF))


def host_swap_matrix():
    Sm = np.zeros((128, 128), np.float32)
    for p in range(128):
        if p % 2 == 0:
            Sm[p + 1, p] = -1.0
        else:
            Sm[p - 1, p] = 1.0
    return Sm.astype(BF)


def host_sel2():
    m = np.zeros((2, 128), np.float32)
    m[0, :64] = 1.0
    m[1, 64:] = 1.0
    return m.astype(BF)


def host_tri_masks():
    k = np.arange(128)[:, None]
    q = np.arange(512)[None, :]
    return np.stack([(k <= q - 128 * i).astype(np.float32) for i in range(4)]).astype(BF)


def host_kperm(c):
    b, (c0, c1) = core_chunks(c)
    own = np.concatenate([np.arange(c0 * CH, (c0 + 1) * CH),
                          np.arange(c1 * CH, (c1 + 1) * CH)])
    rest = np.setdiff1d(np.arange(S), own)
    return np.concatenate([own, rest])


def host_bias_tab(c):
    b, (c0, c1) = core_chunks(c)
    perm = host_kperm(c)
    qbase = [c0 * CH, c0 * CH + 512, c1 * CH, c1 * CH + 512]
    tab = np.zeros((128, 4 * 64), np.float32)
    for qc in range(4):
        for kt in range(64):
            col = qc * 64 + kt
            if 4 * qc <= kt < 4 * qc + 4:
                continue
            kbase = int(perm[kt * 128])
            if kbase < qbase[qc]:
                tab[:, col] = 0.0
            else:
                tab[:, col] = NEG
    return tab



_scr_ctr = [0]


def _mk_dt(nc, io):
    def DT(name, shape, dtype, kind):
        if io is not None:
            if kind == "Internal":
                _scr_ctr[0] += 1
                return nc.dram_tensor(f"{name}_s{_scr_ctr[0]}", shape, dtype, kind="Internal").ap()
            return io[name]
        return nc.dram_tensor(name, shape, dtype, kind=kind).ap()
    return DT




class Ctr:
    def __init__(self, sem):
        self.sem = sem
        self.val = 0

    def inc(self, n=1):
        self.val += n
        return self.val


def tm_layernorm_tiles(nc, v, s, src_tile, dst_tile, scr, m, r, eps):
    v.reduce_sum(m[:, 0:1], src_tile, axis=AX.X)
    v.tensor_scalar(m[:, 1:2], m[:, 0:1], 1.0 / D, None, ALU.mult)
    s.activation(out=scr, in_=src_tile, func=AF.Square, accum_out=m[:, 2:3])
    v.tensor_scalar(m[:, 3:4], m[:, 2:3], 1.0 / D, None, ALU.mult)
    v.tensor_tensor(r[:, 0:1], m[:, 1:2], m[:, 1:2], ALU.mult)
    v.tensor_tensor(r[:, 1:2], m[:, 3:4], r[:, 0:1], ALU.subtract)
    v.tensor_scalar(r[:, 1:2], r[:, 1:2], eps, None, ALU.add)
    s.sqrt(r[:, 2:3], r[:, 1:2])
    v.reciprocal(r[:, 3:4], r[:, 2:3])
    v.tensor_tensor(m[:, 0:1], m[:, 1:2], r[:, 3:4], ALU.mult)
    v.tensor_scalar(m[:, 0:1], m[:, 0:1], -1.0, None, ALU.mult)
    v.tensor_scalar(dst_tile, src_tile, r[:, 3:4], m[:, 0:1], ALU.mult, ALU.add)



def emit_tm_ln(block, sync_waits, x_src, out_dst, n_tiles, buffers, eps=1e-5,
               ld=None, vs=None, sv=None, st=None):
    pass


def build_prep(nc=None, io=None):
    if nc is None:
        nc = bass.Bass("TRN2", target_bir_lowering=False, debug=False)
    DT = _mk_dt(nc, io)
    x = DT("x", [2, CHH, D], F32, "ExternalInput")
    xn = DT("xn", [2, CHH, D], BF16, "ExternalOutput")
    NT = CHH // 128
    n_tiles = 2 * NT
    with (
        nc.sbuf_tensor("xt", [128, 2 * D], F32) as xt,
        nc.sbuf_tensor("ot", [128, 2 * D], BF16) as ot,
        nc.sbuf_tensor("scr", [128, D], F32) as scr,
        nc.sbuf_tensor("mm_", [128, 4], F32) as mc,
        nc.sbuf_tensor("rr_", [128, 4], F32) as rc,
        nc.semaphore("ld") as ld,
        nc.semaphore("ldo") as ldo,
        nc.semaphore("sv") as sv,
        nc.semaphore("vs") as vs,
        nc.semaphore("vch") as vch,
        nc.semaphore("ste") as ste,
        nc.semaphore("sto") as sto,
        nc.Block() as block,
    ):
        @block.sync
        def _(sync):
            for i in range(min(2, n_tiles)):
                ch, t = divmod(i, NT)
                sync.dma_start(out=xt[:, (i % 2) * D:(i % 2 + 1) * D],
                               in_=x[ch, t * 128:(t + 1) * 128, :]).then_inc(ld if i % 2 == 0 else ldo, 16)
            for i in range(n_tiles):
                ch, t = divmod(i, NT)
                buf = i % 2
                sync.wait_ge(vs, 2 * i + 2)
                sync.dma_start(out=xn[ch, t * 128:(t + 1) * 128, :],
                               in_=ot[:, buf * D:(buf + 1) * D]).then_inc(ste if i % 2 == 0 else sto, 16)
                if i + 2 < n_tiles:
                    ch2, t2 = divmod(i + 2, NT)
                    sync.dma_start(out=xt[:, buf * D:(buf + 1) * D],
                                   in_=x[ch2, t2 * 128:(t2 + 1) * 128, :]).then_inc(ld if (i + 2) % 2 == 0 else ldo, 16)
            sync.wait_ge(ste, 16 * ((n_tiles + 1) // 2))
            sync.wait_ge(sto, 16 * (n_tiles // 2))

        @block.vector
        def _(v):
            vc = [0]
            def ch(inst):
                vc[0] += 1
                inst.then_inc(vch, 1)
                v.wait_ge(vch, vc[0])
            for i in range(n_tiles):
                buf = i % 2
                src = xt[:, buf * D:(buf + 1) * D]
                v.wait_ge(ld if i % 2 == 0 else ldo, 16 * (i // 2 + 1))
                if i >= 1:
                    v.wait_ge(vs, 2 * i)
                ch(v.reduce_sum(mc[:, 0:1], src, axis=AX.X))
                ch(v.tensor_scalar(mc[:, 1:2], mc[:, 0:1], 1.0 / D, None, ALU.mult))
                v.wait_ge(sv, 2 * i + 1)
                ch(v.tensor_scalar(mc[:, 3:4], mc[:, 2:3], 1.0 / D, None, ALU.mult))
                ch(v.tensor_tensor(rc[:, 0:1], mc[:, 1:2], mc[:, 1:2], ALU.mult))
                ch(v.tensor_tensor(rc[:, 1:2], mc[:, 3:4], rc[:, 0:1], ALU.subtract))
                ch(v.tensor_scalar(rc[:, 1:2], rc[:, 1:2], 1e-5, None, ALU.add))
                v.sem_inc(vs, 1)
                v.wait_ge(sv, 2 * i + 2)
                ch(v.reciprocal(rc[:, 3:4], rc[:, 2:3]))
                ch(v.tensor_tensor(mc[:, 0:1], mc[:, 1:2], rc[:, 3:4], ALU.mult))
                ch(v.tensor_scalar(mc[:, 0:1], mc[:, 0:1], -1.0, None, ALU.mult))
                if i >= 2:
                    v.wait_ge(ste if i % 2 == 0 else sto, 16 * (i // 2))
                v.tensor_scalar(ot[:, buf * D:(buf + 1) * D], src, rc[:, 3:4],
                                mc[:, 0:1], ALU.mult, ALU.add).then_inc(vs, 1)

        @block.scalar
        def _(s):
            for i in range(n_tiles):
                buf = i % 2
                src = xt[:, buf * D:(buf + 1) * D]
                s.wait_ge(ld if i % 2 == 0 else ldo, 16 * (i // 2 + 1))
                if i >= 1:
                    s.wait_ge(vs, 2 * (i - 1) + 2)
                s.activation(out=scr[:], in_=src, func=AF.Square,
                             accum_out=mc[:, 2:3]).then_inc(sv, 1)
                s.wait_ge(vs, 2 * i + 1)
                s.sqrt(rc[:, 2:3], rc[:, 1:2]).then_inc(sv, 1)
                s.wait_ge(sv, 2 * i + 2)
    return nc


def host_col8(vec):
    return np.ascontiguousarray(vec.reshape(8, 128).T).astype(np.float32)


def host_conv_weights(conv1_w, conv1_b, conv2_w, conv2_b):
    w1g = conv1_w.reshape(1024, 4, 5).reshape(1024, 20)
    b1g = np.ascontiguousarray(conv1_b.reshape(1024, 4))
    w2g = conv2_w.reshape(1024, 20)
    return w1g.astype(np.float32), b1g.astype(np.float32), w2g.astype(np.float32), host_col8(conv2_b)


def host_conv_diag(conv1_w, conv2_w):
    w1 = conv1_w.reshape(1024, 4, 5)
    w2 = conv2_w
    W1D = np.zeros((4, 5, 8, 128, 128), np.float32)
    W2D = np.zeros((4, 5, 8, 128, 128), np.float32)
    idx = np.arange(128)
    for j in range(4):
        for k in range(5):
            for ct in range(8):
                W1D[j, k, ct, idx, idx] = w1[ct * 128:(ct + 1) * 128, j, k]
                W2D[j, k, ct, idx, idx] = w2[ct * 128:(ct + 1) * 128, j, k]
    W1D = np.ascontiguousarray(W1D.reshape(160, 128, 128).transpose(1, 0, 2).reshape(128, 160 * 128))
    W2D = np.ascontiguousarray(W2D.reshape(160, 128, 128).transpose(1, 0, 2).reshape(128, 160 * 128))
    return W1D, W2D


def build_conv(nc=None, io=None):
    WI = 1040
    WG = 1028
    NCH1 = [(0, 512), (512, 512), (1024, 4)]
    if nc is None:
        nc = bass.Bass("TRN2", target_bir_lowering=False, debug=False)
    DT = _mk_dt(nc, io)
    xn = DT("xn", [2, CHH, D], BF16, "ExternalInput")
    w1d = DT("w1d", [128, 160 * 128], F32, "ExternalInput")
    w2d = DT("w2d", [128, 160 * 128], F32, "ExternalInput")
    b1gt = DT("b1g", [D, 4], F32, "ExternalInput")
    b2 = DT("b2", [128, 8], F32, "ExternalInput")
    g0w = DT("g0w", [128, 8], F32, "ExternalInput")
    b0w = DT("b0w", [128, 8], F32, "ExternalInput")
    h0 = DT("h0", [D, T], BF16, "ExternalOutput")

    with contextlib.ExitStack() as ctx:
        E = ctx.enter_context
        E(nc.allow_low_precision(reason="bf16 pipeline; tolerance measured for bf16 matmuls"))
        xnf = E(nc.sbuf_tensor("xnf", [128, 8 * WI], BF16))
        gbuf = E(nc.sbuf_tensor("gbuf", [128, 32 * WG], BF16))
        w1ds = E(nc.sbuf_tensor("w1ds", [128, 160 * 128], BF16))
        w2ds = E(nc.sbuf_tensor("w2ds", [128, 160 * 128], BF16))
        rsb = E(nc.sbuf_tensor("rsb", [128, 8 * 1024], BF16))
        sqb = E(nc.sbuf_tensor("sqb", [128, 2 * 1024], BF16))
        b1s = E(nc.sbuf_tensor("b1s", [128, 32], F32))
        b2s = E(nc.sbuf_tensor("b2s", [128, 8], F32))
        g0s = E(nc.sbuf_tensor("g0s", [128, 8], F32))
        b0s = E(nc.sbuf_tensor("b0s", [128, 8], F32))
        onesb = E(nc.sbuf_tensor("onesb", [128, 1], BF16))
        onesr = E(nc.sbuf_tensor("onesr", [1, 128], F32))
        stat = E(nc.sbuf_tensor("stat", [1, 5 * 1024], F32))
        mbs = E(nc.sbuf_tensor("mbs", [128, 1024], BF16))
        rbs = E(nc.sbuf_tensor("rbs", [128, 1024], BF16))
        ps = E(nc.psum_tensor("cps", [128, 8 * 512], F32))
        ini = E(nc.semaphore("ini"))
        ld = E(nc.semaphore("ld"))
        ldc = E(nc.semaphore("ldc"))
        ldw = E(nc.semaphore("ldw"))
        mm1 = E(nc.semaphore("mm1"))
        gl = E(nc.semaphore("gl"))
        mm2 = E(nc.semaphore("mm2"))
        rdn = E(nc.semaphore("rdn"))
        sqs = E(nc.semaphore("sqs"))
        mms = E(nc.semaphore("mms"))
        mmd = E(nc.semaphore("mmd"))
        std = E(nc.semaphore("std"))
        bcd = E(nc.semaphore("bcd"))
        apd = E(nc.semaphore("apd"))
        vch = E(nc.semaphore("vch"))
        st = E(nc.semaphore("st"))
        block = E(nc.Block())

        vch_n = [0]
        NCONST = 8 + 3

        def wslice(ws, j, k, ct):
            base = ((j * 5 + k) * 8 + ct) * 128
            return ws[:, base:base + 128]

        @block.gpsimd
        def _(g):
            g.memset(onesb[:], 1.0)
            g.memset(onesr[:], 1.0).then_inc(ini, 1)
            g.dma_start(out=w1ds[:], in_=w1d).then_inc(ldw, 16)
            g.dma_start(out=w2ds[:], in_=w2d).then_inc(ldw, 16)

        @block.sync
        def _(sync):
            for ct in range(8):
                sync.dma_start(out=b1s[:, ct * 4:(ct + 1) * 4], in_=b1gt[ct * 128:(ct + 1) * 128, :]).then_inc(ldc, 16)
            sync.dma_start(out=b2s[:], in_=b2).then_inc(ldc, 16)
            sync.dma_start(out=g0s[:], in_=g0w).then_inc(ldc, 16)
            sync.dma_start(out=b0s[:], in_=b0w).then_inc(ldc, 16)
            for ch_idx in range(2):
                for ct in range(8):
                    sync.dma_start_transpose(xnf[:, ct * WI:(ct + 1) * WI],
                                             xn[ch_idx, 56:56 + WI, ct * 128:(ct + 1) * 128]).then_inc(ld, 16)
                for ct in range(8):
                    sync.wait_ge(apd, 8 * ch_idx + ct + 1)
                    sync.dma_start(out=h0[ct * 128:(ct + 1) * 128, ch_idx * 1024:(ch_idx + 1) * 1024],
                                   in_=gbuf[:, ct * WG:ct * WG + 1024]).then_inc(st, 16)
            sync.wait_ge(st, 16 * 16)

        @block.tensor
        def _(t):
            t.wait_ge(ini, 1)
            t.wait_ge(ldc, 16 * NCONST)
            t.wait_ge(ldw, 32)
            n1 = 12 * 8
            n2 = 16
            for ch_idx in range(2):
                t.wait_ge(ld, 16 * 8 * (ch_idx + 1))
                gidx = 0
                if ch_idx > 0:
                    t.wait_ge(bcd, ch_idx)
                for ct in range(8):
                    for j in range(4):
                        for (off, w) in NCH1:
                            bank = gidx % 2
                            if gidx >= 2:
                                t.wait_ge(gl, 96 * ch_idx + gidx - 1)
                            last = None
                            for k in range(5):
                                last = t.matmul(ps[:, bank * 512:bank * 512 + w],
                                                wslice(w1ds, j, k, ct),
                                                xnf[:, ct * WI + 4 + k + off:ct * WI + 4 + k + off + w],
                                                start=(k == 0), stop=(k == 4))
                            last.then_inc(mm1, 1)
                            gidx += 1
                for ct in range(8):
                    t.wait_ge(gl, 96 * ch_idx + 12 * (ct + 1))
                    for ni, (off, w) in enumerate([(0, 512), (512, 512)]):
                        grp = ct * 2 + ni
                        bank = 2 + grp % 2
                        if grp >= 2:
                            t.wait_ge(rdn, 8 * ch_idx * 2 + grp - 1)
                        last = None
                        mmi = 0
                        for j in range(4):
                            for k in range(5):
                                last = t.matmul(ps[:, bank * 512:bank * 512 + w],
                                                wslice(w2ds, j, k, ct),
                                                gbuf[:, (j * 8 + ct) * WG + off + k:(j * 8 + ct) * WG + off + k + w],
                                                start=(mmi == 0), stop=(mmi == 19))
                                mmi += 1
                        last.then_inc(mm2, 1)
                if ch_idx > 0:
                    t.wait_ge(std, 3 * (ch_idx - 1) + 1)
                for ct in range(8):
                    t.wait_ge(rdn, 16 * ch_idx + 2 * (ct + 1))
                    for half in range(2):
                        t.matmul(ps[0:1, (4 + half) * 512:(5 + half) * 512], onesb[:, 0:1],
                                 rsb[:, ct * 1024 + half * 512:ct * 1024 + (half + 1) * 512],
                                 start=(ct == 0), stop=(ct == 7))
                    t.wait_ge(sqs, 8 * ch_idx + ct + 1)
                    last = None
                    for half in range(2):
                        last = t.matmul(ps[0:1, (6 + half) * 512:(7 + half) * 512], onesb[:, 0:1],
                                        sqb[:, (ct % 2) * 1024 + half * 512:(ct % 2) * 1024 + (half + 1) * 512],
                                        start=(ct == 0), stop=(ct == 7))
                    if ct < 7:
                        last.then_inc(mms, 1)
                    else:
                        last.then_inc(mmd, 1)
                t.wait_ge(std, 3 * ch_idx + 3)
                if ch_idx > 0:
                    t.wait_ge(bcd, ch_idx)
                last = None
                for half in range(2):
                    t.matmul(ps[:, (0 + half) * 512:(1 + half) * 512], onesr[0:1, :],
                             stat[0:1, half * 512:512 + half * 512], start=True, stop=True)
                    last = t.matmul(ps[:, (2 + half) * 512:(3 + half) * 512], onesr[0:1, :],
                                    stat[0:1, 3072 + half * 512:3584 + half * 512], start=True, stop=True)
                last.then_inc(mmd, 1)

        @block.scalar
        def _(s):
            s.wait_ge(ldc, 16 * NCONST)
            for ch_idx in range(2):
                if ch_idx > 0:
                    s.wait_ge(st, 16 * 8)
                gidx = 0
                for ct in range(8):
                    for j in range(4):
                        for (off, w) in NCH1:
                            s.wait_ge(mm1, 96 * ch_idx + gidx + 1)
                            bank = gidx % 2
                            s.activation(out=gbuf[:, (j * 8 + ct) * WG + off:(j * 8 + ct) * WG + off + w],
                                         in_=ps[:, bank * 512:bank * 512 + w],
                                         func=AF.Gelu, bias=b1s[:, ct * 4 + j:ct * 4 + j + 1]).then_inc(gl, 1)
                            gidx += 1
                for ct in range(8):
                    s.wait_ge(rdn, 16 * ch_idx + 2 * (ct + 1))
                    if ct >= 2:
                        s.wait_ge(mms, 7 * ch_idx + ct - 1)
                    s.activation(out=sqb[:, (ct % 2) * 1024:(ct % 2 + 1) * 1024],
                                 in_=rsb[:, ct * 1024:(ct + 1) * 1024], func=AF.Square).then_inc(sqs, 1)
                s.wait_ge(std, 3 * ch_idx + 1)
                s.sqrt(stat[0:1, 4096:4608], stat[0:1, 2048:2560])
                s.sqrt(stat[0:1, 4608:5120], stat[0:1, 2560:3072]).then_inc(std, 1)
                s.wait_ge(mmd, 2 * ch_idx + 2)
                s.copy(mbs[:, 0:512], ps[:, 0:512])
                s.copy(mbs[:, 512:1024], ps[:, 512:1024])
                s.copy(rbs[:, 0:512], ps[:, 1024:1536])
                s.copy(rbs[:, 512:1024], ps[:, 1536:2048]).then_inc(bcd, 1)

        @block.vector
        def _(v):
            def chv(inst):
                vch_n[0] += 1
                inst.then_inc(vch, 1)
                v.wait_ge(vch, vch_n[0])
            v.wait_ge(ldc, 16 * NCONST)
            for ch_idx in range(2):
                for ct in range(8):
                    for ni, (off, w) in enumerate([(0, 512), (512, 512)]):
                        grp = ct * 2 + ni
                        bank = 2 + grp % 2
                        v.wait_ge(mm2, 16 * ch_idx + grp + 1)
                        xin = xnf[:, ct * WI + 8 + off:ct * WI + 8 + off + w]
                        v.scalar_tensor_tensor(rsb[:, ct * 1024 + off:ct * 1024 + off + w],
                                               ps[:, bank * 512:bank * 512 + w],
                                               b2s[:, ct:ct + 1], xin,
                                               ALU.add, ALU.add).then_inc(rdn, 1)
                v.wait_ge(mmd, 2 * ch_idx + 1)
                for half in range(2):
                    sm = ps[0:1, (4 + half) * 512:(5 + half) * 512]
                    sq_ = ps[0:1, (6 + half) * 512:(7 + half) * 512]
                    m_ = stat[0:1, half * 512:512 + half * 512]
                    msq = stat[0:1, 1024 + half * 512:1536 + half * 512]
                    var = stat[0:1, 2048 + half * 512:2560 + half * 512]
                    chv(v.tensor_scalar(m_, sm, 1.0 / 1024, None, ALU.mult))
                    chv(v.tensor_scalar(msq, sq_, 1.0 / 1024, None, ALU.mult))
                    chv(v.tensor_tensor(var, m_, m_, ALU.mult))
                    chv(v.tensor_tensor(var, msq, var, ALU.subtract))
                    chv(v.tensor_scalar(var, var, 1e-12, None, ALU.add))
                v.sem_inc(std, 1)
                v.wait_ge(std, 3 * ch_idx + 2)
                for half in range(2):
                    chv(v.reciprocal(stat[0:1, 3072 + half * 512:3584 + half * 512],
                                     stat[0:1, 4096 + half * 512:4608 + half * 512]))
                v.sem_inc(std, 1)
                v.wait_ge(bcd, ch_idx + 1)
                for ct in range(8):
                    r_ = rsb[:, ct * 1024:(ct + 1) * 1024]
                    dst = gbuf[:, ct * WG:ct * WG + 1024]
                    chv(v.tensor_tensor(dst, r_, mbs[:], ALU.subtract))
                    chv(v.tensor_tensor(dst, dst, rbs[:], ALU.mult))
                    v.tensor_scalar(dst, dst, g0s[:, ct:ct + 1], b0s[:, ct:ct + 1],
                                    ALU.mult, ALU.add).then_inc(apd, 1)
    return nc


def build_la_a(nc=None, io=None):
    if nc is None:
        nc = bass.Bass("TRN2", target_bir_lowering=False, debug=False)
    DT = _mk_dt(nc, io)
    h = DT("h", [D, T], BF16, "ExternalInput")
    wq = DT("wq", [D, D], F32, "ExternalInput")
    wk = DT("wk", [D, D], F32, "ExternalInput")
    wv = DT("wv", [D, D], F32, "ExternalInput")
    bq = DT("bq", [128, 8], F32, "ExternalInput")
    bk = DT("bk", [128, 8], F32, "ExternalInput")
    bvr = DT("bvr", [1, D], F32, "ExternalInput")
    cosd = DT("cosd", [D, T], BF16, "ExternalInput")
    sind = DT("sind", [D, T], BF16, "ExternalInput")
    swm = DT("swm", [128, 128], BF16, "ExternalInput")
    kvp = DT("kvp", [128, 1032], F32, "ExternalOutput")
    qo = DT("q", [D, T], BF16, "ExternalOutput")
    qro = DT("qr", [D, T], BF16, "ExternalOutput")
    dkr = DT("dkr", [D, T], BF16, "Internal")

    NCHK = 4
    with contextlib.ExitStack() as ctx:
        E = ctx.enter_context
        E(nc.allow_low_precision(reason="bf16 pipeline; tolerance measured for bf16 matmuls"))
        hs = E(nc.sbuf_tensor("hs", [128, 8 * T], BF16))
        wqs = E(nc.sbuf_tensor("wqs", [128, 8 * D], BF16))
        wks = E(nc.sbuf_tensor("wks", [128, 8 * D], BF16))
        wvs = E(nc.sbuf_tensor("wvs", [128, 8 * D], BF16))
        bqs = E(nc.sbuf_tensor("bqs", [128, 8], F32))
        bks = E(nc.sbuf_tensor("bks", [128, 8], F32))
        bvb = E(nc.sbuf_tensor("bvb", [128, D], BF16))
        cs = E(nc.sbuf_tensor("cs", [128, 2 * 8 * 512], BF16))
        ss = E(nc.sbuf_tensor("ss", [128, 2 * 8 * 512], BF16))
        sws = E(nc.sbuf_tensor("sws", [128, 128], BF16))
        tmp = E(nc.sbuf_tensor("tmp", [128, 8 * 512], BF16))
        kqb = E(nc.sbuf_tensor("kqb", [128, 8 * 512], BF16))
        krb = E(nc.sbuf_tensor("krb", [128, 8 * 512], BF16))
        t1b = E(nc.sbuf_tensor("t1b", [128, 8 * 512], BF16))
        ksum = E(nc.sbuf_tensor("ksum", [128, 10], F32))
        vtm = E(nc.sbuf_tensor("vtm", [128, 2 * 4 * 1024], BF16))
        ktm = E(nc.sbuf_tensor("ktm", [128, 2 * 4 * 1024], BF16))
        kvo = E(nc.sbuf_tensor("kvo", [128, 1032], F32))
        onesr = E(nc.sbuf_tensor("onesr", [1, 128], F32))
        ps = E(nc.psum_tensor("laps", [128, 8 * 512], F32))
        ini = E(nc.semaphore("ini"))
        lh = E(nc.semaphore("lh"))
        ldw = E(nc.semaphore("ldw"))
        ldc = E(nc.semaphore("ldc"))
        ldre = E(nc.semaphore("ldre"))
        ldro = E(nc.semaphore("ldro"))
        mmp = E(nc.semaphore("mmp"))
        evp = E(nc.semaphore("evp"))
        vk1 = E(nc.semaphore("vk1"))
        mmsw = E(nc.semaphore("mmsw"))
        rp1 = E(nc.semaphore("rp1"))
        vkr = E(nc.semaphore("vkr"))
        stkr = E(nc.semaphore("stkr"))
        stq = E(nc.semaphore("stq"))
        ltr = E(nc.semaphore("ltr"))
        mmv = E(nc.semaphore("mmv"))
        vv = E(nc.semaphore("vv"))
        mmkv = E(nc.semaphore("mmkv"))
        kvev = E(nc.semaphore("kvev"))
        evv = E(nc.semaphore("evv"))
        vch = E(nc.semaphore("vch"))
        stf = E(nc.semaphore("stf"))
        block = E(nc.Block())

        vch_n = [0]

        @block.gpsimd
        def _(g):
            g.memset(onesr[:], 1.0).then_inc(ini, 1)
            for wsrc, wdst in ((wq, wqs), (wk, wks), (wv, wvs)):
                for kt in range(8):
                    g.dma_start(out=wdst[:, kt * D:(kt + 1) * D],
                                in_=wsrc[kt * 128:(kt + 1) * 128, :]).then_inc(ldw, 16)

        @block.sync
        def _(sync):
            for kt in range(8):
                sync.dma_start(out=hs[:, kt * T:(kt + 1) * T], in_=h[kt * 128:(kt + 1) * 128, :]).then_inc(lh, 16)
            sync.dma_start(out=bqs[:], in_=bq).then_inc(ldc, 16)
            sync.dma_start(out=bks[:], in_=bk).then_inc(ldc, 16)
            sync.dma_start(out=sws[:], in_=swm).then_inc(ldc, 16)
            sync.dma_start(out=kvo[0:1, 0:1024], in_=bvr).then_inc(ldc, 16)
            for tc in range(min(2, NCHK)):
                sem = ldre if tc % 2 == 0 else ldro
                for mt in range(8):
                    buf = (tc % 2) * 8
                    sync.dma_start(out=cs[:, (buf + mt) * 512:(buf + mt + 1) * 512],
                                   in_=cosd[mt * 128:(mt + 1) * 128, tc * 512:(tc + 1) * 512]).then_inc(sem, 16)
                    sync.dma_start(out=ss[:, (buf + mt) * 512:(buf + mt + 1) * 512],
                                   in_=sind[mt * 128:(mt + 1) * 128, tc * 512:(tc + 1) * 512]).then_inc(sem, 16)
            for tc in range(NCHK):
                for mt in range(8):
                    sync.wait_ge(vkr, 16 * tc + mt + 1)
                    sync.dma_start(out=dkr[mt * 128:(mt + 1) * 128, tc * 512:(tc + 1) * 512],
                                   in_=krb[:, mt * 512:(mt + 1) * 512]).then_inc(stkr, 16)
                for tb in range(4):
                    sync.wait_ge(stkr, 16 * 8 * (tc + 1))
                    if tc >= 2:
                        sync.wait_ge(mmkv, 4 * (tc - 1))
                    sync.dma_start_transpose(ktm[:, ((tc % 2) * 4 + tb) * 1024:((tc % 2) * 4 + tb + 1) * 1024],
                                             dkr[:, tc * 512 + tb * 128:tc * 512 + (tb + 1) * 128]).then_inc(ltr, 16)
                for mt in range(8):
                    sync.wait_ge(vkr, 16 * tc + 8 + mt + 1)
                    sync.dma_start(out=qo[mt * 128:(mt + 1) * 128, tc * 512:(tc + 1) * 512],
                                   in_=kqb[:, mt * 512:(mt + 1) * 512]).then_inc(stq, 16)
                    sync.dma_start(out=qro[mt * 128:(mt + 1) * 128, tc * 512:(tc + 1) * 512],
                                   in_=krb[:, mt * 512:(mt + 1) * 512]).then_inc(stq, 16)
                if tc + 2 < NCHK:
                    sem = ldre if tc % 2 == 0 else ldro
                    for mt in range(8):
                        buf = (tc % 2) * 8
                        sync.dma_start(out=cs[:, (buf + mt) * 512:(buf + mt + 1) * 512],
                                       in_=cosd[mt * 128:(mt + 1) * 128, (tc + 2) * 512:(tc + 3) * 512]).then_inc(sem, 16)
                        sync.dma_start(out=ss[:, (buf + mt) * 512:(buf + mt + 1) * 512],
                                       in_=sind[mt * 128:(mt + 1) * 128, (tc + 2) * 512:(tc + 3) * 512]).then_inc(sem, 16)
            sync.wait_ge(evv, 2)
            sync.dma_start(out=kvp, in_=kvo[:]).then_inc(stf, 16)
            sync.wait_ge(stf, 16)
            sync.wait_ge(stq, 16 * 16 * NCHK)
            sync.wait_ge(stkr, 16 * 8 * NCHK)
            sync.wait_ge(ltr, 64 * NCHK)
            sync.wait_ge(lh, 16 * 8)
            sync.wait_ge(ldw, 16 * 24)
            sync.wait_ge(ldc, 16 * 4)
            sync.wait_ge(ldre, 16 * 32)
            sync.wait_ge(ldro, 16 * 32)

        @block.tensor
        def _(t):
            t.wait_ge(ini, 1)
            t.wait_ge(lh, 16 * 8)
            t.wait_ge(ldw, 16 * 24)
            t.wait_ge(ldc, 16 * 4)
            t.matmul(ps[:, 3 * 512:4 * 512], onesr[0:1, :], kvo[0:1, 0:512], start=True, stop=True)
            t.matmul(ps[:, 2 * 512:3 * 512], onesr[0:1, :], kvo[0:1, 512:1024],
                     start=True, stop=True).then_inc(mmsw, 1)
            pgrp = 0
            sgrp = 0
            for tc in range(NCHK):
                for phase in range(2):
                    wsrc = wks if phase == 0 else wqs
                    for mt in range(8):
                        bank = pgrp % 2
                        if pgrp >= 2:
                            t.wait_ge(evp, pgrp - 1)
                        last = None
                        for kt in range(8):
                            last = t.matmul(ps[:, bank * 512:(bank + 1) * 512],
                                            wsrc[:, kt * D + mt * 128:kt * D + (mt + 1) * 128],
                                            hs[:, kt * T + tc * 512:kt * T + (tc + 1) * 512],
                                            start=(kt == 0), stop=(kt == 7))
                        last.then_inc(mmp, 1)
                        pgrp += 1
                    for mt in range(8):
                        bank = 2 + sgrp % 2
                        if sgrp == 0:
                            t.wait_ge(evv, 1)
                        if sgrp >= 2:
                            t.wait_ge(rp1, sgrp - 1)
                        t.wait_ge(vk1, 16 * tc + phase * 8 + mt + 1)
                        t.matmul(ps[:, bank * 512:(bank + 1) * 512], sws[:],
                                 kqb[:, mt * 512:(mt + 1) * 512], start=True, stop=True).then_inc(mmsw, 1)
                        sgrp += 1
                    if phase == 0:
                        for tb in range(4):
                            g = tc * 4 + tb
                            if g >= 1:
                                t.wait_ge(vv, g)
                            last = None
                            for half in range(2):
                                bank = 6 + half
                                for kt in range(8):
                                    last = t.matmul(ps[:, bank * 512:(bank + 1) * 512],
                                                    hs[:, kt * T + tc * 512 + tb * 128:kt * T + tc * 512 + (tb + 1) * 128],
                                                    wvs[:, kt * D + half * 512:kt * D + (half + 1) * 512],
                                                    start=(kt == 0), stop=(kt == 7))
                                last.then_inc(mmv, 1)
                        for tb in range(4):
                            g = tc * 4 + tb
                            t.wait_ge(ltr, 64 * (tc + 1))
                            t.wait_ge(vv, g + 1)
                            if g >= 1:
                                t.wait_ge(kvev, g)
                            ring = (tc % 2) * 4 + tb
                            last = None
                            for hp in range(8):
                                bank = 4 + hp // 4
                                reg = bank * 512 + (hp % 4) * 128
                                last = t.matmul(ps[:, reg:reg + 128],
                                                ktm[:, ring * 1024 + hp * 128:ring * 1024 + (hp + 1) * 128],
                                                vtm[:, ring * 1024 + hp * 128:ring * 1024 + (hp + 1) * 128],
                                                start=True, stop=True)
                            last.then_inc(mmkv, 1)

        @block.scalar
        def _(s):
            s.wait_ge(ldc, 16 * 4)
            s.wait_ge(mmsw, 1)
            s.copy(bvb[:, 0:512], ps[:, 3 * 512:4 * 512])
            s.copy(bvb[:, 512:1024], ps[:, 2 * 512:3 * 512]).then_inc(evv, 1)
            pgrp = 0
            for tc in range(NCHK):
                for phase in range(2):
                    bias = bks if phase == 0 else bqs
                    for mt in range(8):
                        bank = pgrp % 2
                        s.wait_ge(mmp, pgrp + 1)
                        if pgrp >= 2:
                            s.wait_ge(vk1, pgrp - 1)
                        s.activation(out=tmp[:, mt * 512:(mt + 1) * 512],
                                     in_=ps[:, bank * 512:(bank + 1) * 512],
                                     func=AF.Gelu, bias=bias[:, mt:mt + 1]).then_inc(evp, 1)
                        pgrp += 1


        @block.vector
        def _(v):
            def chv(inst):
                vch_n[0] += 1
                inst.then_inc(vch, 1)
                v.wait_ge(vch, vch_n[0])
            v.wait_ge(ldc, 16 * 4)
            v.wait_ge(mmsw, 1)
            chv(v.memset(kvo[:, 0:1024], 0.0))
            pgrp = 0
            sgrp = 0
            for tc in range(NCHK):
                for phase in range(2):
                    for mt in range(8):
                        v.wait_ge(evp, pgrp + 1)
                        if phase == 0 and tc >= 1:
                            v.wait_ge(stq, 16 * 16 * tc)
                        if phase == 1:
                            v.wait_ge(mmsw, 1 + 16 * tc + mt + 1)
                        v.tensor_scalar(kqb[:, mt * 512:(mt + 1) * 512], tmp[:, mt * 512:(mt + 1) * 512],
                                        0.21, None, ALU.add).then_inc(vk1, 1)
                        pgrp += 1
                    if phase == 0:
                        for mt in range(8):
                            chv(v.reduce_sum(ksum[:, 8:9], kqb[:, mt * 512:(mt + 1) * 512], axis=AX.X))
                            if tc == 0:
                                chv(v.tensor_copy(ksum[:, mt:mt + 1], ksum[:, 8:9]))
                            else:
                                chv(v.tensor_tensor(ksum[:, mt:mt + 1], ksum[:, mt:mt + 1], ksum[:, 8:9], ALU.add))
                    for mt in range(8):
                        v.wait_ge(mmsw, 1 + sgrp + 1)
                        bank = 2 + sgrp % 2
                        buf = (tc % 2) * 8
                        v.wait_ge(ldre if tc % 2 == 0 else ldro, 16 * 16 * (tc // 2 + 1))
                        if phase == 0 and tc >= 1:
                            v.wait_ge(stq, 16 * 16 * tc)
                        if phase == 1:
                            v.wait_ge(stkr, 128 * (tc + 1))
                        t1 = v.tensor_tensor(t1b[:, mt * 512:(mt + 1) * 512],
                                             ps[:, bank * 512:(bank + 1) * 512],
                                             ss[:, (buf + mt) * 512:(buf + mt + 1) * 512], ALU.mult)
                        vch_n[0] += 1
                        t1.then_inc(vch, 1)
                        v.wait_ge(vch, vch_n[0])
                        v.sem_inc(rp1, 1)
                        chv(v.tensor_tensor(krb[:, mt * 512:(mt + 1) * 512],
                                            kqb[:, mt * 512:(mt + 1) * 512],
                                            cs[:, (buf + mt) * 512:(buf + mt + 1) * 512], ALU.mult))
                        v.tensor_tensor(krb[:, mt * 512:(mt + 1) * 512],
                                        krb[:, mt * 512:(mt + 1) * 512],
                                        t1b[:, mt * 512:(mt + 1) * 512], ALU.add).then_inc(vkr, 1)
                        sgrp += 1
                    if phase == 0:
                        for tb in range(4):
                            g = tc * 4 + tb
                            ring = (tc % 2) * 4 + tb
                            v.wait_ge(mmv, 2 * g + 2)
                            if tc >= 2:
                                v.wait_ge(mmkv, 4 * (tc - 1))
                            chv(v.tensor_tensor(vtm[:, ring * 1024:ring * 1024 + 512],
                                                ps[:, 6 * 512:7 * 512], bvb[:, 0:512], ALU.add))
                            v.tensor_tensor(vtm[:, ring * 1024 + 512:ring * 1024 + 1024],
                                            ps[:, 7 * 512:8 * 512], bvb[:, 512:1024],
                                            ALU.add).then_inc(vv, 1)
                        for tb in range(4):
                            g = tc * 4 + tb
                            v.wait_ge(mmkv, g + 1)
                            chv(v.tensor_tensor(kvo[:, 0:512], kvo[:, 0:512], ps[:, 4 * 512:5 * 512], ALU.add))
                            v.tensor_tensor(kvo[:, 512:1024], kvo[:, 512:1024], ps[:, 5 * 512:6 * 512],
                                            ALU.add).then_inc(kvev, 1)
                            v.wait_ge(kvev, g + 1)
            v.wait_ge(evv, 1)
            v.tensor_copy(kvo[:, 1024:1032], ksum[:, 0:8]).then_inc(evv, 1)
    return nc


def build_la_b(nc=None, io=None):
    if nc is None:
        nc = bass.Bass("TRN2", target_bir_lowering=False, debug=False)
    DT = _mk_dt(nc, io)
    kvp4 = DT("kvp4", [4, 128, 1032], F32, "ExternalInput")
    qd = DT("q", [D, T], BF16, "ExternalInput")
    qrd = DT("qr", [D, T], BF16, "ExternalInput")
    hd = DT("h", [D, T], BF16, "ExternalInput")
    lnag = DT("lnag", [128, 8], F32, "ExternalInput")
    lnab = DT("lnab", [128, 8], F32, "ExternalInput")
    sel2d = DT("sel2", [2, 128], BF16, "ExternalInput")
    hn = DT("hn", [D, T], BF16, "ExternalOutput")

    NCHK = 4
    with contextlib.ExitStack() as ctx:
        E = ctx.enter_context
        E(nc.allow_low_precision(reason="bf16 pipeline; tolerance measured for bf16 matmuls"))
        qs = E(nc.sbuf_tensor("qs", [128, 8 * T], BF16))
        qrs = E(nc.sbuf_tensor("qrs", [128, 8 * T], BF16))
        hs = E(nc.sbuf_tensor("hs", [128, 8 * T], BF16))
        rsb = E(nc.sbuf_tensor("rsb", [128, 8 * T], BF16))
        kv4 = E(nc.sbuf_tensor("kv4", [128, 4 * 1032], F32))
        kvs = E(nc.sbuf_tensor("kvs", [128, 1032], F32))
        kvb = E(nc.sbuf_tensor("kvb", [128, 1024], BF16))
        kmb = E(nc.sbuf_tensor("kmb", [128, 8], BF16))
        zlh = E(nc.sbuf_tensor("zlh", [128, 16], BF16))
        sel2s = E(nc.sbuf_tensor("sel2s", [2, 128], BF16))
        zrow = E(nc.sbuf_tensor("zrow", [2, 512], F32))
        zrow2 = E(nc.sbuf_tensor("zrow2", [2, 512], BF16))
        zbs = E(nc.sbuf_tensor("zbs", [128, 2 * 512], BF16))
        o1s = E(nc.sbuf_tensor("o1s", [128, 512], BF16))
        sqb = E(nc.sbuf_tensor("sqb", [128, 2 * T], BF16))
        gs = E(nc.sbuf_tensor("gs", [128, 8], F32))
        bs = E(nc.sbuf_tensor("bs", [128, 8], F32))
        onesb = E(nc.sbuf_tensor("onesb", [128, 1], BF16))
        onesrb = E(nc.sbuf_tensor("onesrb", [1, 128], BF16))
        stat = E(nc.sbuf_tensor("stat", [1, 5 * T], BF16))
        mbs = E(nc.sbuf_tensor("mbs", [128, T], BF16))
        rbs = E(nc.sbuf_tensor("rbs", [128, T], BF16))
        ps = E(nc.psum_tensor("lbps", [128, 8 * 512], F32))
        ini = E(nc.semaphore("ini"))
        lkv = E(nc.semaphore("lkv"))
        lq = E(nc.semaphore("lq"))
        lqr = E(nc.semaphore("lqr"))
        lhm = E(nc.semaphore("lhm"))
        ldc = E(nc.semaphore("ldc"))
        vkv = E(nc.semaphore("vkv"))
        szl = E(nc.semaphore("szl"))
        mo = E(nc.semaphore("mo"))
        mz = E(nc.semaphore("mz"))
        vz = E(nc.semaphore("vz"))
        mzb = E(nc.semaphore("mzb"))
        ezb = E(nc.semaphore("ezb"))
        vr = E(nc.semaphore("vr"))
        sqs = E(nc.semaphore("sqs"))
        mms = E(nc.semaphore("mms"))
        mmd = E(nc.semaphore("mmd"))
        std = E(nc.semaphore("std"))
        bcd = E(nc.semaphore("bcd"))
        apd = E(nc.semaphore("apd"))
        vch = E(nc.semaphore("vch"))
        st = E(nc.semaphore("st"))
        block = E(nc.Block())
        vch_n = [0]

        def chv_of(v):
            def chv(inst):
                vch_n[0] += 1
                inst.then_inc(vch, 1)
                v.wait_ge(vch, vch_n[0])
            return chv

        @block.gpsimd
        def _(g):
            g.memset(onesb[:], 1.0)
            g.memset(onesrb[:], 1.0)
            g.memset(zlh[:], 0.0).then_inc(ini, 1)

        @block.sync
        def _(sync):
            for i in range(4):
                sync.dma_start(out=kv4[:, i * 1032:(i + 1) * 1032], in_=kvp4[i]).then_inc(lkv, 16)
            for kt in range(8):
                sync.dma_start(out=qs[:, kt * T:(kt + 1) * T], in_=qd[kt * 128:(kt + 1) * 128, :]).then_inc(lq, 16)
                sync.dma_start(out=qrs[:, kt * T:(kt + 1) * T], in_=qrd[kt * 128:(kt + 1) * 128, :]).then_inc(lqr, 16)
                sync.dma_start(out=hs[:, kt * T:(kt + 1) * T], in_=hd[kt * 128:(kt + 1) * 128, :]).then_inc(lhm, 16)
            sync.dma_start(out=gs[:], in_=lnag).then_inc(ldc, 16)
            sync.dma_start(out=bs[:], in_=lnab).then_inc(ldc, 16)
            sync.dma_start(out=sel2s[:], in_=sel2d).then_inc(ldc, 16)
            for hp in range(8):
                sync.wait_ge(apd, hp + 1)
                sync.dma_start(out=hn[hp * 128:(hp + 1) * 128, :], in_=rsb[:, hp * T:(hp + 1) * T]).then_inc(st, 16)
            sync.wait_ge(st, 16 * 8)

        @block.vector
        def _(v):
            chv = chv_of(v)
            v.wait_ge(lkv, 16 * 4)
            chv(v.tensor_tensor(kvs[:], kv4[:, 0:1032], kv4[:, 1032:2064], ALU.add))
            chv(v.tensor_tensor(kvs[:], kvs[:], kv4[:, 2064:3096], ALU.add))
            chv(v.tensor_tensor(kvs[:], kvs[:], kv4[:, 3096:4128], ALU.add))
            chv(v.tensor_scalar(kvb[:], kvs[:, 0:1024], 1.0 / S, None, ALU.mult))
            chv(v.tensor_scalar(kmb[:], kvs[:, 1024:1032], 1.0 / S, None, ALU.mult))
            for hp in range(8):
                chv(v.memset(kvb[0:64, hp * 128 + 64:hp * 128 + 128], 0.0))
                chv(v.memset(kvb[64:128, hp * 128:hp * 128 + 64], 0.0))
            v.sem_inc(vkv, 1)
            v.wait_ge(lhm, 16 * 8)
            for hp in range(8):
                for tc in range(NCHK):
                    g = hp * NCHK + tc
                    bank = g % 2
                    v.wait_ge(mz, g + 1)
                    chv(v.tensor_scalar(zrow[0:2, :], ps[0:2, 6 * 512:7 * 512], 1e-6, None, ALU.add))
                    v.reciprocal(zrow2[0:2, :], zrow[0:2, :]).then_inc(vz, 1)
                    v.wait_ge(vz, g + 1)
                    v.wait_ge(mo, g + 1)
                    v.wait_ge(ezb, g + 1)
                    chv(v.tensor_tensor(o1s[:], ps[:, bank * 512:(bank + 1) * 512],
                                        zbs[:, (g % 2) * 512:(g % 2 + 1) * 512], ALU.mult))
                    v.tensor_tensor(rsb[:, hp * T + tc * 512:hp * T + (tc + 1) * 512], o1s[:],
                                    hs[:, hp * T + tc * 512:hp * T + (tc + 1) * 512],
                                    ALU.add).then_inc(vr, 1)
                    v.wait_ge(vr, g + 1)
            v.wait_ge(mmd, 1)
            for qh in range(4):
                sm = ps[0:1, qh * 512:(qh + 1) * 512]
                sq_ = ps[0:1, (4 + qh) * 512:(5 + qh) * 512]
                m_ = stat[0:1, qh * 512:(qh + 1) * 512]
                msq = stat[0:1, T + qh * 512:T + (qh + 1) * 512]
                var = stat[0:1, 2 * T + qh * 512:2 * T + (qh + 1) * 512]
                chv(v.tensor_scalar(m_, sm, 1.0 / 1024, None, ALU.mult))
                chv(v.tensor_scalar(msq, sq_, 1.0 / 1024, None, ALU.mult))
                chv(v.tensor_tensor(var, m_, m_, ALU.mult))
                chv(v.tensor_tensor(var, msq, var, ALU.subtract))
                chv(v.tensor_scalar(var, var, 1e-12, None, ALU.add))
            v.sem_inc(std, 1)
            v.wait_ge(std, 2)
            for qh in range(4):
                chv(v.reciprocal(stat[0:1, 3 * T + qh * 512:3 * T + (qh + 1) * 512],
                                 stat[0:1, 4 * T + qh * 512:4 * T + (qh + 1) * 512]))
            v.sem_inc(std, 1)
            v.wait_ge(bcd, 1)
            for hp in range(8):
                r_ = rsb[:, hp * T:(hp + 1) * T]
                chv(v.tensor_tensor(r_, r_, mbs[:], ALU.subtract))
                chv(v.tensor_tensor(r_, r_, rbs[:], ALU.mult))
                v.tensor_scalar(r_, r_, gs[:, hp:hp + 1], bs[:, hp:hp + 1],
                                ALU.mult, ALU.add).then_inc(apd, 1)
                v.wait_ge(apd, hp + 1)

        @block.scalar
        def _(s):
            s.wait_ge(ldc, 16 * 3)
            s.wait_ge(ini, 1)
            s.wait_ge(vkv, 1)
            for hp in range(8):
                s.copy(zlh[0:64, 2 * hp:2 * hp + 1], kmb[0:64, hp:hp + 1]).then_inc(szl, 1)
                s.copy(zlh[64:128, 2 * hp + 1:2 * hp + 2], kmb[64:128, hp:hp + 1]).then_inc(szl, 1)
            for hp in range(8):
                for tc in range(NCHK):
                    g = hp * NCHK + tc
                    s.wait_ge(mzb, g + 1)
                    if g >= 2:
                        s.wait_ge(vr, g - 1)
                    s.copy(zbs[:, (g % 2) * 512:(g % 2 + 1) * 512], ps[:, 7 * 512:8 * 512]).then_inc(ezb, 1)
            for hp in range(8):
                s.wait_ge(vr, NCHK * (hp + 1))
                if hp >= 2:
                    s.wait_ge(mms, hp - 1)
                s.activation(out=sqb[:, (hp % 2) * T:(hp % 2 + 1) * T],
                             in_=rsb[:, hp * T:(hp + 1) * T], func=AF.Square).then_inc(sqs, 1)
            s.wait_ge(std, 1)
            for qh in range(4):
                inst = s.sqrt(stat[0:1, 4 * T + qh * 512:4 * T + (qh + 1) * 512],
                              stat[0:1, 2 * T + qh * 512:2 * T + (qh + 1) * 512])
            inst.then_inc(std, 1)
            s.wait_ge(mmd, 2)
            for qh in range(4):
                s.copy(mbs[:, qh * 512:(qh + 1) * 512], ps[:, qh * 512:(qh + 1) * 512])
                inst = s.copy(rbs[:, qh * 512:(qh + 1) * 512], ps[:, (4 + qh) * 512:(5 + qh) * 512])
            inst.then_inc(bcd, 1)

        @block.tensor
        def _(t):
            t.wait_ge(ini, 1)
            t.wait_ge(lq, 16 * 8)
            t.wait_ge(lqr, 16 * 8)
            t.wait_ge(ldc, 16 * 3)
            t.wait_ge(vkv, 1)
            t.wait_ge(szl, 16)
            for hp in range(8):
                for tc in range(NCHK):
                    g = hp * NCHK + tc
                    bank = g % 2
                    if g >= 2:
                        t.wait_ge(vr, g - 1)
                    t.matmul(ps[:, bank * 512:(bank + 1) * 512], kvb[:, hp * 128:(hp + 1) * 128],
                             qrs[:, hp * T + tc * 512:hp * T + (tc + 1) * 512],
                             start=True, stop=True).then_inc(mo, 1)
                    if g >= 1:
                        t.wait_ge(vz, g)
                    t.matmul(ps[0:2, 6 * 512:7 * 512], zlh[:, 2 * hp:2 * hp + 2],
                             qs[:, hp * T + tc * 512:hp * T + (tc + 1) * 512],
                             start=True, stop=True).then_inc(mz, 1)
                    t.wait_ge(vz, g + 1)
                    if g >= 1:
                        t.wait_ge(ezb, g)
                    t.matmul(ps[:, 7 * 512:8 * 512], sel2s[:], zrow2[0:2, :],
                             start=True, stop=True).then_inc(mzb, 1)
            t.wait_ge(vr, 32)
            for hp in range(8):
                for qh in range(4):
                    t.matmul(ps[0:1, qh * 512:(qh + 1) * 512], onesb[:, 0:1],
                             rsb[:, hp * T + qh * 512:hp * T + (qh + 1) * 512],
                             start=(hp == 0), stop=(hp == 7))
                t.wait_ge(sqs, hp + 1)
                last = None
                for qh in range(4):
                    last = t.matmul(ps[0:1, (4 + qh) * 512:(5 + qh) * 512], onesb[:, 0:1],
                                    sqb[:, (hp % 2) * T + qh * 512:(hp % 2) * T + (qh + 1) * 512],
                                    start=(hp == 0), stop=(hp == 7))
                if hp < 7:
                    last.then_inc(mms, 1)
                else:
                    last.then_inc(mmd, 1)
            t.wait_ge(std, 3)
            last = None
            for qh in range(4):
                t.matmul(ps[:, qh * 512:(qh + 1) * 512], onesrb[0:1, :],
                         stat[0:1, qh * 512:(qh + 1) * 512], start=True, stop=True)
                last = t.matmul(ps[:, (4 + qh) * 512:(5 + qh) * 512], onesrb[0:1, :],
                                stat[0:1, 3 * T + qh * 512:3 * T + (qh + 1) * 512], start=True, stop=True)
            last.then_inc(mmd, 1)
    return nc


def build_ffn0(nc=None, io=None):
    if nc is None:
        nc = bass.Bass("TRN2", target_bir_lowering=False, debug=False)
    DT = _mk_dt(nc, io)
    hd = DT("h", [D, T], BF16, "ExternalInput")
    w1d = DT("w1", [D, 4096], F32, "ExternalInput")
    b1d = DT("b1c", [128, 32], F32, "ExternalInput")
    w2d = DT("w2", [4096, D], F32, "ExternalInput")
    b2d = DT("b2c", [128, 8], F32, "ExternalInput")
    lngd = DT("lng", [128, 8], F32, "ExternalInput")
    lnbd = DT("lnb", [128, 8], F32, "ExternalInput")
    so = DT("s0", [D, T], BF16, "ExternalOutput")
    dt1 = DT("dt1", [4096, T], BF16, "Internal")

    NCHK = 4
    with contextlib.ExitStack() as ctx:
        E = ctx.enter_context
        E(nc.allow_low_precision(reason="bf16 pipeline; tolerance measured for bf16 matmuls"))
        hs = E(nc.sbuf_tensor("hs", [128, 8 * T], BF16))
        wbuf = E(nc.sbuf_tensor("wbuf", [128, 32 * 1024], BF16))
        t1s = E(nc.sbuf_tensor("t1s", [128, 32 * 512], BF16))
        rsb = E(nc.sbuf_tensor("rsb", [128, 8 * T], BF16))
        sqb = E(nc.sbuf_tensor("sqb", [128, 2 * T], BF16))
        b1s = E(nc.sbuf_tensor("b1s", [128, 32], F32))
        b2s = E(nc.sbuf_tensor("b2s", [128, 8], F32))
        gs = E(nc.sbuf_tensor("gs", [128, 8], F32))
        bs = E(nc.sbuf_tensor("bs", [128, 8], F32))
        onesb = E(nc.sbuf_tensor("onesb", [128, 1], BF16))
        onesrb = E(nc.sbuf_tensor("onesrb", [1, 128], BF16))
        stat = E(nc.sbuf_tensor("stat", [1, 5 * T], BF16))
        mbs = E(nc.sbuf_tensor("mbs", [128, T], BF16))
        rbs = E(nc.sbuf_tensor("rbs", [128, T], BF16))
        ps = E(nc.psum_tensor("ffps", [128, 8 * 512], F32))
        ini = E(nc.semaphore("ini"))
        lhm = E(nc.semaphore("lhm"))
        ldw1 = E(nc.semaphore("ldw1"))
        ldw2 = E(nc.semaphore("ldw2"))
        ldc = E(nc.semaphore("ldc"))
        lt1 = E(nc.semaphore("lt1"))
        mm1 = E(nc.semaphore("mm1"))
        ev1 = E(nc.semaphore("ev1"))
        st8 = [E(nc.semaphore(f"st8_{i}")) for i in range(8)]
        mm2 = E(nc.semaphore("mm2"))
        vr = E(nc.semaphore("vr"))
        sqs = E(nc.semaphore("sqs"))
        mms = E(nc.semaphore("mms"))
        mmd = E(nc.semaphore("mmd"))
        std = E(nc.semaphore("std"))
        bcd = E(nc.semaphore("bcd"))
        apd = E(nc.semaphore("apd"))
        vch = E(nc.semaphore("vch"))
        st = E(nc.semaphore("st"))
        block = E(nc.Block())
        vch_n = [0]

        @block.gpsimd
        def _(g):
            g.memset(onesb[:], 1.0)
            g.memset(onesrb[:], 1.0).then_inc(ini, 1)
            for kt in range(8):
                g.dma_start(out=wbuf[:, kt * 4096:(kt + 1) * 4096],
                            in_=w1d[kt * 128:(kt + 1) * 128, :]).then_inc(ldw1, 16)
            g.wait_ge(mm1, 128)
            for kt in range(32):
                g.dma_start(out=wbuf[:, kt * D:(kt + 1) * D],
                            in_=w2d[kt * 128:(kt + 1) * 128, :]).then_inc(ldw2, 16)

        @block.sync
        def _(sync):
            for kt in range(8):
                sync.dma_start(out=hs[:, kt * T:(kt + 1) * T], in_=hd[kt * 128:(kt + 1) * 128, :]).then_inc(lhm, 16)
            for nm, dst in ((b1d, b1s), (b2d, b2s), (lngd, gs), (lnbd, bs)):
                sync.dma_start(out=dst[:], in_=nm).then_inc(ldc, 16)
            for m in range(32):
                for tc in range(NCHK):
                    g1 = m * 4 + tc
                    sync.wait_ge(ev1, g1 + 1)
                    sync.dma_start(out=dt1[m * 128:(m + 1) * 128, tc * 512:(tc + 1) * 512],
                                   in_=t1s[:, ((m % 8) * 4 + tc) * 512:((m % 8) * 4 + tc + 1) * 512]).then_inc(st8[m % 8], 16)
            for i in range(8):
                sync.wait_ge(st8[i], 16 * 16)
            for tc in range(NCHK):
                if tc >= 1:
                    sync.wait_ge(mm2, 8 * tc)
                for kt in range(32):
                    sync.dma_start(out=t1s[:, kt * 512:(kt + 1) * 512],
                                   in_=dt1[kt * 128:(kt + 1) * 128, tc * 512:(tc + 1) * 512]).then_inc(lt1, 16)
            for hp in range(8):
                sync.wait_ge(apd, hp + 1)
                sync.dma_start(out=so[hp * 128:(hp + 1) * 128, :], in_=rsb[:, hp * T:(hp + 1) * T]).then_inc(st, 16)
            sync.wait_ge(st, 16 * 8)
            sync.wait_ge(lt1, 16 * 128)

        @block.tensor
        def _(t):
            t.wait_ge(ini, 1)
            t.wait_ge(lhm, 16 * 8)
            t.wait_ge(ldw1, 16 * 8)
            t.wait_ge(ldc, 16 * 4)
            for m in range(32):
                grp = m % 2
                if m >= 2:
                    t.wait_ge(ev1, 4 * (m - 1))
                for kt in range(8):
                    for tc in range(NCHK):
                        bank = grp * 4 + tc
                        last = t.matmul(ps[:, bank * 512:(bank + 1) * 512],
                                        wbuf[:, kt * 4096 + m * 128:kt * 4096 + (m + 1) * 128],
                                        hs[:, kt * T + tc * 512:kt * T + (tc + 1) * 512],
                                        start=(kt == 0), stop=(kt == 7))
                        if kt == 7:
                            last.then_inc(mm1, 1)
            t.wait_ge(ldw2, 16 * 32)
            for tc in range(NCHK):
                t.wait_ge(lt1, 16 * 32 * (tc + 1))
                for m2 in range(8):
                    g2 = tc * 8 + m2
                    bank = 2 + g2 % 2
                    if g2 >= 2:
                        t.wait_ge(vr, g2 - 1)
                    last = None
                    for kt in range(32):
                        last = t.matmul(ps[:, bank * 512:(bank + 1) * 512],
                                        wbuf[:, kt * D + m2 * 128:kt * D + (m2 + 1) * 128],
                                        t1s[:, kt * 512:(kt + 1) * 512],
                                        start=(kt == 0), stop=(kt == 31))
                    last.then_inc(mm2, 1)
            t.wait_ge(vr, 32)
            for hp in range(8):
                for qh in range(4):
                    t.matmul(ps[0:1, qh * 512:(qh + 1) * 512], onesb[:, 0:1],
                             rsb[:, hp * T + qh * 512:hp * T + (qh + 1) * 512],
                             start=(hp == 0), stop=(hp == 7))
                t.wait_ge(sqs, hp + 1)
                last = None
                for qh in range(4):
                    last = t.matmul(ps[0:1, (4 + qh) * 512:(5 + qh) * 512], onesb[:, 0:1],
                                    sqb[:, (hp % 2) * T + qh * 512:(hp % 2) * T + (qh + 1) * 512],
                                    start=(hp == 0), stop=(hp == 7))
                if hp < 7:
                    last.then_inc(mms, 1)
                else:
                    last.then_inc(mmd, 1)
            t.wait_ge(std, 3)
            last = None
            for qh in range(4):
                t.matmul(ps[:, qh * 512:(qh + 1) * 512], onesrb[0:1, :],
                         stat[0:1, qh * 512:(qh + 1) * 512], start=True, stop=True)
                last = t.matmul(ps[:, (4 + qh) * 512:(5 + qh) * 512], onesrb[0:1, :],
                                stat[0:1, 3 * T + qh * 512:3 * T + (qh + 1) * 512], start=True, stop=True)
            last.then_inc(mmd, 1)

        @block.scalar
        def _(s):
            s.wait_ge(ldc, 16 * 4)
            for m in range(32):
                grp = m % 2
                for tc in range(NCHK):
                    g1 = m * 4 + tc
                    bank = grp * 4 + tc
                    s.wait_ge(mm1, g1 + 1)
                    if m >= 8:
                        s.wait_ge(st8[m % 8], 64 * (m // 8))
                    s.activation(out=t1s[:, ((m % 8) * 4 + tc) * 512:((m % 8) * 4 + tc + 1) * 512],
                                 in_=ps[:, bank * 512:(bank + 1) * 512],
                                 func=AF.Gelu, bias=b1s[:, m:m + 1]).then_inc(ev1, 1)
            for hp in range(8):
                s.wait_ge(vr, 24 + hp + 1)
                if hp >= 2:
                    s.wait_ge(mms, hp - 1)
                s.activation(out=sqb[:, (hp % 2) * T:(hp % 2 + 1) * T],
                             in_=rsb[:, hp * T:(hp + 1) * T], func=AF.Square).then_inc(sqs, 1)
            s.wait_ge(std, 1)
            inst = None
            for qh in range(4):
                inst = s.sqrt(stat[0:1, 4 * T + qh * 512:4 * T + (qh + 1) * 512],
                              stat[0:1, 2 * T + qh * 512:2 * T + (qh + 1) * 512])
            inst.then_inc(std, 1)
            s.wait_ge(mmd, 2)
            inst = None
            for qh in range(4):
                s.copy(mbs[:, qh * 512:(qh + 1) * 512], ps[:, qh * 512:(qh + 1) * 512])
                inst = s.copy(rbs[:, qh * 512:(qh + 1) * 512], ps[:, (4 + qh) * 512:(5 + qh) * 512])
            inst.then_inc(bcd, 1)

        @block.vector
        def _(v):
            def chv(inst):
                vch_n[0] += 1
                inst.then_inc(vch, 1)
                v.wait_ge(vch, vch_n[0])
            v.wait_ge(ldc, 16 * 4)
            v.wait_ge(lhm, 16 * 8)
            for tc in range(NCHK):
                for m2 in range(8):
                    g2 = tc * 8 + m2
                    bank = 2 + g2 % 2
                    v.wait_ge(mm2, g2 + 1)
                    v.scalar_tensor_tensor(rsb[:, m2 * T + tc * 512:m2 * T + (tc + 1) * 512],
                                           ps[:, bank * 512:(bank + 1) * 512], b2s[:, m2:m2 + 1],
                                           hs[:, m2 * T + tc * 512:m2 * T + (tc + 1) * 512],
                                           ALU.add, ALU.add).then_inc(vr, 1)
                    v.wait_ge(vr, g2 + 1)
            v.wait_ge(mmd, 1)
            for qh in range(4):
                sm = ps[0:1, qh * 512:(qh + 1) * 512]
                sq_ = ps[0:1, (4 + qh) * 512:(5 + qh) * 512]
                m_ = stat[0:1, qh * 512:(qh + 1) * 512]
                msq = stat[0:1, T + qh * 512:T + (qh + 1) * 512]
                var = stat[0:1, 2 * T + qh * 512:2 * T + (qh + 1) * 512]
                chv(v.tensor_scalar(m_, sm, 1.0 / 1024, None, ALU.mult))
                chv(v.tensor_scalar(msq, sq_, 1.0 / 1024, None, ALU.mult))
                chv(v.tensor_tensor(var, m_, m_, ALU.mult))
                chv(v.tensor_tensor(var, msq, var, ALU.subtract))
                chv(v.tensor_scalar(var, var, 1e-12, None, ALU.add))
            v.sem_inc(std, 1)
            v.wait_ge(std, 2)
            for qh in range(4):
                chv(v.reciprocal(stat[0:1, 3 * T + qh * 512:3 * T + (qh + 1) * 512],
                                 stat[0:1, 4 * T + qh * 512:4 * T + (qh + 1) * 512]))
            v.sem_inc(std, 1)
            v.wait_ge(bcd, 1)
            for hp in range(8):
                r_ = rsb[:, hp * T:(hp + 1) * T]
                chv(v.tensor_tensor(r_, r_, mbs[:], ALU.subtract))
                chv(v.tensor_tensor(r_, r_, rbs[:], ALU.mult))
                v.tensor_scalar(r_, r_, gs[:, hp:hp + 1], bs[:, hp:hp + 1],
                                ALU.mult, ALU.add).then_inc(apd, 1)
                v.wait_ge(apd, hp + 1)
    return nc


def build_comb0(nc=None, io=None):
    if nc is None:
        nc = bass.Bass("TRN2", target_bir_lowering=False, debug=False)
    DT = _mk_dt(nc, io)
    xd = DT("x", [2, CHH, D], F32, "ExternalInput")
    s0d = DT("s0", [D, T], BF16, "ExternalInput")
    g0d = DT("g0", [D, T], BF16, "ExternalInput")
    wkvd = DT("wkv", [D, 128], F32, "ExternalInput")
    wqd = DT("wq", [D, D], F32, "ExternalInput")
    x1d = DT("x1", [T, D], F32, "ExternalOutput")
    xn1d = DT("xn1", [T, D], BF16, "ExternalOutput")
    latd = DT("lat", [T, 128], BF16, "ExternalOutput")
    qfmd = DT("qfm", [D, T], BF16, "ExternalOutput")

    with contextlib.ExitStack() as ctx:
        E = ctx.enter_context
        E(nc.allow_low_precision(reason="bf16 pipeline; tolerance measured for bf16 matmuls"))
        xt = E(nc.sbuf_tensor("xt", [128, 2 * D], F32))
        sgt = E(nc.sbuf_tensor("sgt", [128, 2 * 2 * D], BF16))
        x1t = E(nc.sbuf_tensor("x1t", [128, 2 * D], F32))
        ot = E(nc.sbuf_tensor("ot", [128, 2 * D], BF16))
        scr = E(nc.sbuf_tensor("scr", [128, D], F32))
        scr2 = E(nc.sbuf_tensor("scr2", [128, D], F32))
        mc = E(nc.sbuf_tensor("mc", [128, 4], F32))
        rc = E(nc.sbuf_tensor("rc", [128, 4], F32))
        xnf = E(nc.sbuf_tensor("xnf", [128, 8 * T], BF16))
        wkvs = E(nc.sbuf_tensor("wkvs", [128, 8 * 128], BF16))
        wqs = E(nc.sbuf_tensor("wqs", [128, 8 * D], BF16))
        obl = E(nc.sbuf_tensor("obl", [128, 4 * 128], BF16))
        ob = E(nc.sbuf_tensor("ob", [128, 4 * 512], BF16))
        ps = E(nc.psum_tensor("cbps", [128, 8 * 512], F32))
        ldx = E(nc.semaphore("ldx"))
        ldxo = E(nc.semaphore("ldxo"))
        ldsg = E(nc.semaphore("ldsg"))
        ldsgo = E(nc.semaphore("ldsgo"))
        ldw = E(nc.semaphore("ldw"))
        lxf = E(nc.semaphore("lxf"))
        xav = E(nc.semaphore("xav"))
        vs = E(nc.semaphore("vs"))
        sv = E(nc.semaphore("sv"))
        mml = E(nc.semaphore("mml"))
        evl = E(nc.semaphore("evl"))
        vch = E(nc.semaphore("vch"))
        st = E(nc.semaphore("st"))
        sto = E(nc.semaphore("sto"))
        stq = E(nc.semaphore("stq"))
        block = E(nc.Block())
        vch_n = [0]
        NTT = 16

        @block.gpsimd
        def _(g):
            for kt in range(8):
                g.dma_start(out=wkvs[:, kt * 128:(kt + 1) * 128],
                            in_=wkvd[kt * 128:(kt + 1) * 128, :]).then_inc(ldw, 16)
                g.dma_start(out=wqs[:, kt * D:(kt + 1) * D],
                            in_=wqd[kt * 128:(kt + 1) * 128, :]).then_inc(ldw, 16)

        @block.sync
        def _(sync):
            def tile_loads(j):
                ch, tt = divmod(j, 8)
                sem = ldx if j % 2 == 0 else ldxo
                semg = ldsg if j % 2 == 0 else ldsgo
                sync.dma_start(out=xt[:, (j % 2) * D:(j % 2 + 1) * D],
                               in_=xd[ch, HALO + tt * 128:HALO + (tt + 1) * 128, :]).then_inc(sem, 16)
                sync.dma_start_transpose(sgt[:, (j % 2) * 2 * D:(j % 2) * 2 * D + D],
                                         s0d[:, j * 128:(j + 1) * 128]).then_inc(semg, 16)
                sync.dma_start_transpose(sgt[:, (j % 2) * 2 * D + D:(j % 2 + 1) * 2 * D],
                                         g0d[:, j * 128:(j + 1) * 128]).then_inc(semg, 16)
            for i in range(min(2, NTT)):
                tile_loads(i)
            for i in range(NTT):
                buf = i % 2
                sync.wait_ge(vs, 2 * i + 2)
                ssem = st if i % 2 == 0 else sto
                sync.dma_start(out=x1d[i * 128:(i + 1) * 128, :], in_=x1t[:, buf * D:(buf + 1) * D]).then_inc(ssem, 16)
                sync.dma_start(out=xn1d[i * 128:(i + 1) * 128, :], in_=ot[:, buf * D:(buf + 1) * D]).then_inc(ssem, 16)
                if i + 2 < NTT:
                    tile_loads(i + 2)
            sync.wait_ge(st, 16 * NTT)
            sync.wait_ge(sto, 16 * NTT)
            for kt in range(8):
                sync.dma_start_transpose(xnf[:, kt * T:(kt + 1) * T],
                                         xn1d[:, kt * 128:(kt + 1) * 128]).then_inc(lxf, 16)
            for tb in range(16):
                sync.wait_ge(evl, tb + 1)
                sync.dma_start(out=latd[tb * 128:(tb + 1) * 128, :],
                               in_=obl[:, (tb % 4) * 128:(tb % 4 + 1) * 128]).then_inc(stq, 16)
            for g2 in range(32):
                mt, tc = divmod(g2, 4)
                sync.wait_ge(evl, 16 + g2 + 1)
                sync.dma_start(out=qfmd[mt * 128:(mt + 1) * 128, tc * 512:(tc + 1) * 512],
                               in_=ob[:, (g2 % 4) * 512:(g2 % 4 + 1) * 512]).then_inc(stq, 16)
            sync.wait_ge(stq, 16 * 48)
            sync.wait_ge(ldw, 16 * 16)

        @block.vector
        def _(v):
            def chv(inst):
                vch_n[0] += 1
                inst.then_inc(vch, 1)
                v.wait_ge(vch, vch_n[0])
            for i in range(NTT):
                buf = i % 2
                v.wait_ge(ldx if i % 2 == 0 else ldxo, 16 * (i // 2 + 1))
                v.wait_ge(ldsg if i % 2 == 0 else ldsgo, 32 * (i // 2 + 1))
                xsrc = xt[:, buf * D:(buf + 1) * D]
                x1dst = x1t[:, buf * D:(buf + 1) * D]
                s_ = sgt[:, buf * 2 * D:buf * 2 * D + D]
                g_ = sgt[:, buf * 2 * D + D:(buf + 1) * 2 * D]
                chv(v.tensor_tensor(scr[:], s_, g_, ALU.mult))
                if i >= 2:
                    v.wait_ge(st if i % 2 == 0 else sto, 32 * (i // 2))
                v.tensor_tensor(x1dst, xsrc, scr[:], ALU.add).then_inc(xav, 1)
                v.wait_ge(xav, i + 1)
                chv(v.reduce_sum(mc[:, 0:1], x1dst, axis=AX.X))
                chv(v.tensor_scalar(mc[:, 1:2], mc[:, 0:1], 1.0 / D, None, ALU.mult))
                v.wait_ge(sv, 2 * i + 1)
                chv(v.tensor_scalar(mc[:, 3:4], mc[:, 2:3], 1.0 / D, None, ALU.mult))
                chv(v.tensor_tensor(rc[:, 0:1], mc[:, 1:2], mc[:, 1:2], ALU.mult))
                chv(v.tensor_tensor(rc[:, 1:2], mc[:, 3:4], rc[:, 0:1], ALU.subtract))
                chv(v.tensor_scalar(rc[:, 1:2], rc[:, 1:2], 1e-5, None, ALU.add))
                v.sem_inc(vs, 1)
                v.wait_ge(sv, 2 * i + 2)
                chv(v.reciprocal(rc[:, 3:4], rc[:, 2:3]))
                chv(v.tensor_tensor(mc[:, 0:1], mc[:, 1:2], rc[:, 3:4], ALU.mult))
                chv(v.tensor_scalar(mc[:, 0:1], mc[:, 0:1], -1.0, None, ALU.mult))
                v.tensor_scalar(ot[:, buf * D:(buf + 1) * D], x1dst, rc[:, 3:4],
                                mc[:, 0:1], ALU.mult, ALU.add).then_inc(vs, 1)

        @block.scalar
        def _(s):
            for i in range(NTT):
                buf = i % 2
                s.wait_ge(xav, i + 1)
                s.activation(out=scr2[:], in_=x1t[:, buf * D:(buf + 1) * D], func=AF.Square,
                             accum_out=mc[:, 2:3]).then_inc(sv, 1)
                s.wait_ge(vs, 2 * i + 1)
                s.sqrt(rc[:, 2:3], rc[:, 1:2]).then_inc(sv, 1)
                s.wait_ge(sv, 2 * i + 2)
            for tb in range(16):
                s.wait_ge(mml, tb + 1)
                if tb >= 4:
                    s.wait_ge(stq, 16 * tb)
                s.copy(obl[:, (tb % 4) * 128:(tb % 4 + 1) * 128],
                       ps[:, (tb % 4) * 512:(tb % 4) * 512 + 128]).then_inc(evl, 1)
            for g2 in range(32):
                s.wait_ge(mml, 16 + g2 + 1)
                if g2 >= 4:
                    s.wait_ge(stq, 16 * (16 + g2))
                s.copy(ob[:, (g2 % 4) * 512:(g2 % 4 + 1) * 512],
                       ps[:, (4 + g2 % 4) * 512:(5 + g2 % 4) * 512]).then_inc(evl, 1)

        @block.tensor
        def _(t):
            t.wait_ge(lxf, 16 * 8)
            t.wait_ge(ldw, 16 * 16)
            for tb in range(16):
                bank = tb % 4
                if tb >= 4:
                    t.wait_ge(evl, tb - 3)
                last = None
                for kt in range(8):
                    last = t.matmul(ps[:, bank * 512:bank * 512 + 128],
                                    xnf[:, kt * T + tb * 128:kt * T + (tb + 1) * 128],
                                    wkvs[:, kt * 128:(kt + 1) * 128],
                                    start=(kt == 0), stop=(kt == 7))
                last.then_inc(mml, 1)
            for g2 in range(32):
                mt, tc = divmod(g2, 4)
                bank = 4 + g2 % 4
                if g2 >= 4:
                    t.wait_ge(evl, 16 + g2 - 3)
                last = None
                for kt in range(8):
                    last = t.matmul(ps[:, bank * 512:(bank + 1) * 512],
                                    wqs[:, kt * D + mt * 128:kt * D + (mt + 1) * 128],
                                    xnf[:, kt * T + tc * 512:kt * T + (tc + 1) * 512],
                                    start=(kt == 0), stop=(kt == 7))
                last.then_inc(mml, 1)
    return nc


def build_l1ffn_a(nc=None, io=None):
    if nc is None:
        nc = bass.Bass("TRN2", target_bir_lowering=False, debug=False)
    DT = _mk_dt(nc, io)
    xnd = DT("xn1", [T, D], BF16, "ExternalInput")
    w1d = DT("w1t", [D, 4096], F32, "ExternalInput")
    w2d = DT("w2t", [D, 4096], F32, "ExternalInput")
    s1d = DT("s1c", [128, 32], F32, "ExternalInput")
    s2d = DT("s2c", [128, 32], F32, "ExternalInput")
    t1d = DT("t1", [4096, T], BF16, "ExternalOutput")

    with contextlib.ExitStack() as ctx:
        E = ctx.enter_context
        E(nc.allow_low_precision(reason="bf16 pipeline; tolerance measured for bf16 matmuls"))
        xnf = E(nc.sbuf_tensor("xnf", [128, 8 * T], BF16))
        w1s = E(nc.sbuf_tensor("w1s", [128, 8 * 4096], BF16))
        w2s = E(nc.sbuf_tensor("w2s", [128, 8 * 4096], BF16))
        s1s = E(nc.sbuf_tensor("s1s", [128, 32], F32))
        s2s = E(nc.sbuf_tensor("s2s", [128, 32], F32))
        sg = E(nc.sbuf_tensor("sg", [128, 8 * 512], BF16))
        oc = E(nc.sbuf_tensor("oc", [128, 4 * 512], BF16))
        ps = E(nc.psum_tensor("faps", [128, 8 * 512], F32))
        lx = E(nc.semaphore("lx"))
        ldw = E(nc.semaphore("ldw"))
        ldc = E(nc.semaphore("ldc"))
        mmg = E(nc.semaphore("mmg"))
        mmu = E(nc.semaphore("mmu"))
        evg = E(nc.semaphore("evg"))
        vt = E(nc.semaphore("vt"))
        st4 = [E(nc.semaphore(f"st{i}")) for i in range(4)]
        block = E(nc.Block())

        @block.gpsimd
        def _(g):
            for kt in range(8):
                g.dma_start(out=w1s[:, kt * 4096:(kt + 1) * 4096],
                            in_=w1d[kt * 128:(kt + 1) * 128, :]).then_inc(ldw, 16)
                g.dma_start(out=w2s[:, kt * 4096:(kt + 1) * 4096],
                            in_=w2d[kt * 128:(kt + 1) * 128, :]).then_inc(ldw, 16)

        @block.sync
        def _(sync):
            for kt in range(8):
                sync.dma_start_transpose(xnf[:, kt * T:(kt + 1) * T],
                                         xnd[:, kt * 128:(kt + 1) * 128]).then_inc(lx, 16)
            sync.dma_start(out=s1s[:], in_=s1d).then_inc(ldc, 16)
            sync.dma_start(out=s2s[:], in_=s2d).then_inc(ldc, 16)
            for g2 in range(128):
                m, tc = divmod(g2, 4)
                sync.wait_ge(vt, g2 + 1)
                sync.dma_start(out=t1d[m * 128:(m + 1) * 128, tc * 512:(tc + 1) * 512],
                               in_=oc[:, (g2 % 4) * 512:(g2 % 4 + 1) * 512]).then_inc(st4[g2 % 4], 16)
            for i in range(4):
                sync.wait_ge(st4[i], 16 * 32)

        @block.tensor
        def _(t):
            t.wait_ge(lx, 16 * 8)
            t.wait_ge(ldw, 16 * 16)
            t.wait_ge(ldc, 16 * 2)
            for m in range(32):
                if m >= 1:
                    t.wait_ge(evg, 4 * m)
                for kt in range(8):
                    for tc in range(4):
                        last = t.matmul(ps[:, tc * 512:(tc + 1) * 512],
                                        w1s[:, kt * 4096 + m * 128:kt * 4096 + (m + 1) * 128],
                                        xnf[:, kt * T + tc * 512:kt * T + (tc + 1) * 512],
                                        start=(kt == 0), stop=(kt == 7))
                        if kt == 7:
                            last.then_inc(mmg, 1)
                if m >= 1:
                    t.wait_ge(vt, 4 * m)
                for kt in range(8):
                    for tc in range(4):
                        last = t.matmul(ps[:, (4 + tc) * 512:(4 + tc + 1) * 512],
                                        w2s[:, kt * 4096 + m * 128:kt * 4096 + (m + 1) * 128],
                                        xnf[:, kt * T + tc * 512:kt * T + (tc + 1) * 512],
                                        start=(kt == 0), stop=(kt == 7))
                        if kt == 7:
                            last.then_inc(mmu, 1)

        @block.scalar
        def _(s):
            s.wait_ge(ldc, 16 * 2)
            for g2 in range(128):
                m, tc = divmod(g2, 4)
                s.wait_ge(mmg, g2 + 1)
                if m >= 2:
                    s.wait_ge(vt, 4 * (m - 1))
                s.activation(out=sg[:, ((m % 2) * 4 + tc) * 512:((m % 2) * 4 + tc + 1) * 512],
                             in_=ps[:, tc * 512:(tc + 1) * 512],
                             func=AF.Silu, scale=s1s[:, m:m + 1]).then_inc(evg, 1)

        @block.vector
        def _(v):
            for g2 in range(128):
                m, tc = divmod(g2, 4)
                v.wait_ge(mmu, g2 + 1)
                v.wait_ge(evg, g2 + 1)
                if g2 >= 4:
                    v.wait_ge(st4[g2 % 4], 16 * (g2 // 4))
                v.scalar_tensor_tensor(oc[:, (g2 % 4) * 512:(g2 % 4 + 1) * 512],
                                       ps[:, (4 + tc) * 512:(4 + tc + 1) * 512], s2s[:, m:m + 1],
                                       sg[:, ((m % 2) * 4 + tc) * 512:((m % 2) * 4 + tc + 1) * 512],
                                       ALU.mult, ALU.mult).then_inc(vt, 1)
                v.wait_ge(vt, g2 + 1)
    return nc


def build_l1ffn_h(nc=None, io=None):
    if nc is None:
        nc = bass.Bass("TRN2", target_bir_lowering=False, debug=False)
    DT = _mk_dt(nc, io)
    tind = DT("tin", [4096, T], BF16, "ExternalInput")
    wd = DT("wt", [4096, 4096], F32, "ExternalInput")
    scd = DT("sc", [128, 32], F32, "ExternalInput")
    tod = DT("tout", [4096, T], BF16, "ExternalOutput")

    KT, MT, MC, NMC = 32, 32, 512, 8
    with contextlib.ExitStack() as ctx:
        E = ctx.enter_context
        E(nc.allow_low_precision(reason="bf16 pipeline; tolerance measured for bf16 matmuls"))
        tins = E(nc.sbuf_tensor("tins", [128, KT * T], BF16))
        ws = E(nc.sbuf_tensor("ws", [128, 2 * KT * MC], BF16))
        scs = E(nc.sbuf_tensor("scs", [128, 32], F32))
        oc = E(nc.sbuf_tensor("oc", [128, 4 * 512], BF16))
        ps = E(nc.psum_tensor("fhps", [128, 8 * 512], F32))
        lx = E(nc.semaphore("lx"))
        ldwe = E(nc.semaphore("ldwe"))
        ldwo = E(nc.semaphore("ldwo"))
        ldc = E(nc.semaphore("ldc"))
        wfree = E(nc.semaphore("wfree"))
        mm = E(nc.semaphore("mm"))
        ev = E(nc.semaphore("ev"))
        st4 = [E(nc.semaphore(f"st{i}")) for i in range(4)]
        block = E(nc.Block())

        @block.sync
        def _(sync):
            for kt in range(KT):
                sync.dma_start(out=tins[:, kt * T:(kt + 1) * T],
                               in_=tind[kt * 128:(kt + 1) * 128, :]).then_inc(lx, 16)
            sync.dma_start(out=scs[:], in_=scd).then_inc(ldc, 16)
            for g2 in range(128):
                mt, tc = divmod(g2, 4)
                sync.wait_ge(ev, g2 + 1)
                sync.dma_start(out=tod[mt * 128:(mt + 1) * 128, tc * 512:(tc + 1) * 512],
                               in_=oc[:, (g2 % 4) * 512:(g2 % 4 + 1) * 512]).then_inc(st4[g2 % 4], 16)
            for i in range(4):
                sync.wait_ge(st4[i], 16 * 32)
            sync.wait_ge(ldwe, 16 * KT * NMC // 2)
            sync.wait_ge(ldwo, 16 * KT * NMC // 2)

        @block.gpsimd
        def _(g):
            for c in range(NMC):
                buf = c % 2
                if c >= 2:
                    g.wait_ge(ev, 16 * (c - 1))
                for kt in range(KT):
                    g.dma_start(out=ws[:, (buf * KT + kt) * MC:(buf * KT + kt + 1) * MC],
                                in_=wd[kt * 128:(kt + 1) * 128, c * MC:(c + 1) * MC]).then_inc(ldwe if c % 2 == 0 else ldwo, 16)

        @block.tensor
        def _(t):
            t.wait_ge(lx, 16 * KT)
            t.wait_ge(ldc, 16)
            for c in range(NMC):
                buf = c % 2
                t.wait_ge(ldwe if c % 2 == 0 else ldwo, 16 * KT * (c // 2 + 1))
                for j in range(4):
                    mt = c * 4 + j
                    grp = (mt % 2) * 4
                    if mt >= 2:
                        t.wait_ge(ev, 4 * (mt - 1))
                    for kt in range(KT):
                        for tc in range(4):
                            last = t.matmul(ps[:, (grp + tc) * 512:(grp + tc + 1) * 512],
                                            ws[:, (buf * KT + kt) * MC + j * 128:(buf * KT + kt) * MC + (j + 1) * 128],
                                            tins[:, kt * T + tc * 512:kt * T + (tc + 1) * 512],
                                            start=(kt == 0), stop=(kt == KT - 1))
                            if kt == KT - 1:
                                last.then_inc(mm, 1)

        @block.scalar
        def _(s):
            s.wait_ge(ldc, 16)
            for g2 in range(128):
                mt, tc = divmod(g2, 4)
                bank = (mt % 2) * 4 + tc
                s.wait_ge(mm, g2 + 1)
                if g2 >= 4:
                    s.wait_ge(st4[g2 % 4], 16 * (g2 // 4))
                s.activation(out=oc[:, (g2 % 4) * 512:(g2 % 4 + 1) * 512],
                             in_=ps[:, bank * 512:(bank + 1) * 512],
                             func=AF.Gelu, scale=scs[:, mt:mt + 1]).then_inc(ev, 1)
    return nc


def build_l1ffn_c(nc=None, io=None):
    if nc is None:
        nc = bass.Bass("TRN2", target_bir_lowering=False, debug=False)
    DT = _mk_dt(nc, io)
    t3d = DT("t3", [4096, T], BF16, "ExternalInput")
    w3d = DT("w3t", [4096, D], F32, "ExternalInput")
    s3d = DT("s3c", [128, 8], F32, "ExternalInput")
    x1d = DT("x1", [T, D], F32, "ExternalInput")
    s1d = DT("s1f", [D, T], BF16, "ExternalInput")
    yd = DT("y", [T, D], F32, "ExternalOutput")
    dg1 = DT("dg1", [D, T], BF16, "Internal")

    KT = 32
    with contextlib.ExitStack() as ctx:
        E = ctx.enter_context
        E(nc.allow_low_precision(reason="bf16 pipeline; tolerance measured for bf16 matmuls"))
        t3s = E(nc.sbuf_tensor("t3s", [128, KT * T], BF16))
        w3s = E(nc.sbuf_tensor("w3s", [128, KT * 512], BF16))
        s3s = E(nc.sbuf_tensor("s3s", [128, 8], F32))
        oc = E(nc.sbuf_tensor("oc", [128, 4 * 512], BF16))
        x1t = E(nc.sbuf_tensor("x1t", [128, 2 * D], F32))
        sgt = E(nc.sbuf_tensor("sgt", [128, 2 * 2 * D], BF16))
        yt = E(nc.sbuf_tensor("yt", [128, 2 * D], F32))
        scr = E(nc.sbuf_tensor("scr", [128, D], F32))
        ps = E(nc.psum_tensor("fcps", [128, 8 * 512], F32))
        lx = E(nc.semaphore("lx"))
        ldw = E(nc.semaphore("ldw"))
        ldc = E(nc.semaphore("ldc"))
        mm = E(nc.semaphore("mm"))
        ev = E(nc.semaphore("ev"))
        stg4 = [E(nc.semaphore(f"stg{i}")) for i in range(4)]
        lxe = E(nc.semaphore("lxe"))
        lxo = E(nc.semaphore("lxo"))
        vt = E(nc.semaphore("vt"))
        vch = E(nc.semaphore("vch"))
        st = E(nc.semaphore("st"))
        sto = E(nc.semaphore("sto"))
        block = E(nc.Block())
        vch_n = [0]

        @block.gpsimd
        def _(g):
            for half in range(2):
                if half == 1:
                    g.wait_ge(mm, 16)
                for kt in range(KT):
                    g.dma_start(out=w3s[:, kt * 512:(kt + 1) * 512],
                                in_=w3d[kt * 128:(kt + 1) * 128, half * 512:(half + 1) * 512]).then_inc(ldw, 16)

        @block.sync
        def _(sync):
            for kt in range(KT):
                sync.dma_start(out=t3s[:, kt * T:(kt + 1) * T],
                               in_=t3d[kt * 128:(kt + 1) * 128, :]).then_inc(lx, 16)
            sync.dma_start(out=s3s[:], in_=s3d).then_inc(ldc, 16)
            for g2 in range(32):
                mt, tc = divmod(g2, 4)
                sync.wait_ge(ev, g2 + 1)
                sync.dma_start(out=dg1[mt * 128:(mt + 1) * 128, tc * 512:(tc + 1) * 512],
                               in_=oc[:, (g2 % 4) * 512:(g2 % 4 + 1) * 512]).then_inc(stg4[g2 % 4], 16)
            for i in range(4):
                sync.wait_ge(stg4[i], 16 * 8)
            def tile_loads(j):
                sem = lxe if j % 2 == 0 else lxo
                sync.dma_start(out=x1t[:, (j % 2) * D:(j % 2 + 1) * D],
                               in_=x1d[j * 128:(j + 1) * 128, :]).then_inc(sem, 16)
                sync.dma_start_transpose(sgt[:, (j % 2) * 2 * D:(j % 2) * 2 * D + D],
                                         s1d[:, j * 128:(j + 1) * 128]).then_inc(sem, 16)
                sync.dma_start_transpose(sgt[:, (j % 2) * 2 * D + D:(j % 2 + 1) * 2 * D],
                                         dg1[:, j * 128:(j + 1) * 128]).then_inc(sem, 16)
            for j in range(2):
                tile_loads(j)
            for i in range(16):
                buf = i % 2
                sync.wait_ge(vt, i + 1)
                sync.dma_start(out=yd[i * 128:(i + 1) * 128, :],
                               in_=yt[:, buf * D:(buf + 1) * D]).then_inc(st if i % 2 == 0 else sto, 16)
                if i + 2 < 16:
                    tile_loads(i + 2)
            sync.wait_ge(st, 16 * 8)
            sync.wait_ge(sto, 16 * 8)
            sync.wait_ge(ldw, 16 * KT * 2)

        @block.tensor
        def _(t):
            t.wait_ge(lx, 16 * KT)
            t.wait_ge(ldc, 16)
            for g2 in range(32):
                mt, tc = divmod(g2, 4)
                half, mloc = divmod(mt, 4)
                t.wait_ge(ldw, 16 * KT * (half + 1))
                bank = g2 % 4
                if g2 >= 4:
                    t.wait_ge(ev, g2 - 3)
                last = None
                for kt in range(KT):
                    last = t.matmul(ps[:, bank * 512:(bank + 1) * 512],
                                    w3s[:, kt * 512 + mloc * 128:kt * 512 + (mloc + 1) * 128],
                                    t3s[:, kt * T + tc * 512:kt * T + (tc + 1) * 512],
                                    start=(kt == 0), stop=(kt == KT - 1))
                last.then_inc(mm, 1)

        @block.scalar
        def _(s):
            s.wait_ge(ldc, 16)
            for g2 in range(32):
                mt, tc = divmod(g2, 4)
                bank = g2 % 4
                s.wait_ge(mm, g2 + 1)
                if g2 >= 4:
                    s.wait_ge(stg4[g2 % 4], 16 * (g2 // 4))
                s.activation(out=oc[:, (g2 % 4) * 512:(g2 % 4 + 1) * 512],
                             in_=ps[:, bank * 512:(bank + 1) * 512],
                             func=AF.Silu, scale=s3s[:, mt:mt + 1]).then_inc(ev, 1)

        @block.vector
        def _(v):
            def chv(inst):
                vch_n[0] += 1
                inst.then_inc(vch, 1)
                v.wait_ge(vch, vch_n[0])
            for i in range(16):
                buf = i % 2
                v.wait_ge(lxe if i % 2 == 0 else lxo, 48 * (i // 2 + 1))
                s_ = sgt[:, buf * 2 * D:buf * 2 * D + D]
                g_ = sgt[:, buf * 2 * D + D:(buf + 1) * 2 * D]
                chv(v.tensor_tensor(scr[:], s_, g_, ALU.mult))
                if i >= 2:
                    v.wait_ge(st if i % 2 == 0 else sto, 16 * (i // 2))
                v.tensor_tensor(yt[:, buf * D:(buf + 1) * D], x1t[:, buf * D:(buf + 1) * D],
                                scr[:], ALU.add).then_inc(vt, 1)
                v.wait_ge(vt, i + 1)
    return nc


def host_pm(w, ktiles):
    K_, M_ = w.shape
    assert K_ == ktiles * 128
    return np.ascontiguousarray(w.reshape(ktiles, 128, M_).transpose(1, 0, 2).reshape(128, ktiles * M_)).astype(np.float32)


def host_sel8():
    m = np.zeros((8, 8 * 128), np.float32)
    for e in range(8):
        m[e, e * 128:(e + 1) * 128] = 1.0
    return m.astype(BF)


def build_moe(nc=None, io=None):
    if nc is None:
        nc = bass.Bass("TRN2", target_bir_lowering=False, debug=False)
    DT = _mk_dt(nc, io)
    xnd = DT("xn0", [2, CHH, D], BF16, "ExternalInput")
    gwd = DT("gw", [8, D], F32, "ExternalInput")
    cbd = DT("cbr", [128, 8], F32, "ExternalInput")
    idd = DT("identd", [128, 128], F32, "ExternalInput")
    sel8d = DT("sel8", [8, 1024], BF16, "ExternalInput")
    wgp = DT("wgp", [9, 128, 8 * 512], F32, "ExternalInput")
    wup = DT("wup", [9, 128, 8 * 512], F32, "ExternalInput")
    wdp = DT("wdp", [9, 128, 4 * 1024], F32, "ExternalInput")
    g0d = DT("g0", [D, T], BF16, "ExternalOutput")
    dte = DT("dte", [9, 512, T], BF16, "Internal")

    with contextlib.ExitStack() as ctx:
        E = ctx.enter_context
        E(nc.allow_low_precision(reason="bf16 pipeline; tolerance measured for bf16 matmuls"))
        xnf = E(nc.sbuf_tensor("xnf", [128, 8 * T], BF16))
        wgs = E(nc.sbuf_tensor("wgs", [128, 2 * 4096], BF16))
        wus = E(nc.sbuf_tensor("wus", [128, 2 * 4096], BF16))
        wds = E(nc.sbuf_tensor("wds", [128, 2 * 4096], BF16))
        teb = E(nc.sbuf_tensor("teb", [128, 9 * 4 * 512], BF16))
        tsc = E(nc.sbuf_tensor("tsc", [128, 2 * 4 * 512], BF16))
        sgb = E(nc.sbuf_tensor("sgb", [128, 2 * 512], BF16))
        teA = E(nc.sbuf_tensor("teA", [128, 2 * 512], BF16))
        gws = E(nc.sbuf_tensor("gws", [8, D], F32))
        gwt = E(nc.sbuf_tensor("gwt", [128, 8 * 8], BF16))
        ids = E(nc.sbuf_tensor("ids", [128, 128], F32))
        sel8s = E(nc.sbuf_tensor("sel8s", [8, 1024], BF16))
        cbb = E(nc.sbuf_tensor("cbb", [128, 8], F32))
        lgf = E(nc.sbuf_tensor("lgf", [8, T], F32))
        sgm = E(nc.sbuf_tensor("sgm", [128, 128], F32))
        scp = E(nc.sbuf_tensor("scp", [128, 128], F32))
        gsm = E(nc.sbuf_tensor("gsm", [128, 64], F32))
        gtm = E(nc.sbuf_tensor("gtm", [128, 64], F32))
        t2b = E(nc.sbuf_tensor("t2b", [128, 16], F32))
        t2c = E(nc.sbuf_tensor("t2c", [128, 32], F32))
        gmk = E(nc.sbuf_tensor("gmk", [128, 64], F32))
        mek = E(nc.sbuf_tensor("mek", [128, 128], F32))
        vbuf = E(nc.sbuf_tensor("vbuf", [128, 128], F32))
        selp = E(nc.sbuf_tensor("selp", [128, 128], F32))
        cntb = E(nc.sbuf_tensor("cntb", [128, 16], F32))
        dbuf = E(nc.sbuf_tensor("dbuf", [128, 16], F32))
        cmb = E(nc.sbuf_tensor("cmb", [128, 128], F32))
        rpb = E(nc.sbuf_tensor("rpb", [128, 128], F32))
        flb = E(nc.sbuf_tensor("flb", [128, 128], F32))
        wbv = E(nc.sbuf_tensor("wbv", [128, 128], F32))
        denb = E(nc.sbuf_tensor("denb", [128, 16], F32))
        cwt = E(nc.sbuf_tensor("cwt", [128, 128], F32))
        cwf = E(nc.sbuf_tensor("cwf", [8, T], BF16))
        cws = E(nc.sbuf_tensor("cws", [128, 32 * 512], BF16))
        g0t = E(nc.sbuf_tensor("g0t", [128, 4 * 512], BF16))
        ps = E(nc.psum_tensor("moeps", [128, 8 * 512], F32))
        lxn = E(nc.semaphore("lxn"))
        lc = E(nc.semaphore("lc"))
        lwe = E(nc.semaphore("lwe"))
        lwo = E(nc.semaphore("lwo"))
        lwd = [E(nc.semaphore(f"lwd{i}")) for i in range(2)]
        ltb = E(nc.semaphore("ltb"))
        mmA = E(nc.semaphore("mmA"))
        evg = E(nc.semaphore("evg"))
        vte = E(nc.semaphore("vte"))
        s4 = [E(nc.semaphore(f"sp{i}")) for i in range(2)]
        mmR = E(nc.semaphore("mmR"))
        esg = E(nc.semaphore("esg"))
        vrt = E(nc.semaphore("vrt"))
        ecw = E(nc.semaphore("ecw"))
        mmB = E(nc.semaphore("mmB"))
        mmBe = E(nc.semaphore("mmBe"))
        vsc = E(nc.semaphore("vsc"))
        evB = E(nc.semaphore("evB"))
        vch = E(nc.semaphore("vch"))
        st = E(nc.semaphore("st"))
        st4m = [E(nc.semaphore(f"stm{i}")) for i in range(4)]
        block = E(nc.Block())
        vch_n = [0]

        def chv_of(v):
            def chv(inst):
                vch_n[0] += 1
                inst.then_inc(vch, 1)
                v.wait_ge(vch, vch_n[0])
            return chv

        @block.gpsimd
        def _(g):
            for e in range(9):
                sem = lwe if e % 2 == 0 else lwo
                if e >= 2:
                    g.wait_ge(mmA, 32 * (e - 1))
                g.dma_start(out=wgs[:, (e % 2) * 4096:(e % 2 + 1) * 4096], in_=wgp[e]).then_inc(sem, 16)
                g.dma_start(out=wus[:, (e % 2) * 4096:(e % 2 + 1) * 4096], in_=wup[e]).then_inc(sem, 16)
            for tc in range(4):
                for e in range(9):
                    le = tc * 9 + e
                    if le >= 2:
                        g.wait_ge(mmBe, le - 1)
                    g.dma_start(out=wds[:, (le % 2) * 4096:(le % 2 + 1) * 4096],
                                in_=wdp[e]).then_inc(lwd[le % 2], 16)

        @block.sync
        def _(sync):
            for ch in range(2):
                for ct in range(8):
                    sync.dma_start_transpose(xnf[:, ct * T + ch * 1024:ct * T + (ch + 1) * 1024],
                                             xnd[ch, HALO:HALO + 1024, ct * 128:(ct + 1) * 128]).then_inc(lxn, 16)
            sync.dma_start(out=gws[:], in_=gwd).then_inc(lc, 16)
            sync.dma_start(out=cbb[:], in_=cbd).then_inc(lc, 16)
            sync.dma_start(out=ids[:], in_=idd).then_inc(lc, 16)
            sync.dma_start(out=sel8s[:], in_=sel8d).then_inc(lc, 16)
            pc = 0
            for e in range(9):
                for it in range(4):
                    for tc in range(4):
                        sync.wait_ge(vte, pc + 1)
                        sync.dma_start(out=dte[e, it * 128:(it + 1) * 128, tc * 512:(tc + 1) * 512],
                                       in_=teA[:, (pc % 2) * 512:(pc % 2 + 1) * 512]).then_inc(s4[pc % 2], 16)
                        pc += 1
            def pb_loads(tc):
                if tc == 0:
                    for i in range(2):
                        sync.wait_ge(s4[i], 16 * 72)
                else:
                    sync.wait_ge(mmBe, 9 * tc)
                for e in range(9):
                    for kt in range(4):
                        sync.dma_start(out=teb[:, (e * 4 + kt) * 512:(e * 4 + kt + 1) * 512],
                                       in_=dte[e, kt * 128:(kt + 1) * 128, tc * 512:(tc + 1) * 512]).then_inc(ltb, 16)
            def pb_stores(tc):
                for m in range(8):
                    sync.wait_ge(evB, tc * 8 + m + 1)
                    sync.dma_start(out=g0d[m * 128:(m + 1) * 128, tc * 512:(tc + 1) * 512],
                                   in_=g0t[:, (m % 4) * 512:(m % 4 + 1) * 512]).then_inc(st4m[m % 4], 16)
            pb_loads(0)
            pb_loads(1)
            pb_stores(0)
            pb_loads(2)
            pb_stores(1)
            pb_loads(3)
            pb_stores(2)
            pb_stores(3)
            for i in range(4):
                sync.wait_ge(st4m[i], 16 * 8)
            sync.wait_ge(lwd[0], 16 * 18)
            sync.wait_ge(lwd[1], 16 * 18)

        @block.tensor
        def _(t):
            t.wait_ge(lxn, 16 * 16)
            t.wait_ge(lc, 16 * 4)
            for kt in range(8):
                if kt >= 1:
                    t.wait_ge(esg, kt)
                t.transpose(ps[:, 7 * 512:7 * 512 + 8], gws[0:8, kt * 128:(kt + 1) * 128], ids[0:8, 0:8]).then_inc(mmR, 1)
            t.wait_ge(esg, 8)
            for tc in range(4):
                if tc >= 1:
                    t.wait_ge(esg, 8 + tc)
                last = None
                for kt in range(8):
                    last = t.matmul(ps[0:8, 4 * 512:4 * 512 + 512], gwt[:, kt * 8:(kt + 1) * 8],
                                    xnf[:, kt * T + tc * 512:kt * T + (tc + 1) * 512],
                                    start=(kt == 0), stop=(kt == 7))
                last.then_inc(mmR, 1)
            t.wait_ge(esg, 12)
            for tt in range(16):
                if tt >= 1:
                    t.wait_ge(esg, 12 + tt)
                t.transpose(ps[:, 7 * 512:7 * 512 + 8], lgf[0:8, tt * 128:(tt + 1) * 128], ids[0:8, 0:8]).then_inc(mmR, 1)
            ga = 0
            for e in range(9):
                t.wait_ge(lwe if e % 2 == 0 else lwo, 32 * (e // 2 + 1))
                for it in range(4):
                    ei = e * 4 + it
                    if ei >= 1:
                        t.wait_ge(evg, 4 * ei)
                    for kt in range(8):
                        for tc in range(4):
                            last = t.matmul(ps[:, tc * 512:(tc + 1) * 512],
                                            wgs[:, (e % 2) * 4096 + kt * 512 + it * 128:(e % 2) * 4096 + kt * 512 + (it + 1) * 128],
                                            xnf[:, kt * T + tc * 512:kt * T + (tc + 1) * 512],
                                            start=(kt == 0), stop=(kt == 7))
                            if kt == 7:
                                last.then_inc(mmA, 1)
                    if ei == 0:
                        t.wait_ge(esg, 28)
                    else:
                        t.wait_ge(vte, 4 * ei)
                    for kt in range(8):
                        for tc in range(4):
                            last = t.matmul(ps[:, (4 + tc) * 512:(4 + tc + 1) * 512],
                                            wus[:, (e % 2) * 4096 + kt * 512 + it * 128:(e % 2) * 4096 + kt * 512 + (it + 1) * 128],
                                            xnf[:, kt * T + tc * 512:kt * T + (tc + 1) * 512],
                                            start=(kt == 0), stop=(kt == 7))
                            if kt == 7:
                                last.then_inc(mmA, 1)
            t.wait_ge(vrt, 1)
            t.wait_ge(vte, 144)
            for tt in range(16):
                if tt >= 1:
                    t.wait_ge(ecw, tt)
                t.transpose(ps[0:8, 7 * 512:7 * 512 + 128], cwt[:, tt * 8:(tt + 1) * 8], ids[:]).then_inc(mmR, 1)
            t.wait_ge(ecw, 16)
            for e in range(8):
                for tc in range(4):
                    i2 = e * 4 + tc
                    if i2 >= 1:
                        t.wait_ge(ecw, 16 + i2)
                    t.matmul(ps[:, 7 * 512:8 * 512], sel8s[0:8, e * 128:(e + 1) * 128],
                             cwf[0:8, tc * 512:(tc + 1) * 512], start=True, stop=True).then_inc(mmR, 1)
            t.wait_ge(ecw, 48)
            for tc in range(4):
                t.wait_ge(ltb, 16 * 36 * (tc + 1))
                if tc >= 1:
                    t.wait_ge(evB, 8 * tc)
                for e in range(9):
                    le = tc * 9 + e
                    t.wait_ge(lwd[le % 2], 16 * (le // 2 + 1))
                    if e < 8:
                        t.wait_ge(vsc, tc * 8 + e + 1)
                        rhsbuf = tsc
                        rbase = (e % 2) * 4 * 512
                    else:
                        rhsbuf = teb
                        rbase = 8 * 4 * 512
                    last = None
                    for m in range(8):
                        for kt in range(4):
                            last = t.matmul(ps[:, m * 512:(m + 1) * 512],
                                            wds[:, (le % 2) * 4096 + kt * 1024 + m * 128:(le % 2) * 4096 + kt * 1024 + (m + 1) * 128],
                                            rhsbuf[:, rbase + kt * 512:rbase + (kt + 1) * 512],
                                            start=(e == 0 and kt == 0), stop=(e == 8 and kt == 3))
                    last.then_inc(mmBe, 1)

        @block.scalar
        def _(s):
            s.wait_ge(lc, 16 * 4)
            for kt in range(8):
                s.wait_ge(mmR, kt + 1)
                s.copy(gwt[:, kt * 8:(kt + 1) * 8], ps[:, 7 * 512:7 * 512 + 8]).then_inc(esg, 1)
            for tc in range(4):
                s.wait_ge(mmR, 8 + tc + 1)
                s.copy(lgf[0:8, tc * 512:(tc + 1) * 512], ps[0:8, 4 * 512:4 * 512 + 512]).then_inc(esg, 1)
            for tt in range(16):
                s.wait_ge(mmR, 12 + tt + 1)
                s.activation(out=sgm[:, tt * 8:(tt + 1) * 8], in_=ps[:, 7 * 512:7 * 512 + 8],
                             func=AF.Sigmoid).then_inc(esg, 1)
            ga = 0
            for e in range(9):
                for it in range(4):
                    ei = e * 4 + it
                    for tc in range(4):
                        s.wait_ge(mmA, ei * 8 + tc + 1)
                        if ga >= 2:
                            s.wait_ge(vte, ga - 1)
                        s.activation(out=sgb[:, (ga % 2) * 512:(ga % 2 + 1) * 512],
                                     in_=ps[:, tc * 512:(tc + 1) * 512], func=AF.Silu).then_inc(evg, 1)
                        ga += 1
            for tt in range(16):
                s.wait_ge(mmR, 28 + tt + 1)
                s.copy(cwf[0:8, tt * 128:(tt + 1) * 128], ps[0:8, 7 * 512:7 * 512 + 128]).then_inc(ecw, 1)
            for i2 in range(32):
                s.wait_ge(mmR, 44 + i2 + 1)
                s.copy(cws[:, i2 * 512:(i2 + 1) * 512], ps[:, 7 * 512:8 * 512]).then_inc(ecw, 1)
            for tc in range(4):
                for m in range(8):
                    s.wait_ge(mmBe, tc * 9 + 9)
                    nprev = tc * 2 + m // 4
                    if nprev >= 1:
                        s.wait_ge(st4m[m % 4], 16 * nprev)
                    s.activation(out=g0t[:, (m % 4) * 512:(m % 4 + 1) * 512],
                                 in_=ps[:, m * 512:(m + 1) * 512], func=AF.Silu).then_inc(evB, 1)

        @block.vector
        def _(v):
            chv = chv_of(v)
            v.wait_ge(lc, 16 * 4)
            pass
            ga = 0
            for e in range(9):
                for it in range(4):
                    ei = e * 4 + it
                    for tc in range(4):
                        v.wait_ge(mmA, ei * 8 + 4 + tc + 1)
                        v.wait_ge(evg, ga + 1)
                        if ga >= 2:
                            v.wait_ge(s4[ga % 2], 16 * (ga // 2))
                        v.tensor_tensor(teA[:, (ga % 2) * 512:(ga % 2 + 1) * 512],
                                        sgb[:, (ga % 2) * 512:(ga % 2 + 1) * 512],
                                        ps[:, (4 + tc) * 512:(4 + tc + 1) * 512], ALU.mult).then_inc(vte, 1)
                        v.wait_ge(vte, ga + 1)
                        ga += 1
            v.wait_ge(esg, 28)
            sc3i0 = scp[:].rearrange("p (t e) -> p t e", e=8)
            sg3 = sgm[:].rearrange("p (t e) -> p t e", e=8)
            g3 = gsm[:].rearrange("p (t j) -> p t j", j=4)
            gt3 = gtm[:].rearrange("p (t j) -> p t j", j=4)
            gm3 = gmk[:].rearrange("p (t j) -> p t j", j=4)
            me3 = mek[:].rearrange("p (t e) -> p t e", e=8)
            v3 = vbuf[:].rearrange("p (t e) -> p t e", e=8)
            sp3 = selp[:].rearrange("p (t e) -> p t e", e=8)
            cm3 = cmb[:].rearrange("p (t e) -> p t e", e=8)
            r3 = rpb[:].rearrange("p (t e) -> p t e", e=8)
            fl3 = flb[:].rearrange("p (t e) -> p t e", e=8)
            w3 = wbv[:].rearrange("p (t e) -> p t e", e=8)
            cw3 = cwt[:].rearrange("p (t e) -> p t e", e=8)
            for tt in range(16):
                chv(v.tensor_tensor(scp[:, tt * 8:(tt + 1) * 8], sgm[:, tt * 8:(tt + 1) * 8], cbb[:], ALU.add))
            chv(v.tensor_tensor(g3[:, :, :], sc3i0[:, :, 0:8:2], sc3i0[:, :, 1:8:2], ALU.add))
            chv(v.tensor_tensor(gt3[:, :, 0:4:2], g3[:, :, 0:4:2], g3[:, :, 1:4:2], ALU.max))
            chv(v.tensor_tensor(gt3[:, :, 1:4:2], g3[:, :, 0:4:2], g3[:, :, 1:4:2], ALU.min))
            chv(v.tensor_tensor(t2c[:, 0:16], gt3[:, :, 0], gt3[:, :, 2], ALU.min))
            chv(v.tensor_tensor(t2c[:, 16:32], gt3[:, :, 1], gt3[:, :, 3], ALU.max))
            chv(v.tensor_tensor(t2b[:], t2c[:, 0:16], t2c[:, 16:32], ALU.max))
            for j in range(4):
                chv(v.tensor_tensor(gm3[:, :, j], g3[:, :, j], t2b[:], ALU.is_ge))
            for e in range(8):
                chv(v.tensor_copy(me3[:, :, e], gm3[:, :, e // 2]))
            chv(v.tensor_tensor(vbuf[:], scp[:], mek[:], ALU.mult))
            chv(v.tensor_scalar(selp[:], vbuf[:], 0.0, None, ALU.is_gt))
            chv(v.reduce_sum(cntb[:], sp3[:, :, :], axis=AX.X))
            chv(v.tensor_scalar(dbuf[:], cntb[:], -1.0, 4.0, ALU.mult, ALU.add))
            chv(v.tensor_scalar(cmb[:], mek[:], -1.0, 1.0, ALU.mult, ALU.add))
            chv(v.tensor_copy(r3[:, :, 0], cm3[:, :, 0]))
            for e in range(1, 8):
                chv(v.tensor_tensor(r3[:, :, e], r3[:, :, e - 1], cm3[:, :, e], ALU.add))
            for e in range(8):
                chv(v.tensor_tensor(fl3[:, :, e], r3[:, :, e], dbuf[:], ALU.is_le))
            chv(v.tensor_tensor(flb[:], flb[:], cmb[:], ALU.mult))
            chv(v.tensor_tensor(selp[:], selp[:], flb[:], ALU.add))
            chv(v.tensor_tensor(wbv[:], sgm[:], selp[:], ALU.mult))
            chv(v.reduce_sum(denb[:], w3[:, :, :], axis=AX.X))
            chv(v.tensor_scalar(denb[:], denb[:], 1e-20, None, ALU.add))
            chv(v.reciprocal(denb[:], denb[:]))
            for e in range(8):
                chv(v.tensor_tensor(cw3[:, :, e], w3[:, :, e], denb[:], ALU.mult))
            v.tensor_scalar(cwt[:], cwt[:], 2.5, None, ALU.mult).then_inc(vrt, 1)
            for tc in range(4):
                for e in range(8):
                    i2 = tc * 8 + e
                    v.wait_ge(ltb, 16 * 36 * (tc + 1))
                    v.wait_ge(ecw, 16 + e * 4 + tc + 1)
                    if i2 >= 2:
                        v.wait_ge(mmBe, 9 * tc + e - 1)
                    last = None
                    for kt in range(4):
                        last = v.tensor_tensor(tsc[:, (e % 2) * 2048 + kt * 512:(e % 2) * 2048 + (kt + 1) * 512],
                                               teb[:, (e * 4 + kt) * 512:(e * 4 + kt + 1) * 512],
                                               cws[:, (e * 4 + tc) * 512:(e * 4 + tc + 1) * 512], ALU.mult)
                        vch_n[0] += 1
                        last.then_inc(vch, 1)
                        v.wait_ge(vch, vch_n[0])
                    v.sem_inc(vsc, 1)
    return nc


def host_latp(lat_full_b, c):
    perm = host_kperm(c)
    lp = lat_full_b[perm]
    lk = lp
    lv = np.concatenate([lp[:, 64:128], lp[:, 0:64]], 1)
    return np.ascontiguousarray(lk), np.ascontiguousarray(lv)


def _l1_slots(qc, j):
    if qc == 0:
        full = list(range(16, 16 + 8 * j))
    elif qc == 1:
        full = list(range(16, 16 + 8 * j)) + [0, 1, 2, 3]
    elif qc == 2:
        full = list(range(16, 16 + 8 * (6 - j))) + list(range(0, 8))
    else:
        full = list(range(16, 16 + 8 * (6 - j))) + list(range(0, 12))
    diag = [4 * qc + i for i in range(4)]
    return full + diag, len(full)


def build_l1attn(nc=None, io=None):
    if nc is None:
        nc = bass.Bass("TRN2", target_bir_lowering=False, debug=False)
    DT = _mk_dt(nc, io)
    latkd = DT("latk", [S, 128], BF16, "ExternalInput")
    latvd = DT("latv", [S, 128], BF16, "ExternalInput")
    qd = DT("qfm", [D, T], BF16, "ExternalInput")
    wkud = DT("wku", [32, 128], F32, "ExternalInput")
    wvud = DT("wvu", [32, 128], F32, "ExternalInput")
    wod = DT("wo", [D, D], F32, "ExternalInput")
    btd = DT("biastab", [128, 256], F32, "ExternalInput")
    trid = DT("tri", [4, 128, 512], BF16, "ExternalInput")
    jd = DT("jidx", [1, 1], mybir.dt.int32, "ExternalInput")
    s1d = DT("s1", [D, T], BF16, "ExternalOutput")

    SCALE = 1.0 / float(np.sqrt(128.0))
    with contextlib.ExitStack() as ctx:
        E = ctx.enter_context
        E(nc.allow_low_precision(reason="bf16 pipeline; tolerance measured for bf16 matmuls"))
        latk = E(nc.sbuf_tensor("latks", [128, S], BF16))
        latv = E(nc.sbuf_tensor("latvs", [128, S], BF16))
        ksb = E(nc.sbuf_tensor("ksb", [128, 2 * S], BF16))
        vsb = E(nc.sbuf_tensor("vsb", [128, 2 * S], BF16))
        qsb = E(nc.sbuf_tensor("qsb", [128, 8 * T], BF16))
        ysb = E(nc.sbuf_tensor("ysb", [128, 8 * T], BF16))
        wku = E(nc.sbuf_tensor("wkus", [64, 128], BF16))
        wvu = E(nc.sbuf_tensor("wvus", [64, 128], BF16))
        wos = E(nc.sbuf_tensor("wos", [128, 8 * D], BF16))
        bts = E(nc.sbuf_tensor("bts", [128, 256], F32))
        btf = E(nc.sbuf_tensor("btf", [128, 256], F32))
        tris = E(nc.sbuf_tensor("tris", [128, 4 * 512], BF16))
        onesb = E(nc.sbuf_tensor("onesb", [128, 1], BF16))
        onesrf = E(nc.sbuf_tensor("onesrf", [1, 128], F32))
        nacc = E(nc.sbuf_tensor("nacc", [1, 8], F32))
        ccol = E(nc.sbuf_tensor("ccol", [128, 1], F32))
        pbuf = E(nc.sbuf_tensor("pbuf", [128, 2 * 512], BF16))
        rws = E(nc.sbuf_tensor("rws", [1, 512], F32))
        rbc = E(nc.sbuf_tensor("rbc", [128, 512], BF16))
        js = E(nc.sbuf_tensor("js", [1, 1], mybir.dt.int32))
        oc = E(nc.sbuf_tensor("oc", [128, 4 * 512], BF16))
        ps = E(nc.psum_tensor("atps", [128, 8 * 512], F32))
        ini = E(nc.semaphore("ini"))
        ldj = E(nc.semaphore("ldj"))
        llat = E(nc.semaphore("llat"))
        lq = E(nc.semaphore("lq"))
        lw = E(nc.semaphore("lw"))
        lcst = E(nc.semaphore("lcst"))
        mmU = E(nc.semaphore("mmU"))
        evK = E(nc.semaphore("evK"))
        evV = E(nc.semaphore("evV"))
        sqd = E(nc.semaphore("sqd"))
        mmN = E(nc.semaphore("mmN"))
        vnm = E(nc.semaphore("vnm"))
        scd = E(nc.semaphore("scd"))
        vbf = E(nc.semaphore("vbf"))
        mmS = E(nc.semaphore("mmS"))
        evS = E(nc.semaphore("evS"))
        msk = E(nc.semaphore("msk"))
        mmP = E(nc.semaphore("mmP"))
        mmD = E(nc.semaphore("mmD"))
        vrc = E(nc.semaphore("vrc"))
        mmBC = E(nc.semaphore("mmBC"))
        evBC = E(nc.semaphore("evBC"))
        vy = E(nc.semaphore("vy"))
        mmO = E(nc.semaphore("mmO"))
        evO = E(nc.semaphore("evO"))
        st4 = [E(nc.semaphore(f"sa{i}")) for i in range(4)]
        vch = E(nc.semaphore("vch"))
        block = E(nc.Block())
        vch_n = [0]

        NQSQ = 8 * 4 + 2 * 16

        @block.gpsimd
        def _(g):
            g.memset(onesb[:], 1.0)
            g.memset(onesrf[:], 1.0).then_inc(ini, 1)
            for kt in range(8):
                g.dma_start(out=wos[:, kt * D:(kt + 1) * D],
                            in_=wod[kt * 128:(kt + 1) * 128, :]).then_inc(lw, 16)
            g.dma_start(out=wku[0:32, :], in_=wkud).then_inc(lw, 16)
            g.dma_start(out=wku[32:64, :], in_=wkud).then_inc(lw, 16)
            g.dma_start(out=wvu[0:32, :], in_=wvud).then_inc(lw, 16)
            g.dma_start(out=wvu[32:64, :], in_=wvud).then_inc(lw, 16)

        @block.sync
        def _(sync):
            sync.dma_start_transpose(latk[:], latkd).then_inc(llat, 16)
            sync.dma_start_transpose(latv[:], latvd).then_inc(llat, 16)
            for kt in range(8):
                sync.dma_start(out=qsb[:, kt * T:(kt + 1) * T], in_=qd[kt * 128:(kt + 1) * 128, :]).then_inc(lq, 16)
            sync.dma_start(out=bts[:], in_=btd).then_inc(lcst, 16)
            sync.dma_start(out=js[0:1, :], in_=jd).then_inc(ldj, 16)
            for i in range(4):
                sync.dma_start(out=tris[:, i * 512:(i + 1) * 512], in_=trid[i]).then_inc(lcst, 16)
            for g2 in range(32):
                mt, tc = divmod(g2, 4)
                sync.wait_ge(evO, g2 + 1)
                sync.dma_start(out=s1d[mt * 128:(mt + 1) * 128, tc * 512:(tc + 1) * 512],
                               in_=oc[:, (g2 % 4) * 512:(g2 % 4 + 1) * 512]).then_inc(st4[g2 % 4], 16)
            for i in range(4):
                sync.wait_ge(st4[i], 16 * 8)
            sync.wait_ge(lw, 16 * 12)

        @block.tensor
        def _(t):
            t.wait_ge(ini, 1)
            t.wait_ge(llat, 16 * 2)
            t.wait_ge(lw, 16 * 12)
            gk = 0
            for kvh in range(2):
                for cchunk in range(16):
                    bank = 2 + gk % 2
                    if gk >= 2:
                        t.wait_ge(evK, gk - 1)
                    t.matmul(ps[:, bank * 512:(bank + 1) * 512], wku[kvh * 32:(kvh + 1) * 32, :],
                             latk[kvh * 32:(kvh + 1) * 32, cchunk * 512:(cchunk + 1) * 512],
                             start=True, stop=True).then_inc(mmU, 1)
                    gk += 1
            for g2 in range(32):
                kvh, qd4 = divmod(g2, 16)
                bank = 2 + (gk % 2)
                if g2 >= 2:
                    t.wait_ge(evV, g2 - 1)
                else:
                    t.wait_ge(evK, 31 + g2)
                last = None
                for j in range(4):
                    tb = qd4 * 4 + j
                    last = t.matmul(ps[:, bank * 512 + j * 128:bank * 512 + (j + 1) * 128],
                                    latv[kvh * 32:(kvh + 1) * 32, tb * 128:(tb + 1) * 128],
                                    wvu[kvh * 32:(kvh + 1) * 32, :], start=True, stop=True)
                last.then_inc(mmU, 1)
                gk += 1
            t.wait_ge(lq, 16 * 8)
            nn = 0
            for i in range(10):
                t.wait_ge(sqd, i + 1)
                nch = 4 if i < 8 else 16
                for cchunk in range(nch):
                    if nn >= 1:
                        t.wait_ge(vnm, nn)
                    t.matmul(ps[0:1, 4 * 512:4 * 512 + 512], onesb[:, 0:1],
                             ysb[:, cchunk * 512:(cchunk + 1) * 512],
                             start=True, stop=True).then_inc(mmN, 1)
                    nn += 1
            t.wait_ge(scd, 3)
            t.matmul(ps[:, 5 * 512:5 * 512 + 1], onesrf[0:1, :], nacc[0:1, 0:1],
                     start=True, stop=True).then_inc(mmN, 1)
            t.wait_ge(vbf, 1)
            t.wait_ge(ldj, 16)
            treg = t.alloc_register()
            t.reg_load(treg, js[0:1, 0:1])
            trv = t.snap(treg, min_val=0, max_val=3)
            for j in t.Switch(trv, n=4):
                it = 0
                ss = 0
                for kvh in range(2):
                    for rep in range(4):
                        qt = kvh * 4 + rep
                        for qc in range(4):
                            LS, nfull = _l1_slots(qc, j)
                            N = len(LS)
                            if it >= 1:
                                t.wait_ge(vy, it)
                            ssb = ss
                            for idx in range(N + 1):
                                if idx < N:
                                    slot = LS[idx]
                                    g = ssb + idx
                                    sbank = 6 + g % 2
                                    if g >= 2:
                                        t.wait_ge(evS, g - 1)
                                    t.matmul(ps[:, sbank * 512:(sbank + 1) * 512],
                                             ksb[:, kvh * S + slot * 128:kvh * S + (slot + 1) * 128],
                                             qsb[:, qt * T + qc * 512:qt * T + (qc + 1) * 512],
                                             start=True, stop=True).then_inc(mmS, 1)
                                if idx >= 1:
                                    k = idx - 1
                                    s_ = LS[k]
                                    gs_ = ssb + k
                                    if k >= nfull:
                                        t.wait_ge(msk, it * 4 + (k - nfull) + 1)
                                    else:
                                        t.wait_ge(evS, gs_ + 1)
                                    t.matmul(ps[:, 0:512], vsb[:, kvh * S + s_ * 128:kvh * S + (s_ + 1) * 128],
                                             pbuf[:, (gs_ % 2) * 512:(gs_ % 2 + 1) * 512],
                                             start=(k == 0), stop=(k == N - 1)).then_inc(mmP, 1)
                                    last = t.matmul(ps[0:1, 1 * 512:1 * 512 + 512], onesb[:, 0:1],
                                                    pbuf[:, (gs_ % 2) * 512:(gs_ % 2 + 1) * 512],
                                                    start=(k == 0), stop=(k == N - 1))
                                    if k == N - 1:
                                        last.then_inc(mmD, 1)
                            ss = ssb + N
                            t.wait_ge(vrc, it + 1)
                            if it == 0:
                                t.wait_ge(vbf, 1)
                            else:
                                t.wait_ge(vy, it)
                            t.matmul(ps[:, 5 * 512:5 * 512 + 512], onesrf[0:1, :], rws[0:1, :],
                                     start=True, stop=True).then_inc(mmBC, 1)
                            it += 1
            t.wait_ge(vy, 32)
            for g2 in range(32):
                mt, tc = divmod(g2, 4)
                bank = g2 % 4
                if g2 >= 4:
                    t.wait_ge(evO, g2 - 3)
                last = None
                for kt in range(8):
                    last = t.matmul(ps[:, bank * 512:(bank + 1) * 512],
                                    wos[:, kt * D + mt * 128:kt * D + (mt + 1) * 128],
                                    ysb[:, kt * T + tc * 512:kt * T + (tc + 1) * 512],
                                    start=(kt == 0), stop=(kt == 7))
                last.then_inc(mmO, 1)

        @block.scalar
        def _(s):
            s.wait_ge(lcst, 16 * 5)
            gk = 0
            for kvh in range(2):
                for cchunk in range(16):
                    bank = 2 + gk % 2
                    s.wait_ge(mmU, gk + 1)
                    s.copy(ksb[:, kvh * S + cchunk * 512:kvh * S + (cchunk + 1) * 512],
                           ps[:, bank * 512:(bank + 1) * 512]).then_inc(evK, 1)
                    gk += 1
            for g2 in range(32):
                kvh, qd4 = divmod(g2, 16)
                bank = 2 + (gk % 2)
                s.wait_ge(mmU, gk + 1)
                s.copy(vsb[:, kvh * S + qd4 * 512:kvh * S + (qd4 + 1) * 512],
                       ps[:, bank * 512:(bank + 1) * 512]).then_inc(evV, 1)
                gk += 1
            s.wait_ge(lq, 16 * 8)
            for i in range(10):
                if i >= 1:
                    prev = 4 * i if i <= 8 else 48
                    s.wait_ge(vnm, (4 * i) if i <= 8 else 48)
                if i < 8:
                    s.activation(out=ysb[:, 0:T], in_=qsb[:, i * T:(i + 1) * T],
                                 func=AF.Square).then_inc(sqd, 1)
                else:
                    kvh = i - 8
                    s.activation(out=ysb[:, 0:S], in_=ksb[:, kvh * S:(kvh + 1) * S],
                                 func=AF.Square).then_inc(sqd, 1)
            s.wait_ge(scd, 1)
            s.sqrt(nacc[0:1, 3:4], nacc[0:1, 2:3]).then_inc(scd, 1)
            s.wait_ge(mmN, NQSQ + 1)
            s.copy(ccol[:], ps[:, 5 * 512:5 * 512 + 1]).then_inc(vbf, 1)
            s.wait_ge(vbf, 2)
            s.wait_ge(ldj, 16)
            sreg = s.alloc_register()
            s.reg_load(sreg, js[0:1, 0:1])
            srv = s.snap(sreg, min_val=0, max_val=3)
            for j in s.Switch(srv, n=4):
                ss = 0
                for it in range(32):
                    qc = it % 4
                    LS, nfull = _l1_slots(qc, j)
                    for slot in LS:
                        s.wait_ge(mmS, ss + 1)
                        if ss >= 2:
                            s.wait_ge(mmP, ss - 1)
                        sbank = 6 + ss % 2
                        s.activation(out=pbuf[:, (ss % 2) * 512:(ss % 2 + 1) * 512],
                                     in_=ps[:, sbank * 512:(sbank + 1) * 512], func=AF.Exp,
                                     scale=SCALE, bias=btf[:, qc * 64 + slot:qc * 64 + slot + 1]).then_inc(evS, 1)
                        ss += 1
                    s.wait_ge(mmBC, it + 1)
                    s.copy(rbc[:], ps[:, 5 * 512:6 * 512]).then_inc(evBC, 1)
            for g2 in range(32):
                bank = g2 % 4
                s.wait_ge(mmO, g2 + 1)
                if g2 >= 4:
                    s.wait_ge(st4[g2 % 4], 16 * (g2 // 4))
                s.copy(oc[:, (g2 % 4) * 512:(g2 % 4 + 1) * 512],
                       ps[:, bank * 512:(bank + 1) * 512]).then_inc(evO, 1)

        @block.vector
        def _(v):
            def chv(inst):
                vch_n[0] += 1
                inst.then_inc(vch, 1)
                v.wait_ge(vch, vch_n[0])
            v.wait_ge(lcst, 16 * 5)
            nn = 0
            for i in range(10):
                nch = 4 if i < 8 else 16
                for cchunk in range(nch):
                    v.wait_ge(mmN, nn + 1)
                    chv(v.reduce_max(nacc[0:1, 4:5], ps[0:1, 4 * 512:4 * 512 + 512], axis=AX.X))
                    dst = nacc[0:1, 0:1] if i < 8 else nacc[0:1, 1:2]
                    if nn == 0 or (i == 8 and cchunk == 0):
                        chv(v.tensor_copy(dst, nacc[0:1, 4:5]))
                    else:
                        chv(v.tensor_tensor(dst, dst, nacc[0:1, 4:5], ALU.max))
                    v.sem_inc(vnm, 1)
                    nn += 1
            chv(v.tensor_tensor(nacc[0:1, 2:3], nacc[0:1, 0:1], nacc[0:1, 1:2], ALU.mult))
            v.sem_inc(scd, 1)
            v.wait_ge(scd, 2)
            chv(v.tensor_scalar(nacc[0:1, 0:1], nacc[0:1, 3:4], SCALE, None, ALU.mult))
            v.sem_inc(scd, 1)
            v.wait_ge(vbf, 1)
            v.tensor_scalar(btf[:], bts[:], ccol[:, 0:1], None, ALU.subtract).then_inc(vbf, 1)
            v.wait_ge(ldj, 16)
            vreg = v.alloc_register()
            v.reg_load(vreg, js[0:1, 0:1])
            vrv = v.snap(vreg, min_val=0, max_val=3)
            for j in v.Switch(vrv, n=4):
                ss = 0
                for it in range(32):
                    qt = it // 4
                    qc = it % 4
                    LS, nfull = _l1_slots(qc, j)
                    for i in range(4):
                        gs_ = ss + nfull + i
                        v.wait_ge(evS, gs_ + 1)
                        v.tensor_tensor(pbuf[:, (gs_ % 2) * 512:(gs_ % 2 + 1) * 512],
                                        pbuf[:, (gs_ % 2) * 512:(gs_ % 2 + 1) * 512],
                                        tris[:, i * 512:(i + 1) * 512], ALU.mult).then_inc(msk, 1)
                        v.wait_ge(msk, it * 4 + i + 1)
                    ss += len(LS)
                    v.wait_ge(mmD, it + 1)
                    v.reciprocal(rws[0:1, :], ps[0:1, 1 * 512:1 * 512 + 512]).then_inc(vrc, 1)
                    v.wait_ge(vrc, it + 1)
                    v.wait_ge(evBC, it + 1)
                    v.tensor_tensor(ysb[:, qt * T + qc * 512:qt * T + (qc + 1) * 512],
                                    ps[:, 0:512], rbc[:], ALU.mult).then_inc(vy, 1)
                    v.wait_ge(vy, it + 1)
    return nc



RG4 = [[0, 1, 2, 3], [4, 5, 6, 7]]
STAGE_MARKS = []


def emit_ag(nc, cin, cout):
    with contextlib.ExitStack() as c:
        cs = c.enter_context(nc.semaphore("ccs"))
        blk = c.enter_context(nc.Block())

        @blk.gpsimd
        def _(g):
            g.collective_compute("AllGather", ALU.bypass, replica_groups=RG4,
                                 ins=[cin.opt()], outs=[cout.opt()]).then_inc(cs)
            g.wait_ge(cs, 1)


def emit_latx(nc, lat, latg, latk, latv, csrc):
    with contextlib.ExitStack() as c:
        ob = c.enter_context(nc.sbuf_tensor("obx", [1, 8], mybir.dt.int32))
        cs = c.enter_context(nc.semaphore("ccs"))
        lo = c.enter_context(nc.semaphore("lox"))
        sp = c.enter_context(nc.semaphore("spx"))
        blk = c.enter_context(nc.Block())

        @blk.gpsimd
        def _(g):
            g.collective_compute("AllGather", ALU.bypass, replica_groups=RG4,
                                 ins=[lat.opt()], outs=[latg.opt()]).then_inc(cs)
            g.wait_ge(cs, 1)
            g.dma_start(out=ob[0:1, :], in_=csrc).then_inc(lo, 16)
            g.wait_ge(lo, 16)
            for j in range(8):
                reg = g.alloc_register()
                g.reg_load(reg, ob[0:1, j:j + 1])
                r = g.snap(reg, min_val=0, max_val=S - 1024)
                g.dma_start(out=latk[j * 1024:(j + 1) * 1024, :],
                            in_=latg[bass.ds(r, 1024), :]).then_inc(sp, 16)
            g.wait_ge(sp, 16 * 8)
            g.dma_start(out=latv[:, 0:64], in_=latk[:, 64:128]).then_inc(sp, 16)
            g.dma_start(out=latv[:, 64:128], in_=latk[:, 0:64]).then_inc(sp, 16)
            g.wait_ge(sp, 16 * 10)


def build_fused():
    nc = bass.Bass("TRN2", target_bir_lowering=False, debug=False)
    ext = {}

    def EI(name, shape, dtype):
        ext[name] = nc.dram_tensor(name, shape, dtype, kind="ExternalInput").ap()
        return ext[name]

    EI("x", [2, CHH, D], F32)
    EI("w1d", [128, 160 * 128], F32); EI("w2d", [128, 160 * 128], F32)
    EI("b1g", [D, 4], F32); EI("b2", [128, 8], F32); EI("g0w", [128, 8], F32); EI("b0w", [128, 8], F32)
    for i in range(4):
        EI(f"wq{i}", [D, D], F32); EI(f"wk{i}", [D, D], F32); EI(f"wv{i}", [D, D], F32)
        EI(f"bq{i}", [128, 8], F32); EI(f"bk{i}", [128, 8], F32); EI(f"bvr{i}", [1, D], F32)
        EI(f"lnag{i}", [128, 8], F32); EI(f"lnab{i}", [128, 8], F32)
    EI("cosd", [D, T], BF16); EI("sind", [D, T], BF16)
    EI("swm", [128, 128], BF16); EI("sel2", [2, 128], BF16)
    EI("fw1", [D, 4096], F32); EI("fb1c", [128, 32], F32); EI("fw2", [4096, D], F32)
    EI("fb2c", [128, 8], F32); EI("flng", [128, 8], F32); EI("flnb", [128, 8], F32)
    EI("gw", [8, D], F32); EI("cbr", [128, 8], F32); EI("identd", [128, 128], F32)
    EI("sel8", [8, 1024], BF16)
    EI("wgp", [9, 128, 8 * 512], F32); EI("wup", [9, 128, 8 * 512], F32); EI("wdp", [9, 128, 4 * 1024], F32)
    EI("wkv", [D, 128], F32); EI("wqL1", [D, D], F32)
    EI("wku", [32, 128], F32); EI("wvu", [32, 128], F32); EI("wo", [D, D], F32)
    EI("biastab", [128, 256], F32); EI("tri", [4, 128, 512], BF16); EI("csrc", [1, 8], mybir.dt.int32)
    EI("jidx", [1, 1], mybir.dt.int32)
    EI("fa_w1t", [D, 4096], F32); EI("fa_w2t", [D, 4096], F32)
    EI("fa_s1c", [128, 32], F32); EI("fa_s2c", [128, 32], F32)
    EI("h1t", [4096, 4096], F32); EI("h2t", [4096, 4096], F32)
    EI("h1s", [128, 32], F32); EI("h2s", [128, 32], F32)
    EI("w3t", [4096, D], F32); EI("s3c", [128, 8], F32)
    y = nc.dram_tensor("y", [T, D], F32, kind="ExternalOutput").ap()

    INT = lambda n, s, d: nc.dram_tensor(n, s, d, kind="Internal").ap()
    xn0 = INT("i_xn0", [2, CHH, D], BF16)
    hA = INT("i_hA", [D, T], BF16)
    hB = INT("i_hB", [D, T], BF16)
    kvp = INT("i_kvp", [128, 1032], F32)
    kvp4g = INT("i_kvp4", [512, 1032], F32)
    qi = INT("i_q", [D, T], BF16)
    qri = INT("i_qr", [D, T], BF16)
    s0i = INT("i_s0", [D, T], BF16)
    g0i = INT("i_g0", [D, T], BF16)
    x1i = INT("i_x1", [T, D], F32)
    xn1i = INT("i_xn1", [T, D], BF16)
    lati = INT("i_lat", [T, 128], BF16)
    latg = INT("i_latg", [S, 128], BF16)
    latki = INT("i_latk", [S, 128], BF16)
    latvi = INT("i_latv", [S, 128], BF16)
    s1i = INT("i_s1", [D, T], BF16)
    t1i = INT("i_t1", [4096, T], BF16)
    t2i = INT("i_t2", [4096, T], BF16)
    t3i = INT("i_t3", [4096, T], BF16)

    bases = dict(sb=nc.sbuf_base, st=nc.sbuf_top, pb=nc.psum_base, pt=nc.psum_top,
                 ld=nc.local_dram_base, sd=nc.shared_dram_base)

    stage_n = [0]
    orig_sbuf, orig_psum, orig_sem = nc.sbuf_tensor, nc.psum_tensor, nc.semaphore

    def _mangle(args):
        if args and isinstance(args[0], str):
            return (f"{args[0]}_f{stage_n[0]}",) + tuple(args[1:])
        return args

    nc.sbuf_tensor = lambda *a, **k: orig_sbuf(*_mangle(a), **k)
    nc.psum_tensor = lambda *a, **k: orig_psum(*_mangle(a), **k)
    nc.semaphore = lambda *a, **k: orig_sem(*_mangle(a), **k)

    def R():
        STAGE_MARKS.append((stage_n[0], nc.next_id()))
        stage_n[0] += 1
        nc.reset(previous_sbuf_base=bases["sb"], previous_sbuf_top=bases["st"],
                 previous_psum_base=bases["pb"], previous_psum_top=bases["pt"],
                 previous_local_dram_base=nc.local_dram_base,
                 previous_shared_dram_base=nc.shared_dram_base)

    build_prep(nc, {"x": ext["x"], "xn": xn0}); R()
    build_conv(nc, {"xn": xn0, "w1d": ext["w1d"], "w2d": ext["w2d"], "b1g": ext["b1g"],
                    "b2": ext["b2"], "g0w": ext["g0w"], "b0w": ext["b0w"], "h0": hA}); R()
    h_cur, h_nxt = hA, hB
    for i in range(4):
        build_la_a(nc, {"h": h_cur, "wq": ext[f"wq{i}"], "wk": ext[f"wk{i}"], "wv": ext[f"wv{i}"],
                        "bq": ext[f"bq{i}"], "bk": ext[f"bk{i}"], "bvr": ext[f"bvr{i}"],
                        "cosd": ext["cosd"], "sind": ext["sind"], "swm": ext["swm"],
                        "kvp": kvp, "q": qi, "qr": qri}); R()
        emit_ag(nc, kvp, kvp4g); R()
        build_la_b(nc, {"kvp4": kvp4g.rearrange("(r p) n -> r p n", p=128),
                        "q": qi, "qr": qri, "h": h_cur, "lnag": ext[f"lnag{i}"],
                        "lnab": ext[f"lnab{i}"], "sel2": ext["sel2"], "hn": h_nxt}); R()
        h_cur, h_nxt = h_nxt, h_cur
    build_ffn0(nc, {"h": h_cur, "w1": ext["fw1"], "b1c": ext["fb1c"], "w2": ext["fw2"],
                    "b2c": ext["fb2c"], "lng": ext["flng"], "lnb": ext["flnb"], "s0": s0i}); R()
    build_moe(nc, {"xn0": xn0, "gw": ext["gw"], "cbr": ext["cbr"], "identd": ext["identd"],
                   "sel8": ext["sel8"], "wgp": ext["wgp"], "wup": ext["wup"], "wdp": ext["wdp"],
                   "g0": g0i}); R()
    build_comb0(nc, {"x": ext["x"], "s0": s0i, "g0": g0i, "wkv": ext["wkv"], "wq": ext["wqL1"],
                     "x1": x1i, "xn1": xn1i, "lat": lati, "qfm": qi}); R()
    emit_latx(nc, lati, latg, latki, latvi, ext["csrc"]); R()
    build_l1attn(nc, {"latk": latki, "latv": latvi, "qfm": qi, "wku": ext["wku"],
                      "wvu": ext["wvu"], "wo": ext["wo"], "biastab": ext["biastab"],
                      "tri": ext["tri"], "jidx": ext["jidx"], "s1": s1i}); R()
    build_l1ffn_a(nc, {"xn1": xn1i, "w1t": ext["fa_w1t"], "w2t": ext["fa_w2t"],
                       "s1c": ext["fa_s1c"], "s2c": ext["fa_s2c"], "t1": t1i}); R()
    build_l1ffn_h(nc, {"tin": t1i, "wt": ext["h1t"], "sc": ext["h1s"], "tout": t2i}); R()
    build_l1ffn_h(nc, {"tin": t2i, "wt": ext["h2t"], "sc": ext["h2s"], "tout": t3i}); R()
    build_l1ffn_c(nc, {"t3": t3i, "w3t": ext["w3t"], "s3c": ext["s3c"], "x1": x1i,
                       "s1f": s1i, "y": y})
    STAGE_MARKS.append((stage_n[0], nc.next_id()))
    return nc


def host_csrc(c):
    b, _ = core_chunks(c)
    perm = host_kperm(c)
    rows = []
    for j in range(8):
        gpos = int(perm[j * 1024])
        gchunk = gpos // 1024
        for r in range(4):
            rb, (rc0, rc1) = core_chunks((c // 4) * 4 + r)
            if rc0 == gchunk:
                rows.append(r * 2048 + 0)
                break
            if rc1 == gchunk:
                rows.append(r * 2048 + 1024)
                break
    assert len(rows) == 8
    return np.asarray(rows, np.int32).reshape(1, 8)



_PROGS = {}


def _prog(name, builder):
    if name not in _PROGS:
        _PROGS[name] = builder()
    return _PROGS[name]


def _col32(v):
    return np.ascontiguousarray(v.reshape(32, 128).T).astype(np.float32)


def host_fused_maps(I, c):
    x = I["L0_attn_x"].astype(np.float32)
    w1g, b1g, w2g, b2c = host_conv_weights(I["L0_attn_conv1_w"], I["L0_attn_conv1_b"],
                                           I["L0_attn_conv2_w"], I["L0_attn_conv2_b"])
    W1D, W2D = host_conv_diag(I["L0_attn_conv1_w"], I["L0_attn_conv2_w"])
    cos, sin = host_rope_tables(c)
    wg = np.concatenate([I["L0_ffn_Wg"], I["L0_ffn_Wg_s"][None]], 0)
    wu = np.concatenate([I["L0_ffn_Wu"], I["L0_ffn_Wu_s"][None]], 0)
    wd = np.concatenate([I["L0_ffn_Wd"], I["L0_ffn_Wd_s"][None]], 0)
    m = {
        "x": host_x_slice(x, c),
        "w1d": W1D, "w2d": W2D, "b1g": b1g, "b2": b2c,
        "g0w": host_col8(I["L0_attn_ln0_g"]), "b0w": host_col8(I["L0_attn_ln0_b"]),
        "cosd": cos, "sind": sin, "swm": host_swap_matrix(), "sel2": host_sel2(),
        "fw1": I["L0_attn_W1"].astype(np.float32), "fb1c": _col32(I["L0_attn_b1"]),
        "fw2": I["L0_attn_W2"].astype(np.float32), "fb2c": host_col8(I["L0_attn_b2"]),
        "flng": host_col8(I["L0_attn_ln2_g"]), "flnb": host_col8(I["L0_attn_ln2_b"]),
        "gw": I["L0_ffn_gate_w"].astype(np.float32),
        "cbr": np.broadcast_to(I["L0_ffn_correction_bias"][None, :], (128, 8)).copy().astype(np.float32),
        "identd": np.eye(128, dtype=np.float32), "sel8": host_sel8(),
        "wgp": np.stack([host_pm(wg[e], 8) for e in range(9)]),
        "wup": np.stack([host_pm(wu[e], 8) for e in range(9)]),
        "wdp": np.stack([host_pm(wd[e], 4) for e in range(9)]),
        "wkv": np.concatenate([I["L1_attn_wk_down"], I["L1_attn_wv_down"]], 1).astype(np.float32),
        "wqL1": I["L1_attn_wq"].astype(np.float32),
        "wku": I["L1_attn_wk_up"].astype(np.float32), "wvu": I["L1_attn_wv_up"].astype(np.float32),
        "wo": I["L1_attn_wo"].astype(np.float32),
        "biastab": host_bias_tab(c), "tri": host_tri_masks(), "csrc": host_csrc(c),
        "jidx": np.array([[c % 4]], np.int32),
        "fa_w1t": np.ascontiguousarray(I["L1_ffn_w1_w"].T).astype(np.float32),
        "fa_w2t": np.ascontiguousarray(I["L1_ffn_w2_w"].T).astype(np.float32),
        "fa_s1c": _col32(I["L1_ffn_w1_s"]), "fa_s2c": _col32(I["L1_ffn_w2_s"]),
        "h1t": np.ascontiguousarray(I["L1_ffn_h1_w"].T).astype(np.float32),
        "h2t": np.ascontiguousarray(I["L1_ffn_h2_w"].T).astype(np.float32),
        "h1s": _col32(I["L1_ffn_h1_s"]), "h2s": _col32(I["L1_ffn_h2_s"]),
        "w3t": np.ascontiguousarray(I["L1_ffn_w3_w"].T).astype(np.float32),
        "s3c": host_col8(I["L1_ffn_w3_s"]),
    }
    for i in range(4):
        m[f"wq{i}"] = np.ascontiguousarray(I["L0_attn_Wq"][i]).astype(np.float32)
        m[f"wk{i}"] = np.ascontiguousarray(I["L0_attn_Wk"][i]).astype(np.float32)
        m[f"wv{i}"] = np.ascontiguousarray(I["L0_attn_Wv"][i]).astype(np.float32)
        m[f"bq{i}"] = host_col8(I["L0_attn_bq"][i])
        m[f"bk{i}"] = host_col8(I["L0_attn_bk"][i])
        m[f"bvr{i}"] = I["L0_attn_bv"][i].reshape(1, D).astype(np.float32)
        m[f"lnag{i}"] = host_col8(I["L0_attn_lnA_g"][i])
        m[f"lnab{i}"] = host_col8(I["L0_attn_lnA_b"][i])
    return m


def kernel(**I):
    nc = _prog("fused", build_fused)
    cores = list(range(NCORE))
    maps = [host_fused_maps(I, c) for c in cores]
    res = run_bass_kernel_spmd(nc, maps, core_ids=cores).results
    out = np.zeros((2, S, D), np.float32)
    for c in cores:
        b, (c0, c1) = core_chunks(c)
        y = np.asarray(res[c]["y"])
        out[b, c0 * CH:(c0 + 1) * CH] = y[0:CH]
        out[b, c1 * CH:(c1 + 1) * CH] = y[CH:2 * CH]
    return out


def kernel_unfused(**I):
    x = I["L0_attn_x"].astype(np.float32)
    cores = list(range(NCORE))

    def run(name, builder, maps):
        nc = _prog(name, builder)
        return run_bass_kernel_spmd(nc, maps, core_ids=cores).results

    def fm_of(res, key):
        return [np.asarray(r[key]) for r in res]

    res = run("prep", build_prep, [{"x": host_x_slice(x, c)} for c in cores])
    xn0 = fm_of(res, "xn")

    w1g, b1g, w2g, b2c = host_conv_weights(I["L0_attn_conv1_w"], I["L0_attn_conv1_b"],
                                           I["L0_attn_conv2_w"], I["L0_attn_conv2_b"])
    W1D, W2D = host_conv_diag(I["L0_attn_conv1_w"], I["L0_attn_conv2_w"])
    g0c = host_col8(I["L0_attn_ln0_g"])
    b0c = host_col8(I["L0_attn_ln0_b"])
    res = run("conv", build_conv, [{"xn": xn0[c], "w1d": W1D, "w2d": W2D, "b1g": b1g,
                                    "b2": b2c, "g0w": g0c, "b0w": b0c} for c in cores])
    h = fm_of(res, "h0")

    swm = host_swap_matrix()
    sel2 = host_sel2()
    ropes = [host_rope_tables(c) for c in cores]
    for layer in range(4):
        Wq = np.ascontiguousarray(I["L0_attn_Wq"][layer])
        Wk = np.ascontiguousarray(I["L0_attn_Wk"][layer])
        Wv = np.ascontiguousarray(I["L0_attn_Wv"][layer])
        bq = host_col8(I["L0_attn_bq"][layer])
        bk = host_col8(I["L0_attn_bk"][layer])
        bvr = I["L0_attn_bv"][layer].reshape(1, D).astype(np.float32)
        res = run("la_a", build_la_a,
                  [{"h": h[c], "wq": Wq, "wk": Wk, "wv": Wv, "bq": bq, "bk": bk,
                    "bvr": bvr, "cosd": ropes[c][0], "sind": ropes[c][1], "swm": swm}
                   for c in cores])
        kvp = fm_of(res, "kvp")
        qs = fm_of(res, "q")
        qrs = fm_of(res, "qr")
        lnag = host_col8(I["L0_attn_lnA_g"][layer])
        lnab = host_col8(I["L0_attn_lnA_b"][layer])
        kvp_g = [np.stack([kvp[g * 4 + i] for i in range(4)]).astype(np.float32) for g in range(2)]
        res = run("la_b", build_la_b,
                  [{"kvp4": kvp_g[c // 4], "q": qs[c], "qr": qrs[c], "h": h[c],
                    "lnag": lnag, "lnab": lnab, "sel2": sel2} for c in cores])
        h = fm_of(res, "hn")

    b1c32 = _col32(I["L0_attn_b1"])
    res = run("ffn0", build_ffn0,
              [{"h": h[c], "w1": I["L0_attn_W1"], "b1c": b1c32, "w2": I["L0_attn_W2"],
                "b2c": host_col8(I["L0_attn_b2"]), "lng": host_col8(I["L0_attn_ln2_g"]),
                "lnb": host_col8(I["L0_attn_ln2_b"])} for c in cores])
    s0 = fm_of(res, "s0")

    wg = np.concatenate([I["L0_ffn_Wg"], I["L0_ffn_Wg_s"][None]], 0)
    wu = np.concatenate([I["L0_ffn_Wu"], I["L0_ffn_Wu_s"][None]], 0)
    wd = np.concatenate([I["L0_ffn_Wd"], I["L0_ffn_Wd_s"][None]], 0)
    wgp = np.stack([host_pm(wg[e], 8) for e in range(9)])
    wup = np.stack([host_pm(wu[e], 8) for e in range(9)])
    wdp = np.stack([host_pm(wd[e], 4) for e in range(9)])
    cbb = np.broadcast_to(I["L0_ffn_correction_bias"][None, :], (128, 8)).copy().astype(np.float32)
    identd = np.eye(128, dtype=np.float32)
    sel8 = host_sel8()
    res = run("moe", build_moe,
              [{"xn0": xn0[c], "gw": I["L0_ffn_gate_w"], "cbr": cbb, "identd": identd,
                "sel8": sel8, "wgp": wgp, "wup": wup, "wdp": wdp} for c in cores])
    g0 = fm_of(res, "g0")

    wkv = np.concatenate([I["L1_attn_wk_down"], I["L1_attn_wv_down"]], 1).astype(np.float32)
    res = run("comb0", build_comb0,
              [{"x": host_x_slice(x, c), "s0": s0[c], "g0": g0[c], "wkv": wkv,
                "wq": I["L1_attn_wq"]} for c in cores])
    x1 = fm_of(res, "x1")
    lat = fm_of(res, "lat")
    qfm = fm_of(res, "qfm")
    xn1 = fm_of(res, "xn1")

    lat_full = np.zeros((2, S, 128), lat[0].dtype)
    for c in cores:
        b, (c0, c1) = core_chunks(c)
        lat_full[b, c0 * CH:(c0 + 1) * CH] = lat[c][0:CH]
        lat_full[b, c1 * CH:(c1 + 1) * CH] = lat[c][CH:2 * CH]

    tri = host_tri_masks()
    maps = []
    for c in cores:
        b, _ = core_chunks(c)
        lk, lv = host_latp(lat_full[b], c)
        maps.append({"latk": lk, "latv": lv, "qfm": qfm[c],
                     "wku": I["L1_attn_wk_up"], "wvu": I["L1_attn_wv_up"],
                     "wo": I["L1_attn_wo"], "biastab": host_bias_tab(c), "tri": tri})
    res = run("l1attn", build_l1attn, maps)
    s1 = fm_of(res, "s1")

    w1t = np.ascontiguousarray(I["L1_ffn_w1_w"].T).astype(np.float32)
    w2t = np.ascontiguousarray(I["L1_ffn_w2_w"].T).astype(np.float32)
    h1t = np.ascontiguousarray(I["L1_ffn_h1_w"].T).astype(np.float32)
    h2t = np.ascontiguousarray(I["L1_ffn_h2_w"].T).astype(np.float32)
    w3t = np.ascontiguousarray(I["L1_ffn_w3_w"].T).astype(np.float32)
    res = run("l1ffn_a", build_l1ffn_a,
              [{"xn1": xn1[c], "w1t": w1t, "w2t": w2t, "s1c": _col32(I["L1_ffn_w1_s"]),
                "s2c": _col32(I["L1_ffn_w2_s"])} for c in cores])
    t1 = fm_of(res, "t1")
    res = run("l1ffn_h", build_l1ffn_h,
              [{"tin": t1[c], "wt": h1t, "sc": _col32(I["L1_ffn_h1_s"])} for c in cores])
    t2 = fm_of(res, "tout")
    res = run("l1ffn_h", build_l1ffn_h,
              [{"tin": t2[c], "wt": h2t, "sc": _col32(I["L1_ffn_h2_s"])} for c in cores])
    t3 = fm_of(res, "tout")
    res = run("l1ffn_c", build_l1ffn_c,
              [{"t3": t3[c], "w3t": w3t, "s3c": host_col8(I["L1_ffn_w3_s"]),
                "x1": x1[c], "s1f": s1[c]} for c in cores])
    y = fm_of(res, "y")

    out = np.zeros((2, S, D), np.float32)
    for c in cores:
        b, (c0, c1) = core_chunks(c)
        out[b, c0 * CH:(c0 + 1) * CH] = y[c][0:CH]
        out[b, c1 * CH:(c1 + 1) * CH] = y[c][CH:2 * CH]
    return out
